# version 33
# baseline (speedup 1.0000x reference)
import numpy as np
import ml_dtypes
import concourse.bass as bass
import concourse.mybir as mybir
from concourse.bass_utils import run_bass_kernel_spmd
from contextlib import ExitStack

F32 = mybir.dt.float32
BF16 = mybir.dt.bfloat16
AF = mybir.ActivationFunctionType
ALU = mybir.AluOpType
AX = mybir.AxisListType

B, T, D = 4, 4096, 1024
HALF = T // 2
ST = 256
NST = HALF // ST
NSUB = ST // 128
NCH = D // 128
CLIP_INV = 1.0e10
LN10E = 23.025850929940457

ENGINES = ("sync", "gpsimd", "vector", "scalar", "tensor")


class Op:
    __slots__ = ("engine", "fn", "deps", "kind", "ndma", "awaited", "cnt", "idx",
                 "semkey")

    def __init__(self, engine, fn, deps, kind, ndma, idx):
        self.engine = engine
        self.fn = fn
        self.deps = deps
        self.kind = kind
        self.ndma = ndma
        self.awaited = False
        self.cnt = 0
        self.idx = idx


class Prog:
    def __init__(self):
        self.ops = []
        self.writer = {}
        self.readers = {}

    def op(self, engine, fn, reads=(), writes=(), kind="comp", ndma=1):
        if kind == "dma":
            assert writes, "dma ops must name their destination buffer"
        deps = []
        for b in reads:
            w = self.writer.get(b)
            if w is not None:
                deps.append(w)
        for b in writes:
            deps.extend(self.readers.get(b, ()))
            w = self.writer.get(b)
            if w is not None:
                deps.append(w)
        seen = set()
        ud = []
        for d in deps:
            if id(d) not in seen:
                seen.add(id(d))
                ud.append(d)
        h = Op(engine, fn, ud, kind, ndma, len(self.ops))
        h.semkey = writes[0] if kind == "dma" else engine
        self.ops.append(h)
        for b in reads:
            self.readers.setdefault(b, []).append(h)
        for b in writes:
            self.writer[b] = h
            self.readers[b] = []
        return h

    def finalize(self):
        for h in self.ops:
            for d in h.deps:
                d.awaited = True
        cnt = {e: 0 for e in ENGINES}
        for h in self.ops:
            if h.kind == "dma":
                cnt.setdefault(h.semkey, 0)
                cnt[h.semkey] += 16 * h.ndma
            elif h.awaited:
                cnt[h.engine] += 1
            h.cnt = cnt[h.semkey]
        self.totals = cnt
        self.dma_semkeys = sorted(k for k in cnt if k not in ENGINES)

    def emit(self, eng_name, eng, sems):
        last = {}
        for h in self.ops:
            if h.engine != eng_name:
                continue
            for d in h.deps:
                if (h.engine == "tensor" and d.engine == "tensor"
                        and d.kind != "dma"):
                    continue
                if not d.awaited or d.cnt <= last.get(d.semkey, 0):
                    continue
                eng.wait_ge(sems[d.semkey], d.cnt)
                last[d.semkey] = d.cnt
            insts = h.fn(eng)
            if h.kind == "dma":
                for i in insts:
                    i.then_inc(sems[h.semkey], 16)
            elif h.awaited:
                insts[-1].then_inc(sems[h.engine], 1)


def build_nc(debug_taps=False):
    nc = bass.Bass("TRN2", target_bir_lowering=False)

    xmain_h = nc.dram_tensor("xmain", [HALF, D], F32, kind="ExternalInput")
    xpre_h = nc.dram_tensor("xpre", [HALF, D], F32, kind="ExternalInput")
    w_h = {m: nc.dram_tensor(m, [D, D], F32, kind="ExternalInput")
           for m in ("wk", "wv", "wr", "wo")}
    ewt_h = nc.dram_tensor("ewt", [8, D, D], F32, kind="ExternalInput")
    gwt_h = nc.dram_tensor("gwt", [D, 8], F32, kind="ExternalInput")
    dec_h = nc.dram_tensor("dec8", [128, 8], F32, kind="ExternalInput")
    lnw_h = nc.dram_tensor("lnw8", [128, 8], F32, kind="ExternalInput")
    iom_h = nc.dram_tensor("iomain", [128, HALF], F32, kind="ExternalInput")
    iop_h = nc.dram_tensor("iopre", [128, HALF], F32, kind="ExternalInput")
    id_h = nc.dram_tensor("ident", [128, 128], BF16, kind="ExternalInput")
    io8_h = nc.dram_tensor("io8", [128, 8], F32, kind="ExternalInput")
    io8r_h = nc.dram_tensor("io8r", [128, 8], F32, kind="ExternalInput")
    out_h = nc.dram_tensor("out", [HALF, D], F32, kind="ExternalOutput")
    escr_h = nc.dram_tensor("escr", [8, 128, NCH * D], BF16, kind="Internal")

    P = Prog()
    tap_handles = {}

    with ExitStack() as stack:
        def sb(name, shape, dt=F32):
            return stack.enter_context(nc.sbuf_tensor(name, shape, dt))

        wT = {m: sb(m + "T", [128, NCH * D], BF16) for m in ("wk", "wv", "wr", "wo")}
        ebuf = [sb(f"ebuf{i}", [128, NCH * D], BF16) for i in range(2)]
        gws = sb("gws", [128, NCH * 8], BF16)
        gwsf = sb("gwsf", [128, NCH * 8], F32)
        xb = [sb(f"xb{i}", [128, NSUB * D]) for i in range(2)]
        iot = [sb(f"iot{i}", [128, ST]) for i in range(2)]
        xn_tm = [sb(f"xn_tm{i}", [128, NSUB * D], BF16) for i in range(2)]
        xn_fm = [sb(f"xn_fm{i}", [128, NCH * ST], BF16) for i in range(2)]
        ybf = sb("ybf", [128, NCH * ST], BF16)
        y2g = sb("y2g", [128, NCH * ST], BF16)
        moe = [sb(f"moe{i}", [128, NSUB * D], BF16) for i in range(2)]
        sA = sb("sA", [128, NCH * ST])
        sB = sb("sB", [128, NCH * ST])
        sC = sb("sC", [128, NCH * ST])
        sD = sb("sD", [128, NCH * ST])
        s_sb = sb("s_sb", [128, NSUB * D])
        stageQ = [sb(f"stageQ{i}", [128, D]) for i in range(2)]
        stageP = [sb(f"stageP{i}", [128, D]) for i in range(2)]
        z256 = sb("z256", [128, ST])
        ident = sb("identb", [128, 128], BF16)
        onesb = sb("onesb", [128, 1], BF16)
        io8 = sb("io8b", [128, 8])
        io8r = sb("io8rb", [128, 8])
        decs = sb("decs", [128, 8])
        sgd = sb("sgd", [128, 8])
        dmx = sb("dmx", [128, 8])
        Lc = sb("Lc", [128, 8])
        nLc = sb("nLc", [128, 8])
        lnw = sb("lnwb", [128, 8])
        carry = sb("carryb", [128, NCH])
        tcols = sb("tcols", [128, NST * NCH])
        tc2 = sb("tc2", [128, 4 * NCH])
        scb = sb("scb", [128, 2 * 8])
        ohb = [sb(f"ohb{i}", [128, 2 * 8]) for i in range(2)]
        t8a = sb("t8a", [128, 8])
        t8b = sb("t8b", [128, 8])
        t8c = sb("t8c", [128, 8])
        t8d = sb("t8d", [128, 8])
        gmx = sb("gmx", [128, 2])
        gmx2 = sb("gmx2", [128, 2])
        m3g = sb("m3g", [128, 16])
        mx3 = sb("mx3", [128, 2])
        idxc = sb("idxc", [128, 2])
        mcol = sb("mcol", [128, 2])
        sq2 = sb("sq2", [128, 2])
        t1c = sb("t1c", [128, 2])
        s2c = sb("s2c", [128, 2])
        nvar = sb("nvar", [128, 2])
        lnc = sb("lncol", [128, 2])
        acol = sb("acol", [128, 2])
        acol2 = sb("acol2", [128, 2])
        nbc = sb("nbc", [128, 2])
        nm = sb("nm", [128, 2])
        m2c = sb("m2c", [128, 2])
        s2g = sb("s2g", [128, 2])
        rsv = sb("rsv", [128, 2])
        rsv2 = sb("rsv2", [128, 2])
        rsl = sb("rsl", [128, 2])
        cst = sb("cstb", [128, 4])
        qa = sb("qa", [128, 1])
        qs = sb("qs", [128, 1])
        qsh = sb("qsh", [128, 1])
        qnsh = sb("qnsh", [128, 1])
        qwb = sb("qwb", [128, D], BF16)

        pb = [stack.enter_context(nc.psum_tensor(f"pb{i}", [128, 512], F32))
              for i in range(8)]
        sems = {e: stack.enter_context(nc.semaphore("sem_" + e)) for e in ENGINES}
        block = None

        op = P.op

        def tap(name, ap, readkeys):
            if not debug_taps:
                return
            th = nc.dram_tensor("tap_" + name, list(ap.shape), ap.dtype,
                                kind="ExternalOutput")
            tap_handles[name] = th
            op("sync", lambda e, th=th, ap=ap: [e.dma_start(th[:], ap)],
               reads=list(readkeys), writes=["taphbm"], kind="dma")

        op("sync", lambda e: [e.dma_start(ident[:], id_h[:])], writes=["ident"], kind="dma")
        op("sync", lambda e: [e.dma_start(io8[:], io8_h[:])], writes=["io8"], kind="dma")
        op("sync", lambda e: [e.dma_start(io8r[:], io8r_h[:])], writes=["io8r"], kind="dma")
        op("sync", lambda e: [e.dma_start(decs[:], dec_h[:])], writes=["decs"], kind="dma")
        op("sync", lambda e: [e.dma_start(lnw[:], lnw_h[:])], writes=["lnw"], kind="dma")
        op("sync", lambda e: [e.dma_start(gwsf[:, kc * 8:(kc + 1) * 8],
                                          gwt_h[kc * 128:(kc + 1) * 128, :])
                              for kc in range(NCH)],
           writes=["gwsf"], kind="dma", ndma=NCH)
        op("vector", lambda e: [e.memset(z256[:], 0.0)], writes=["z256"])
        op("vector", lambda e: [e.memset(onesb[:], 1.0)], writes=["onesb"])
        op("vector", lambda e: [e.memset(carry[:], 0.0)], writes=["carry"])
        op("vector", lambda e: [e.memset(cst[:, 0:1], 1e-5)], writes=["cst"])
        op("vector", lambda e: [e.memset(cst[:, 1:2], 1e-6)], writes=["cst"])
        op("vector", lambda e: [e.memset(cst[:, 2:3], 1.0)], writes=["cst"])
        op("vector", lambda e: [e.memset(cst[:, 3:4], -8.0)], writes=["cst"])
        op("vector", lambda e: [e.tensor_copy(gws[:], gwsf[:])], reads=["gwsf"], writes=["gws"])
        op("scalar", lambda e: [e.activation(sgd[:], decs[:], AF.Sigmoid)],
           reads=["decs"], writes=["sgd"])
        op("vector", lambda e: [e.tensor_scalar(dmx[:], sgd[:], 1e-7, None, ALU.max)],
           reads=["sgd"], writes=["dmx"])
        op("scalar", lambda e: [e.activation(Lc[:], dmx[:], AF.Ln)],
           reads=["dmx"], writes=["Lc"])
        op("vector", lambda e: [e.tensor_scalar(nLc[:], Lc[:], -1.0, None, ALU.mult)],
           reads=["Lc"], writes=["nLc"])

        import numpy as _np
        _eps = float(_np.float32(1e-5))

        def emit_quant_round(m, c):
            wTm = wT[m]
            sg = stageQ[c % 2]
            skey = f"stageQ{c % 2}"
            op("sync", lambda e, m=m, c=c, sg=sg: [
                e.dma_start(sg[:], w_h[m][c * 128:(c + 1) * 128, :])],
                writes=[skey], kind="dma")
            qta = sA[:, 0:D]
            qtb = sB[:, 0:D]
            qwq = sC[:, 0:D]
            qwbf = qwb[:]
            op("scalar", lambda e, sg=sg, qta=qta: [
                e.activation(qta, sg[:], AF.Abs, accum_out=qa[:, 0:1])],
               reads=[skey], writes=["sA", "qa"])
            op("gpsimd", lambda e: [
                e.tensor_scalar(qs[:], qa[:], 1.0 / D, _eps, ALU.mult, ALU.max),
                e.tensor_scalar(qsh[:], qa[:], 0.5 / D, 0.5 * _eps, ALU.mult, ALU.max)],
               reads=["qa"], writes=["qs", "qsh"])
            op("scalar", lambda e, sg=sg, qtb=qtb: [
                e.activation(qtb, sg[:], AF.Sign)],
               reads=[skey], writes=["sB"])
            op("vector", lambda e, qwq=qwq, qta=qta: [
                e.tensor_scalar(qwq, qta, qsh[:, 0:1], qs[:, 0:1],
                                ALU.is_ge, ALU.mult)],
               reads=["sA", "qsh", "qs"], writes=["sC"])
            op("vector", lambda e, qwbf=qwbf, qwq=qwq, qtb=qtb: [
                e.tensor_tensor(qwbf, qwq, qtb, ALU.mult)],
               reads=["sC", "sB"], writes=["qwb"])
            for q in range(2):
                bank = pb[q]

                def trq(e, q=q, bank=bank, qwbf=qwbf):
                    return [e.matmul(bank[:, i2 * 128:(i2 + 1) * 128],
                                     qwbf[:, (4 * q + i2) * 128:(4 * q + i2 + 1) * 128],
                                     ident[:], start=True, stop=True)
                            for i2 in range(4)]
                op("tensor", trq, reads=["qwb", "ident"], writes=[f"p{q}"])

                def evq(e, q=q, bank=bank, m=m, c=c, wTm=wTm):
                    ins = []
                    for i2 in range(4):
                        kc = 4 * q + i2
                        dst = wTm[:, kc * D + c * 128: kc * D + (c + 1) * 128]
                        src = bank[:, i2 * 128:(i2 + 1) * 128]
                        if m == "wo":
                            ins.append(e.tensor_scalar(dst, src, lnw[:, kc:kc + 1], None, ALU.mult))
                        else:
                            ins.append(e.tensor_copy(dst, src))
                    return ins
                op("vector", evq, reads=[f"p{q}", "lnw"], writes=[m + "T"])

        def quantize_matrix(m):
            for c in range(8):
                emit_quant_round(m, c)

        quantize_matrix("wk")
        quantize_matrix("wv")

        def emit_prep_expert(ex):
            kb = ebuf[ex % 2]
            bkey = f"ebuf{ex % 2}"

            def ldcast(e, ex=ex, kb=kb):
                return [e.dma_start(kb[:, hh * D:(hh + 1) * D],
                                    ewt_h[ex, hh * 128:(hh + 1) * 128, :])
                        for hh in range(8)]
            op("gpsimd", ldcast, writes=[bkey], kind="dma", ndma=8)
            op("gpsimd", lambda e, ex=ex, kb=kb: [e.dma_start(escr_h[ex], kb[:])],
               reads=[bkey], writes=[f"escr{ex}"], kind="dma")

        MAGICBIG = None

        def emit_ln_load(st, pp, x_hbm, prefix):
            xs = xb[pp]
            io_hbm = iop_h if prefix else iom_h

            def ldx(e, st=st, xs=xs, x_hbm=x_hbm):
                return [e.dma_start(xs[:, j * D:(j + 1) * D],
                                    x_hbm[st * ST + j * 128: st * ST + (j + 1) * 128, :])
                        for j in range(NSUB)]
            op("sync", ldx, writes=[f"x{pp}"], kind="dma", ndma=NSUB)
            op("sync", lambda e, st=st: [e.dma_start(iot[pp][:], io_hbm[:, st * ST:(st + 1) * ST])],
               writes=[f"iot{pp}"], kind="dma")

        def emit_ln_math(st, pp, x_hbm, prefix):
            xs = xb[pp]
            xnt = xn_tm[pp]
            if prefix:
                op("vector", lambda e, xs=xs: [
                    e.reduce_sum(t1c[:, j:j + 1], xs[:, j * D:(j + 1) * D], axis=AX.X)
                    for j in range(NSUB)],
                    reads=[f"x{pp}"], writes=["t1c"])
                op("scalar", lambda e: [
                    e.mul(mcol[:, j:j + 1], t1c[:, j:j + 1], 1.0 / D)
                    for j in range(NSUB)],
                    reads=["t1c"], writes=["mcol"])
                op("scalar", lambda e, xs=xs: [
                    e.activation(sA[:, j * D:(j + 1) * D], xs[:, j * D:(j + 1) * D],
                                 AF.Square, accum_out=sq2[:, j:j + 1])
                    for j in range(NSUB)],
                    reads=[f"x{pp}"], writes=["sA", "sq2"])
                op("scalar", lambda e: [
                    e.mul(s2c[:, j:j + 1], sq2[:, j:j + 1], 1.0 / D)
                    for j in range(NSUB)],
                    reads=["sq2"], writes=["s2c"])
                op("vector", lambda e: [
                    e.scalar_tensor_tensor(nvar[:, j:j + 1], mcol[:, j:j + 1],
                                           mcol[:, j:j + 1], s2c[:, j:j + 1],
                                           ALU.mult, ALU.subtract)
                    for j in range(NSUB)],
                    reads=["mcol", "s2c"], writes=["nvar"])
                op("scalar", lambda e: [
                    e.activation(lnc[:, j:j + 1], nvar[:, j:j + 1], AF.Ln,
                                 bias=cst[:, 0:1], scale=-1.0)
                    for j in range(NSUB)],
                    reads=["nvar", "cst"], writes=["lncol"])
                op("scalar", lambda e: [
                    e.activation(acol[:, j:j + 1], lnc[:, j:j + 1], AF.Exp, scale=-0.5)
                    for j in range(NSUB)],
                    reads=["lncol"], writes=["acol"])
                op("vector", lambda e, xs=xs, xnt=xnt: [
                    e.tensor_scalar(xnt[:, j * D:(j + 1) * D], xs[:, j * D:(j + 1) * D],
                                    mcol[:, j:j + 1], acol[:, j:j + 1],
                                    ALU.subtract, ALU.mult)
                    for j in range(NSUB)],
                    reads=[f"x{pp}", "mcol", "acol"], writes=[f"xn_tm{pp}"])
            else:
                op("scalar", lambda e, xs=xs: [
                    e.activation(sA[:, j * D:(j + 1) * D], xs[:, j * D:(j + 1) * D],
                                 AF.Identity, accum_out=t1c[:, j:j + 1])
                    for j in range(NSUB)],
                    reads=[f"x{pp}"], writes=["sA", "t1c"])
                op("gpsimd", lambda e: [
                    e.tensor_scalar(nm[:, 0:NSUB], t1c[:, 0:NSUB], -1.0 / D,
                                    None, ALU.mult)],
                   reads=["t1c"], writes=["nm"])
                op("scalar", lambda e, xs=xs: [
                    e.activation(sA[:, j * D:(j + 1) * D], xs[:, j * D:(j + 1) * D],
                                 AF.Square, accum_out=sq2[:, j:j + 1])
                    for j in range(NSUB)],
                    reads=[f"x{pp}"], writes=["sA", "sq2"])
                op("gpsimd", lambda e: [
                    e.tensor_scalar(s2g[:, 0:NSUB], sq2[:, 0:NSUB], 1.0 / D,
                                    float(np.float32(1e-5)), ALU.mult, ALU.add)],
                   reads=["sq2"], writes=["s2g"])
                op("scalar", lambda e: [
                    e.activation(m2c[:, j:j + 1], t1c[:, j:j + 1], AF.Square,
                                 scale=1.0 / D)
                    for j in range(NSUB)],
                    reads=["t1c"], writes=["m2c"])
                op("scalar", lambda e: [
                    e.activation(lnc[:, j:j + 1], m2c[:, j:j + 1], AF.Ln,
                                 bias=s2g[:, j:j + 1], scale=-1.0)
                    for j in range(NSUB)],
                    reads=["m2c", "s2g"], writes=["lncol"])
                op("scalar", lambda e: [
                    e.activation(acol[:, j:j + 1], lnc[:, j:j + 1], AF.Exp, scale=-0.5)
                    for j in range(NSUB)],
                    reads=["lncol"], writes=["acol"])
                op("gpsimd", lambda e: [
                    e.tensor_copy(acol2[:, 0:NSUB], acol[:, 0:NSUB])],
                   reads=["acol"], writes=["acol2"])
                op("scalar", lambda e, xs=xs: [
                    e.activation(sA[:, j * D:(j + 1) * D], xs[:, j * D:(j + 1) * D],
                                 AF.Identity, bias=nm[:, j:j + 1])
                    for j in range(NSUB)],
                    reads=[f"x{pp}", "nm"], writes=["sA"])
                op("scalar", lambda e, xnt=xnt: [
                    e.activation(xnt[:, j * D:(j + 1) * D], sA[:, j * D:(j + 1) * D],
                                 AF.Identity, scale=acol2[:, j:j + 1])
                    for j in range(NSUB)],
                    reads=["sA", "acol2"], writes=[f"xn_tm{pp}"])

        def emit_ln(st, pp, x_hbm, prefix):
            emit_ln_load(st, pp, x_hbm, prefix)
            emit_ln_math(st, pp, x_hbm, prefix)

        def emit_transposes(pp):
            xnt, xnf = xn_tm[pp], xn_fm[pp]
            for g in range(4):
                bank = pb[g % 2]

                def trg(e, g=g, bank=bank, xnt=xnt):
                    ins = []
                    for u2 in range(4):
                        c = 2 * g + u2 // 2
                        j = u2 % 2
                        ins.append(e.matmul(bank[:, u2 * 128:(u2 + 1) * 128],
                                            xnt[:, j * D + c * 128: j * D + (c + 1) * 128],
                                            ident[:], start=True, stop=True))
                    return ins
                op("tensor", trg, reads=[f"xn_tm{pp}", "ident"], writes=[f"p{g % 2}"])
                op("scalar", lambda e, g=g, bank=bank, xnf=xnf: [
                    e.copy(xnf[:, (2 * g) * ST:(2 * g + 2) * ST], bank[:, 0:512])],
                    reads=[f"p{g % 2}"], writes=[f"xn_fm{pp}"])

        def emit_gate(pp, dbg=False):
            xnf = xn_fm[pp]
            ohx = ohb[pp]
            for j in range(NSUB):
                gb = pb[2 + j]
                gbk = f"p{2 + j}"

                def gmm(e, j=j, xnf=xnf, gb=gb):
                    ins = []
                    for kc in range(NCH):
                        ins.append(e.matmul(gb[:, j * 8:(j + 1) * 8],
                                            xnf[:, kc * ST + j * 128: kc * ST + (j + 1) * 128],
                                            gws[:, kc * 8:(kc + 1) * 8],
                                            start=(kc == 0), stop=(kc == NCH - 1)))
                    return ins
                op("tensor", gmm, reads=[f"xn_fm{pp}", "gws"], writes=[gbk])
                op("scalar", lambda e, j=j, gb=gb: [
                    e.copy(scb[:, j * 8:(j + 1) * 8], gb[:, j * 8:(j + 1) * 8])],
                    reads=[gbk], writes=["scb"])
            for j in range(NSUB):
                op("vector", lambda e, j=j: [
                    e.reduce_max(gmx[:, j:j + 1], scb[:, j * 8:(j + 1) * 8], axis=AX.X)],
                    reads=["scb"], writes=["gmx"])
            op("scalar", lambda e: [
                e.mul(gmx2[:], gmx[:], -1.0)],
                reads=["gmx"], writes=["gmx2"])
            for j, (ta, tb) in enumerate(((t8a, t8b), (t8c, t8d))):
                ka, kb = f"t8{j}a", f"t8{j}b"
                op("scalar", lambda e, j=j, ta=ta: [
                    e.activation(ta[:], scb[:, j * 8:(j + 1) * 8], AF.Identity,
                                 bias=gmx2[:, j:j + 1])],
                    reads=["scb", "gmx2"], writes=[ka])
                op("scalar", lambda e, ta=ta, tb=tb: [
                    e.activation(tb[:], ta[:], AF.Sign)],
                    reads=[ka], writes=[kb])
                op("scalar", lambda e, ta=ta, tb=tb: [
                    e.activation(ta[:], tb[:], AF.Abs)],
                    reads=[kb], writes=[ka])
                op("scalar", lambda e, ta=ta, tb=tb: [
                    e.activation(tb[:], ta[:], AF.Identity, bias=cst[:, 2:3], scale=-1.0)],
                    reads=[ka, "cst"], writes=[kb])
                op("vector", lambda e, ta=ta, tb=tb: [
                    e.tensor_tensor(ta[:], tb[:], io8r[:], ALU.mult)],
                    reads=[kb, "io8r"], writes=[ka])
            op("scalar", lambda e: [
                e.copy(m3g[:, 0:8], t8a[:]),
                e.copy(m3g[:, 8:16], t8c[:])],
                reads=["t80a", "t81a"], writes=["m3g"])
            for j in range(NSUB):
                op("vector", lambda e, j=j: [
                    e.reduce_max(mx3[:, j:j + 1], m3g[:, j * 8:(j + 1) * 8], axis=AX.X)],
                    reads=["m3g"], writes=["mx3"])
            op("scalar", lambda e: [
                e.activation(idxc[:, j:j + 1], mx3[:, j:j + 1], AF.Identity,
                             bias=cst[:, 3:4])
                for j in range(NSUB)],
                reads=["mx3", "cst"], writes=["idxc"])
            for j, (ta, tb) in enumerate(((t8a, t8b), (t8c, t8d))):
                ka, kb = f"t8{j}a", f"t8{j}b"
                op("scalar", lambda e, j=j, ta=ta: [
                    e.activation(ta[:], io8[:], AF.Identity, bias=idxc[:, j:j + 1])],
                    reads=["io8", "idxc"], writes=[ka])
                op("scalar", lambda e, ta=ta, tb=tb: [
                    e.activation(tb[:], ta[:], AF.Sign)],
                    reads=[ka], writes=[kb])
                op("scalar", lambda e, ta=ta, tb=tb: [
                    e.activation(ta[:], tb[:], AF.Abs)],
                    reads=[kb], writes=[ka])
                op("scalar", lambda e, j=j, ta=ta, ohx=ohx: [
                    e.activation(ohx[:, j * 8:(j + 1) * 8], ta[:], AF.Identity,
                                 bias=cst[:, 2:3], scale=-1.0)],
                    reads=[ka, "cst"], writes=[f"ohb{pp}"])
            if dbg:
                tap("idxc_g", idxc[:], ["idxc"])

        def emit_proj(pp, m, banks, bankkeys):
            xnf = xn_fm[pp]
            for i in range(4):
                bank = banks[i]

                def pmm(e, i=i, bank=bank, m=m, xnf=xnf):
                    ins = []
                    for ce in (2 * i, 2 * i + 1):
                        for kc in range(NCH):
                            ins.append(e.matmul(
                                bank[:, (ce % 2) * ST:(ce % 2) * ST + ST],
                                wT[m][:, kc * D + ce * 128: kc * D + (ce + 1) * 128],
                                xnf[:, kc * ST:(kc + 1) * ST],
                                start=(kc == 0), stop=(kc == NCH - 1)))
                    return ins
                op("tensor", pmm, reads=[m + "T", f"xn_fm{pp}"], writes=[bankkeys[i]])

        def emit_scan_ue(st, pp):
            iost = iot[pp]
            op("vector", lambda e, iost=iost: [
                e.tensor_scalar(sA[:, c * ST:(c + 1) * ST], iost[:],
                                nLc[:, c:c + 1], 23.025850929940457,
                                ALU.mult, ALU.min)
                for c in range(NCH)],
                reads=[f"iot{pp}", "nLc"], writes=["sA"])
            op("scalar", lambda e: [e.activation(sB[:], sA[:], AF.Exp)],
               reads=["sA"], writes=["sB"])

        def emit_scan_a(st, pp, with_scale, kb0=4):
            iost = iot[pp]
            op("vector", lambda e, kb0=kb0: [
                e.scalar_tensor_tensor(sC[:, 2 * i * ST:(2 * i + 2) * ST],
                                       sB[:, 2 * i * ST:(2 * i + 2) * ST],
                                       CLIP_INV, pb[kb0 + i][:, 0:512],
                                       ALU.min, ALU.mult)
                for i in range(4)],
                reads=["sB"] + [f"p{kb0 + i}" for i in range(4)], writes=["sC"])
            if with_scale:
                op("scalar", lambda e, iost=iost: [
                    e.activation(sA[:, c * ST:(c + 1) * ST], iost[:], AF.Exp,
                                 scale=Lc[:, c:c + 1])
                    for c in range(NCH)],
                    reads=[f"iot{pp}", "Lc"], writes=["sA"])

        def emit_scan_b(st, pp, prefix_st=None):
            if prefix_st is not None:
                op("vector", lambda e, k=prefix_st: [
                    e.scalar_tensor_tensor(sD[:, c * ST:(c + 1) * ST],
                                           pb[c // 2][:, (c % 2) * ST:(c % 2) * ST + ST],
                                           1.0,
                                           sC[:, c * ST:(c + 1) * ST],
                                           ALU.mult, ALU.mult,
                                           accum_out=tcols[:, k * 8 + c:k * 8 + c + 1])
                    for c in range(NCH)],
                    reads=["p0", "p1", "p2", "p3", "sC"], writes=["sD", "tcols"])
                return
            op("vector", lambda e: [
                e.tensor_tensor(sD[:, 2 * i * ST:(2 * i + 2) * ST],
                                pb[i][:, 0:512],
                                sC[:, 2 * i * ST:(2 * i + 2) * ST], ALU.mult)
                for i in range(4)],
                reads=["p0", "p1", "p2", "p3", "sC"], writes=["sD"])
            op("vector", lambda e: [
                e.tensor_tensor_scan(sC[:, c * ST:(c + 1) * ST],
                                     sD[:, c * ST:(c + 1) * ST], z256[:],
                                     carry[:, c:c + 1], ALU.add, ALU.add)
                for c in range(NCH)],
                reads=["sD", "z256", "carry"], writes=["sC"])
            op("scalar", lambda e: [
                e.copy(carry[:, c:c + 1], sC[:, c * ST + ST - 1: c * ST + ST])
                for c in range(NCH)],
                reads=["sC"], writes=["carry"])

        def emit_y(rb0=4):
            op("scalar", lambda e, rb0=rb0: [
                e.activation(sD[:, 2 * i * ST:(2 * i + 2) * ST],
                             pb[rb0 + i][:, 0:512], AF.Sigmoid)
                for i in range(4)],
                reads=[f"p{rb0 + i}" for i in range(4)], writes=["sD"])
            op("vector", lambda e: [e.tensor_tensor(sB[:], sD[:], sA[:], ALU.mult)],
               reads=["sD", "sA"], writes=["sB"])
            op("vector", lambda e: [e.tensor_tensor(ybf[:], sC[:], sB[:], ALU.mult)],
               reads=["sC", "sB"], writes=["ybf"])
            op("scalar", lambda e: [e.activation(y2g[:], ybf[:], AF.Square)],
               reads=["ybf"], writes=["y2g"])

        esched = []
        for _p in range(NST):
            _order = range(7, -1, -1) if _p % 2 == 0 else range(8)
            for _i, _ex in enumerate(_order):
                _buf = (_i + 1) % 2 if _p % 2 == 0 else _i % 2
                esched.append((_ex, _buf, _i >= 2))

        def emit_expert_load(g):
            if g >= len(esched):
                return
            ex, buf, needs = esched[g]
            if not needs:
                return
            op("gpsimd", lambda e, ex=ex, kbuf=ebuf[buf]: [e.dma_start(kbuf[:], escr_h[ex])],
               reads=[f"escr{ex}"], writes=[f"ebuf{buf}"], kind="dma")

        def emit_experts(st, pp, hooks=None):
            xnf = xn_fm[pp]
            for i in range(8):
                if hooks and i in hooks:
                    hooks[i]()
                ex, buf, _ = esched[st * 8 + i]
                kbuf = ebuf[buf]
                bkey = f"ebuf{buf}"
                for j in range(NSUB):
                    for nh in range(2):
                        bank = pb[4 + j * 2 + nh]
                        bk = f"p{4 + j * 2 + nh}"

                        def emm(e, j=j, nh=nh, bank=bank, kbuf=kbuf, xnf=xnf):
                            ins = []
                            for kc in range(NCH):
                                ins.append(e.matmul(
                                    bank[:, 0:512],
                                    xnf[:, kc * ST + j * 128: kc * ST + (j + 1) * 128],
                                    kbuf[:, kc * D + nh * 512: kc * D + nh * 512 + 512],
                                    start=(kc == 0), stop=(kc == NCH - 1)))
                            return ins
                        op("tensor", emm, reads=[f"xn_fm{pp}", bkey], writes=[bk])
                        dst = moe[(i + 1) % 2][:, j * D + nh * 512: j * D + nh * 512 + 512]
                        ohcol = ohb[pp][:, j * 8 + ex: j * 8 + ex + 1]
                        if i == 0:
                            op("vector", lambda e, dst=dst, bank=bank, ohcol=ohcol: [
                                e.tensor_scalar(dst, bank[:, 0:512], ohcol, None, ALU.mult)],
                                reads=[bk, f"ohb{pp}"], writes=["moe1"])
                        else:
                            src2 = moe[i % 2][:, j * D + nh * 512: j * D + nh * 512 + 512]
                            op("vector", lambda e, dst=dst, bank=bank, ohcol=ohcol, src2=src2: [
                                e.scalar_tensor_tensor(dst, bank[:, 0:512], ohcol, src2,
                                                       ALU.mult, ALU.add)],
                                reads=[bk, f"ohb{pp}", f"moe{i % 2}"], writes=[f"moe{(i + 1) % 2}"])
                emit_expert_load(st * 8 + i + 2)
            if hooks and 8 in hooks:
                hooks[8]()

        def emit_rms_wo_out(st, pp):
            xs = xb[pp]
            for j in range(NSUB):
                def s2mm(e, j=j):
                    ins = []
                    for kc in range(NCH):
                        ins.append(e.matmul(pb[5][:, 16 + j:17 + j],
                                            y2g[:, kc * ST + j * 128: kc * ST + (j + 1) * 128],
                                            onesb[:],
                                            start=(kc == 0), stop=(kc == NCH - 1)))
                    return ins
                op("tensor", s2mm, reads=["y2g", "onesb"], writes=["p5"])
                op("scalar", lambda e, j=j: [
                    e.activation(rsl[:, j:j + 1], pb[5][:, 16 + j:17 + j], AF.Ln,
                                 bias=cst[:, 1:2], scale=1.0 / D)],
                    reads=["p5", "cst"], writes=["rsl"])
                op("scalar", lambda e, j=j: [
                    e.activation(rsv[:, j:j + 1], rsl[:, j:j + 1], AF.Exp, scale=-0.5)],
                    reads=["rsl"], writes=["rsv"])
            op("vector", lambda e: [
                e.tensor_scalar(rsv2[:, 0:NSUB], rsv[:, 0:NSUB], 1.0, None, ALU.mult)],
               reads=["rsv"], writes=["rsv2"])
            wob = {0: (4, 6), 1: (7, 5)}
            for j in range(NSUB):
                for nh in range(2):
                    bank = pb[wob[j][nh]]
                    bk = f"p{wob[j][nh]}"

                    def womm(e, j=j, nh=nh, bank=bank):
                        ins = []
                        for kc in range(NCH):
                            ins.append(e.matmul(
                                bank[:, 0:512],
                                ybf[:, kc * ST + j * 128: kc * ST + (j + 1) * 128],
                                wT["wo"][:, kc * D + nh * 512: kc * D + nh * 512 + 512],
                                start=(kc == 0), stop=(kc == NCH - 1)))
                        return ins
                    op("tensor", womm, reads=["ybf", "woT"], writes=[bk])
                    op("scalar", lambda e, j=j, nh=nh, bank=bank: [
                        e.activation(s_sb[:, j * D + nh * 512: j * D + nh * 512 + 512],
                                     bank[:, 0:512], AF.Identity,
                                     scale=rsv2[:, j:j + 1])],
                        reads=[bk, "rsv2"], writes=["s_sb"])
        def emit_gpre(st, pp):
            for j in range(NSUB):
                op("vector", lambda e, j=j, pp=pp: [
                    e.tensor_tensor(stageP[j][:], moe[0][:, j * D:(j + 1) * D],
                                    xn_tm[pp][:, j * D:(j + 1) * D], ALU.add)],
                    reads=["moe0", f"xn_tm{pp}"], writes=[f"stageP{j}"])

        def emit_xt_load(st):
            def ldxt(e, st=st):
                return [e.dma_start(stageQ[j][:],
                                    xmain_h[st * ST + j * 128: st * ST + (j + 1) * 128, :])
                        for j in range(NSUB)]
            op("sync", ldxt, writes=["stageQ0", "stageQ1"], kind="dma", ndma=NSUB)

        def emit_tail2a(st, pp):
            op("scalar", lambda e: [
                e.activation(sD[:, 0:D], stageP[0][:], AF.Sigmoid),
                e.activation(sD[:, D:2 * D], stageP[1][:], AF.Sigmoid)],
               reads=["stageP0", "stageP1"], writes=["sD"])
            for j in range(NSUB):
                op("vector", lambda e, j=j: [
                    e.tensor_tensor(sB[:, j * D:(j + 1) * D], stageP[j][:],
                                    sD[:, j * D:(j + 1) * D], ALU.mult)],
                    reads=[f"stageP{j}", "sD"], writes=["sB"])

        def emit_tail2b(st, pp):
            for j in range(NSUB):
                op("vector", lambda e, j=j: [
                    e.tensor_tensor(sD[:, j * D:(j + 1) * D], s_sb[:, j * D:(j + 1) * D],
                                    sB[:, j * D:(j + 1) * D], ALU.mult)],
                    reads=["s_sb", "sB"], writes=["sD"])
                op("vector", lambda e, j=j: [
                    e.tensor_tensor(s_sb[:, j * D:(j + 1) * D], stageQ[j][:],
                                    sD[:, j * D:(j + 1) * D], ALU.add)],
                    reads=[f"stageQ{j}", "sD"], writes=["s_sb"])
                op("sync", lambda e, st=st, j=j: [
                    e.dma_start(out_h[st * ST + j * 128: st * ST + (j + 1) * 128, :],
                                s_sb[:, j * D:(j + 1) * D])],
                    reads=["s_sb"], writes=["outhbm"], kind="dma")

        def emit_tail2(st, pp):
            emit_tail2a(st, pp)
            emit_tail2b(st, pp)

        quantize_matrix("wk")
        quantize_matrix("wv")
        emit_ln(0, 0, xpre_h, prefix=True)
        emit_scan_ue(0, 0)
        for st in range(NST):
            pp = st % 2
            emit_transposes(pp)
            emit_proj(pp, "wk", [pb[4 + i] for i in range(4)], [f"p{4 + i}" for i in range(4)])
            emit_proj(pp, "wv", [pb[i] for i in range(4)], [f"p{i}" for i in range(4)])
            if st + 1 < NST:
                emit_ln(st + 1, (st + 1) % 2, xpre_h, prefix=True)
            emit_scan_a(st, pp, with_scale=False, kb0=4)
            if st + 1 < NST:
                emit_scan_ue(st + 1, (st + 1) % 2)
            emit_scan_b(st, pp, prefix_st=st)
            emit_quant_round("wr" if st < 4 else "wo", 2 * (st % 4))
            emit_quant_round("wr" if st < 4 else "wo", 2 * (st % 4) + 1)
            emit_prep_expert(st)
        op("vector", lambda e: [
            e.tensor_tensor(tc2[:], tcols[:, 0:32], tcols[:, 32:64], ALU.add)],
            reads=["tcols"], writes=["tc2"])
        op("vector", lambda e: [
            e.tensor_tensor(tcols[:, 0:16], tc2[:, 0:16], tc2[:, 16:32], ALU.add)],
            reads=["tc2"], writes=["tcols"])
        op("vector", lambda e: [
            e.tensor_tensor(carry[:], tcols[:, 0:8], tcols[:, 8:16], ALU.add)],
            reads=["tcols"], writes=["carry"])
        if debug_taps:
            for m in ("wk", "wv", "wr", "wo"):
                tap(m + "T", wT[m][:], [m + "T"])

        def front_A(st):
            pp = st % 2
            emit_transposes(pp)
            emit_gate(pp)

        def front_B(st):
            pp = st % 2
            emit_proj(pp, "wk", [pb[i] for i in range(4)], [f"p{i}" for i in range(4)])
            emit_scan_a(st, pp, with_scale=True, kb0=0)

        def front_C(st):
            pp = st % 2
            emit_proj(pp, "wv", [pb[i] for i in range(4)], [f"p{i}" for i in range(4)])
            emit_scan_b(st, pp)

        def front_D(st):
            pp = st % 2
            emit_proj(pp, "wr", [pb[i] for i in range(4)], [f"p{i}" for i in range(4)])

        emit_ln(0, 0, xmain_h, prefix=False)
        emit_scan_ue(0, 0)
        front_A(0)
        front_B(0)
        front_C(0)
        front_D(0)
        def mk_hooks(st):
            pp0 = (st - 1) % 2

            def h1():
                emit_y(rb0=0)
                if st > 0:
                    emit_gpre(st - 1, pp0)
                if st + 1 < NST:
                    emit_ln_load(st + 1, (st + 1) % 2, xmain_h, prefix=False)

            def h2():
                if st + 1 < NST:
                    emit_ln_math(st + 1, (st + 1) % 2, xmain_h, prefix=False)
                    emit_scan_ue(st + 1, (st + 1) % 2)

            def h3():
                if st + 1 < NST:
                    front_A(st + 1)

            def h4():
                if st > 0:
                    emit_tail2a(st - 1, pp0)

            def h5():
                if st + 1 < NST:
                    front_B(st + 1)
                if st > 0:
                    emit_tail2b(st - 1, pp0)
                emit_xt_load(st)

            def h6():
                if st + 1 < NST:
                    front_C(st + 1)

            def h8():
                if st + 1 < NST:
                    front_D(st + 1)
            return {1: h1, 2: h2, 3: h3, 4: h4, 5: h5, 6: h6, 8: h8}

        for st in range(NST):
            pp = st % 2
            emit_experts(st, pp, mk_hooks(st))
            emit_rms_wo_out(st, pp)
        emit_gpre(NST - 1, (NST - 1) % 2)
        emit_tail2(NST - 1, (NST - 1) % 2)

        P.finalize()
        for k in P.dma_semkeys:
            sems[k] = stack.enter_context(nc.semaphore("sd_" + k))
        block = stack.enter_context(nc.Block())

        @block.sync
        def _(sync):
            P.emit("sync", sync, sems)
            sync.wait_ge(sems["outhbm"], P.totals["outhbm"])
            if debug_taps and "taphbm" in sems:
                sync.wait_ge(sems["taphbm"], P.totals["taphbm"])

        @block.gpsimd
        def _(gpsimd):
            P.emit("gpsimd", gpsimd, sems)

        @block.vector
        def _(vector):
            P.emit("vector", vector, sems)

        @block.scalar
        def _(scalar):
            P.emit("scalar", scalar, sems)

        @block.tensor
        def _(tensor):
            P.emit("tensor", tensor, sems)

    return nc


_NC_CACHE = {}


def _get_nc():
    if "nc" not in _NC_CACHE:
        _NC_CACHE["nc"] = build_nc()
    return _NC_CACHE["nc"]


def kernel(L0_attn_x, L0_attn_Wr, L0_attn_Wk, L0_attn_Wv, L0_attn_Wo,
           L0_attn_decay, L0_attn_ln_w, L0_ffn_gate_w, L0_ffn_expert_w):
    x = np.ascontiguousarray(np.asarray(L0_attn_x, dtype=np.float32))
    Wr = np.ascontiguousarray(np.asarray(L0_attn_Wr, dtype=np.float32))
    Wk = np.ascontiguousarray(np.asarray(L0_attn_Wk, dtype=np.float32))
    Wv = np.ascontiguousarray(np.asarray(L0_attn_Wv, dtype=np.float32))
    Wo = np.ascontiguousarray(np.asarray(L0_attn_Wo, dtype=np.float32))
    decay = np.asarray(L0_attn_decay, dtype=np.float32)
    ln_w = np.asarray(L0_attn_ln_w, dtype=np.float32)
    gate_w = np.asarray(L0_ffn_gate_w, dtype=np.float32)
    expert_w = np.asarray(L0_ffn_expert_w, dtype=np.float32)

    ident = np.eye(128, dtype=np.float32).astype(ml_dtypes.bfloat16)
    io8 = np.broadcast_to(np.arange(8, dtype=np.float32), (128, 8)).copy()
    io8r = (8.0 - io8).astype(np.float32)
    dec8 = np.ascontiguousarray(decay.reshape(8, 128).T)
    lnw8 = np.ascontiguousarray(ln_w.reshape(8, 128).T)
    gwt = np.ascontiguousarray(gate_w.T)
    ewt = np.ascontiguousarray(expert_w.transpose(0, 2, 1))
    ar = np.arange(HALF, dtype=np.float32)
    io_lo = np.broadcast_to(ar, (128, HALF)).copy()
    io_hi = np.broadcast_to(ar + HALF, (128, HALF)).copy()
    zeros_pre = np.zeros((HALF, D), dtype=np.float32)

    in_maps = []
    for c in range(8):
        b, half = c // 2, c % 2
        in_maps.append({
            "xmain": np.ascontiguousarray(x[b, half * HALF:(half + 1) * HALF]),
            "xpre": (np.ascontiguousarray(x[b, 0:HALF]) if half else zeros_pre),
            "wk": Wk, "wv": Wv, "wr": Wr, "wo": Wo,
            "ewt": ewt, "gwt": gwt,
            "dec8": dec8, "lnw8": lnw8,
            "iomain": io_hi if half else io_lo,
            "iopre": io_lo,
            "ident": ident, "io8": io8, "io8r": io8r,
        })

    nc = _get_nc()
    res = run_bass_kernel_spmd(nc, in_maps, core_ids=list(range(8)))
    out = np.empty((B, T, D), dtype=np.float32)
    for c in range(8):
        b, half = c // 2, c % 2
        out[b, half * HALF:(half + 1) * HALF] = res.results[c]["out"]
    return out


# version 34
# speedup vs baseline: 1.0045x; 1.0045x over previous
import numpy as np
import ml_dtypes
import concourse.bass as bass
import concourse.mybir as mybir
from concourse.bass_utils import run_bass_kernel_spmd
from contextlib import ExitStack

F32 = mybir.dt.float32
BF16 = mybir.dt.bfloat16
AF = mybir.ActivationFunctionType
ALU = mybir.AluOpType
AX = mybir.AxisListType

B, T, D = 4, 4096, 1024
HALF = T // 2
ST = 256
NST = HALF // ST
NSUB = ST // 128
NCH = D // 128
CLIP_INV = 1.0e10
LN10E = 23.025850929940457

ENGINES = ("sync", "gpsimd", "vector", "scalar", "tensor")


class Op:
    __slots__ = ("engine", "fn", "deps", "kind", "ndma", "awaited", "cnt", "idx",
                 "semkey")

    def __init__(self, engine, fn, deps, kind, ndma, idx):
        self.engine = engine
        self.fn = fn
        self.deps = deps
        self.kind = kind
        self.ndma = ndma
        self.awaited = False
        self.cnt = 0
        self.idx = idx


class Prog:
    def __init__(self):
        self.ops = []
        self.writer = {}
        self.readers = {}

    def op(self, engine, fn, reads=(), writes=(), kind="comp", ndma=1):
        if kind == "dma":
            assert writes, "dma ops must name their destination buffer"
        deps = []
        for b in reads:
            w = self.writer.get(b)
            if w is not None:
                deps.append(w)
        for b in writes:
            deps.extend(self.readers.get(b, ()))
            w = self.writer.get(b)
            if w is not None:
                deps.append(w)
        seen = set()
        ud = []
        for d in deps:
            if id(d) not in seen:
                seen.add(id(d))
                ud.append(d)
        h = Op(engine, fn, ud, kind, ndma, len(self.ops))
        h.semkey = writes[0] if kind == "dma" else engine
        self.ops.append(h)
        for b in reads:
            self.readers.setdefault(b, []).append(h)
        for b in writes:
            self.writer[b] = h
            self.readers[b] = []
        return h

    def finalize(self):
        for h in self.ops:
            for d in h.deps:
                d.awaited = True
        cnt = {e: 0 for e in ENGINES}
        for h in self.ops:
            if h.kind == "dma":
                cnt.setdefault(h.semkey, 0)
                cnt[h.semkey] += 16 * h.ndma
            elif h.awaited:
                cnt[h.engine] += 1
            h.cnt = cnt[h.semkey]
        self.totals = cnt
        self.dma_semkeys = sorted(k for k in cnt if k not in ENGINES)

    def emit(self, eng_name, eng, sems):
        last = {}
        for h in self.ops:
            if h.engine != eng_name:
                continue
            for d in h.deps:
                if (h.engine == "tensor" and d.engine == "tensor"
                        and d.kind != "dma"):
                    continue
                if not d.awaited or d.cnt <= last.get(d.semkey, 0):
                    continue
                eng.wait_ge(sems[d.semkey], d.cnt)
                last[d.semkey] = d.cnt
            insts = h.fn(eng)
            if h.kind == "dma":
                for i in insts:
                    i.then_inc(sems[h.semkey], 16)
            elif h.awaited:
                insts[-1].then_inc(sems[h.engine], 1)


def build_nc(debug_taps=False):
    nc = bass.Bass("TRN2", target_bir_lowering=False)

    xmain_h = nc.dram_tensor("xmain", [HALF, D], F32, kind="ExternalInput")
    xpre_h = nc.dram_tensor("xpre", [HALF, D], F32, kind="ExternalInput")
    w_h = {m: nc.dram_tensor(m, [D, D], F32, kind="ExternalInput")
           for m in ("wk", "wv", "wr", "wo")}
    ewt_h = nc.dram_tensor("ewt", [8, D, D], F32, kind="ExternalInput")
    gwt_h = nc.dram_tensor("gwt", [D, 8], F32, kind="ExternalInput")
    dec_h = nc.dram_tensor("dec8", [128, 8], F32, kind="ExternalInput")
    lnw_h = nc.dram_tensor("lnw8", [128, 8], F32, kind="ExternalInput")
    iom_h = nc.dram_tensor("iomain", [128, HALF], F32, kind="ExternalInput")
    iop_h = nc.dram_tensor("iopre", [128, HALF], F32, kind="ExternalInput")
    id_h = nc.dram_tensor("ident", [128, 128], BF16, kind="ExternalInput")
    io8_h = nc.dram_tensor("io8", [128, 8], F32, kind="ExternalInput")
    io8r_h = nc.dram_tensor("io8r", [128, 8], F32, kind="ExternalInput")
    out_h = nc.dram_tensor("out", [HALF, D], F32, kind="ExternalOutput")
    escr_h = nc.dram_tensor("escr", [8, 128, NCH * D], BF16, kind="Internal")

    P = Prog()
    tap_handles = {}

    with ExitStack() as stack:
        def sb(name, shape, dt=F32):
            return stack.enter_context(nc.sbuf_tensor(name, shape, dt))

        wT = {m: sb(m + "T", [128, NCH * D], BF16) for m in ("wk", "wv", "wr", "wo")}
        ebuf = [sb(f"ebuf{i}", [128, NCH * D], BF16) for i in range(2)]
        gws = sb("gws", [128, NCH * 8], BF16)
        gwsf = sb("gwsf", [128, NCH * 8], F32)
        xb = [sb(f"xb{i}", [128, NSUB * D]) for i in range(2)]
        iot = [sb(f"iot{i}", [128, ST]) for i in range(2)]
        xn_tm = [sb(f"xn_tm{i}", [128, NSUB * D], BF16) for i in range(2)]
        xn_fm = [sb(f"xn_fm{i}", [128, NCH * ST], BF16) for i in range(2)]
        ybf = sb("ybf", [128, NCH * ST], BF16)
        y2g = sb("y2g", [128, NCH * ST], BF16)
        moe = [sb(f"moe{i}", [128, NSUB * D], BF16) for i in range(2)]
        sA = sb("sA", [128, NCH * ST])
        sB = sb("sB", [128, NCH * ST])
        sC = sb("sC", [128, NCH * ST])
        sD = sb("sD", [128, NCH * ST])
        s_sb = sb("s_sb", [128, NSUB * D])
        stageQ = [sb(f"stageQ{i}", [128, D]) for i in range(2)]
        stageP = [sb(f"stageP{i}", [128, D]) for i in range(2)]
        z256 = sb("z256", [128, ST])
        ident = sb("identb", [128, 128], BF16)
        onesb = sb("onesb", [128, 1], BF16)
        io8 = sb("io8b", [128, 8])
        io8r = sb("io8rb", [128, 8])
        decs = sb("decs", [128, 8])
        sgd = sb("sgd", [128, 8])
        dmx = sb("dmx", [128, 8])
        Lc = sb("Lc", [128, 8])
        nLc = sb("nLc", [128, 8])
        lnw = sb("lnwb", [128, 8])
        carry = sb("carryb", [128, NCH])
        tcols = sb("tcols", [128, NST * NCH])
        tc2 = sb("tc2", [128, 4 * NCH])
        scb = sb("scb", [128, 2 * 8])
        ohb = [sb(f"ohb{i}", [128, 2 * 8]) for i in range(2)]
        t8a = sb("t8a", [128, 8])
        t8b = sb("t8b", [128, 8])
        t8c = sb("t8c", [128, 8])
        t8d = sb("t8d", [128, 8])
        gmx = sb("gmx", [128, 2])
        gmx2 = sb("gmx2", [128, 2])
        m3g = sb("m3g", [128, 16])
        mx3 = sb("mx3", [128, 2])
        idxc = sb("idxc", [128, 2])
        mcol = sb("mcol", [128, 2])
        sq2 = sb("sq2", [128, 2])
        t1c = sb("t1c", [128, 2])
        s2c = sb("s2c", [128, 2])
        nvar = sb("nvar", [128, 2])
        lnc = sb("lncol", [128, 2])
        acol = sb("acol", [128, 2])
        acol2 = sb("acol2", [128, 2])
        nbc = sb("nbc", [128, 2])
        rsv = sb("rsv", [128, 2])
        rsv2 = sb("rsv2", [128, 2])
        rsl = sb("rsl", [128, 2])
        cst = sb("cstb", [128, 4])
        qa = sb("qa", [128, 1])
        qs = sb("qs", [128, 1])
        qsh = sb("qsh", [128, 1])
        qnsh = sb("qnsh", [128, 1])
        qwb = sb("qwb", [128, D], BF16)

        pb = [stack.enter_context(nc.psum_tensor(f"pb{i}", [128, 512], F32))
              for i in range(8)]
        sems = {e: stack.enter_context(nc.semaphore("sem_" + e)) for e in ENGINES}
        block = None

        op = P.op

        def tap(name, ap, readkeys):
            if not debug_taps:
                return
            th = nc.dram_tensor("tap_" + name, list(ap.shape), ap.dtype,
                                kind="ExternalOutput")
            tap_handles[name] = th
            op("sync", lambda e, th=th, ap=ap: [e.dma_start(th[:], ap)],
               reads=list(readkeys), writes=["taphbm"], kind="dma")

        op("sync", lambda e: [e.dma_start(ident[:], id_h[:])], writes=["ident"], kind="dma")
        op("sync", lambda e: [e.dma_start(io8[:], io8_h[:])], writes=["io8"], kind="dma")
        op("sync", lambda e: [e.dma_start(io8r[:], io8r_h[:])], writes=["io8r"], kind="dma")
        op("sync", lambda e: [e.dma_start(decs[:], dec_h[:])], writes=["decs"], kind="dma")
        op("sync", lambda e: [e.dma_start(lnw[:], lnw_h[:])], writes=["lnw"], kind="dma")
        op("sync", lambda e: [e.dma_start(gwsf[:, kc * 8:(kc + 1) * 8],
                                          gwt_h[kc * 128:(kc + 1) * 128, :])
                              for kc in range(NCH)],
           writes=["gwsf"], kind="dma", ndma=NCH)
        op("vector", lambda e: [e.memset(z256[:], 0.0)], writes=["z256"])
        op("vector", lambda e: [e.memset(onesb[:], 1.0)], writes=["onesb"])
        op("vector", lambda e: [e.memset(carry[:], 0.0)], writes=["carry"])
        op("vector", lambda e: [e.memset(cst[:, 0:1], 1e-5)], writes=["cst"])
        op("vector", lambda e: [e.memset(cst[:, 1:2], 1e-6)], writes=["cst"])
        op("vector", lambda e: [e.memset(cst[:, 2:3], 1.0)], writes=["cst"])
        op("vector", lambda e: [e.memset(cst[:, 3:4], -8.0)], writes=["cst"])
        op("vector", lambda e: [e.tensor_copy(gws[:], gwsf[:])], reads=["gwsf"], writes=["gws"])
        op("scalar", lambda e: [e.activation(sgd[:], decs[:], AF.Sigmoid)],
           reads=["decs"], writes=["sgd"])
        op("vector", lambda e: [e.tensor_scalar(dmx[:], sgd[:], 1e-7, None, ALU.max)],
           reads=["sgd"], writes=["dmx"])
        op("scalar", lambda e: [e.activation(Lc[:], dmx[:], AF.Ln)],
           reads=["dmx"], writes=["Lc"])
        op("vector", lambda e: [e.tensor_scalar(nLc[:], Lc[:], -1.0, None, ALU.mult)],
           reads=["Lc"], writes=["nLc"])

        import numpy as _np
        _eps = float(_np.float32(1e-5))

        def emit_quant_round(m, c):
            wTm = wT[m]
            sg = stageQ[c % 2]
            skey = f"stageQ{c % 2}"
            op("sync", lambda e, m=m, c=c, sg=sg: [
                e.dma_start(sg[:], w_h[m][c * 128:(c + 1) * 128, :])],
                writes=[skey], kind="dma")
            qta = sA[:, 0:D]
            qtb = sB[:, 0:D]
            qwq = sC[:, 0:D]
            qwbf = qwb[:]
            op("scalar", lambda e, sg=sg, qta=qta: [
                e.activation(qta, sg[:], AF.Abs, accum_out=qa[:, 0:1])],
               reads=[skey], writes=["sA", "qa"])
            op("gpsimd", lambda e: [
                e.tensor_scalar(qs[:], qa[:], 1.0 / D, _eps, ALU.mult, ALU.max),
                e.tensor_scalar(qsh[:], qa[:], 0.5 / D, 0.5 * _eps, ALU.mult, ALU.max)],
               reads=["qa"], writes=["qs", "qsh"])
            op("scalar", lambda e, sg=sg, qtb=qtb: [
                e.activation(qtb, sg[:], AF.Sign)],
               reads=[skey], writes=["sB"])
            op("vector", lambda e, qwq=qwq, qta=qta: [
                e.tensor_scalar(qwq, qta, qsh[:, 0:1], qs[:, 0:1],
                                ALU.is_ge, ALU.mult)],
               reads=["sA", "qsh", "qs"], writes=["sC"])
            op("vector", lambda e, qwbf=qwbf, qwq=qwq, qtb=qtb: [
                e.tensor_tensor(qwbf, qwq, qtb, ALU.mult)],
               reads=["sC", "sB"], writes=["qwb"])
            for q in range(2):
                bank = pb[q]

                def trq(e, q=q, bank=bank, qwbf=qwbf):
                    return [e.matmul(bank[:, i2 * 128:(i2 + 1) * 128],
                                     qwbf[:, (4 * q + i2) * 128:(4 * q + i2 + 1) * 128],
                                     ident[:], start=True, stop=True)
                            for i2 in range(4)]
                op("tensor", trq, reads=["qwb", "ident"], writes=[f"p{q}"])

                def evq(e, q=q, bank=bank, m=m, c=c, wTm=wTm):
                    ins = []
                    for i2 in range(4):
                        kc = 4 * q + i2
                        dst = wTm[:, kc * D + c * 128: kc * D + (c + 1) * 128]
                        src = bank[:, i2 * 128:(i2 + 1) * 128]
                        if m == "wo":
                            ins.append(e.tensor_scalar(dst, src, lnw[:, kc:kc + 1], None, ALU.mult))
                        else:
                            ins.append(e.tensor_copy(dst, src))
                    return ins
                op("vector", evq, reads=[f"p{q}", "lnw"], writes=[m + "T"])

        def quantize_matrix(m):
            for c in range(8):
                emit_quant_round(m, c)

        quantize_matrix("wk")
        quantize_matrix("wv")

        def emit_prep_expert(ex):
            kb = ebuf[ex % 2]
            bkey = f"ebuf{ex % 2}"

            def ldcast(e, ex=ex, kb=kb):
                return [e.dma_start(kb[:, hh * D:(hh + 1) * D],
                                    ewt_h[ex, hh * 128:(hh + 1) * 128, :])
                        for hh in range(8)]
            op("gpsimd", ldcast, writes=[bkey], kind="dma", ndma=8)
            op("gpsimd", lambda e, ex=ex, kb=kb: [e.dma_start(escr_h[ex], kb[:])],
               reads=[bkey], writes=[f"escr{ex}"], kind="dma")

        MAGICBIG = None

        def emit_ln_load(st, pp, x_hbm, prefix):
            xs = xb[pp]
            io_hbm = iop_h if prefix else iom_h

            def ldx(e, st=st, xs=xs, x_hbm=x_hbm):
                return [e.dma_start(xs[:, j * D:(j + 1) * D],
                                    x_hbm[st * ST + j * 128: st * ST + (j + 1) * 128, :])
                        for j in range(NSUB)]
            op("sync", ldx, writes=[f"x{pp}"], kind="dma", ndma=NSUB)
            op("sync", lambda e, st=st: [e.dma_start(iot[pp][:], io_hbm[:, st * ST:(st + 1) * ST])],
               writes=[f"iot{pp}"], kind="dma")

        def emit_ln_math(st, pp, x_hbm, prefix):
            xs = xb[pp]
            xnt = xn_tm[pp]
            if prefix:
                op("vector", lambda e, xs=xs: [
                    e.reduce_sum(t1c[:, j:j + 1], xs[:, j * D:(j + 1) * D], axis=AX.X)
                    for j in range(NSUB)],
                    reads=[f"x{pp}"], writes=["t1c"])
            else:
                op("scalar", lambda e, xs=xs: [
                    e.activation(sA[:, j * D:(j + 1) * D], xs[:, j * D:(j + 1) * D],
                                 AF.Identity, accum_out=t1c[:, j:j + 1])
                    for j in range(NSUB)],
                    reads=[f"x{pp}"], writes=["sA", "t1c"])
            op("scalar", lambda e: [
                e.mul(mcol[:, j:j + 1], t1c[:, j:j + 1], 1.0 / D)
                for j in range(NSUB)],
                reads=["t1c"], writes=["mcol"])
            op("scalar", lambda e, xs=xs: [
                e.activation(sA[:, j * D:(j + 1) * D], xs[:, j * D:(j + 1) * D],
                             AF.Square, accum_out=sq2[:, j:j + 1])
                for j in range(NSUB)],
                reads=[f"x{pp}"], writes=["sA", "sq2"])
            op("scalar", lambda e: [
                e.mul(s2c[:, j:j + 1], sq2[:, j:j + 1], 1.0 / D)
                for j in range(NSUB)],
                reads=["sq2"], writes=["s2c"])
            op("vector", lambda e: [
                e.scalar_tensor_tensor(nvar[:, j:j + 1], mcol[:, j:j + 1],
                                       mcol[:, j:j + 1], s2c[:, j:j + 1],
                                       ALU.mult, ALU.subtract)
                for j in range(NSUB)],
                reads=["mcol", "s2c"], writes=["nvar"])
            op("scalar", lambda e: [
                e.activation(lnc[:, j:j + 1], nvar[:, j:j + 1], AF.Ln,
                             bias=cst[:, 0:1], scale=-1.0)
                for j in range(NSUB)],
                reads=["nvar", "cst"], writes=["lncol"])
            op("scalar", lambda e: [
                e.activation(acol[:, j:j + 1], lnc[:, j:j + 1], AF.Exp, scale=-0.5)
                for j in range(NSUB)],
                reads=["lncol"], writes=["acol"])
            if prefix:
                op("vector", lambda e, xs=xs, xnt=xnt: [
                    e.tensor_scalar(xnt[:, j * D:(j + 1) * D], xs[:, j * D:(j + 1) * D],
                                    mcol[:, j:j + 1], acol[:, j:j + 1],
                                    ALU.subtract, ALU.mult)
                    for j in range(NSUB)],
                    reads=[f"x{pp}", "mcol", "acol"], writes=[f"xn_tm{pp}"])
            else:
                op("vector", lambda e: [
                    e.tensor_scalar(acol2[:, 0:NSUB], acol[:, 0:NSUB], 1.0,
                                    None, ALU.mult)],
                   reads=["acol"], writes=["acol2"])
                op("vector", lambda e: [
                    e.scalar_tensor_tensor(nbc[:, 0:NSUB], mcol[:, 0:NSUB], -1.0,
                                           acol[:, 0:NSUB], ALU.mult, ALU.mult)],
                   reads=["mcol", "acol"], writes=["nbc"])
                op("scalar", lambda e, xs=xs, xnt=xnt: [
                    e.activation(xnt[:, j * D:(j + 1) * D], xs[:, j * D:(j + 1) * D],
                                 AF.Identity, bias=nbc[:, j:j + 1],
                                 scale=acol2[:, j:j + 1])
                    for j in range(NSUB)],
                    reads=[f"x{pp}", "nbc", "acol2"], writes=[f"xn_tm{pp}"])

        def emit_ln(st, pp, x_hbm, prefix):
            emit_ln_load(st, pp, x_hbm, prefix)
            emit_ln_math(st, pp, x_hbm, prefix)

        def emit_transposes(pp):
            xnt, xnf = xn_tm[pp], xn_fm[pp]
            for g in range(4):
                bank = pb[g % 2]

                def trg(e, g=g, bank=bank, xnt=xnt):
                    ins = []
                    for u2 in range(4):
                        c = 2 * g + u2 // 2
                        j = u2 % 2
                        ins.append(e.matmul(bank[:, u2 * 128:(u2 + 1) * 128],
                                            xnt[:, j * D + c * 128: j * D + (c + 1) * 128],
                                            ident[:], start=True, stop=True))
                    return ins
                op("tensor", trg, reads=[f"xn_tm{pp}", "ident"], writes=[f"p{g % 2}"])
                op("scalar", lambda e, g=g, bank=bank, xnf=xnf: [
                    e.copy(xnf[:, (2 * g) * ST:(2 * g + 2) * ST], bank[:, 0:512])],
                    reads=[f"p{g % 2}"], writes=[f"xn_fm{pp}"])

        def emit_gate(pp, dbg=False):
            xnf = xn_fm[pp]
            ohx = ohb[pp]
            for j in range(NSUB):
                gb = pb[2 + j]
                gbk = f"p{2 + j}"

                def gmm(e, j=j, xnf=xnf, gb=gb):
                    ins = []
                    for kc in range(NCH):
                        ins.append(e.matmul(gb[:, j * 8:(j + 1) * 8],
                                            xnf[:, kc * ST + j * 128: kc * ST + (j + 1) * 128],
                                            gws[:, kc * 8:(kc + 1) * 8],
                                            start=(kc == 0), stop=(kc == NCH - 1)))
                    return ins
                op("tensor", gmm, reads=[f"xn_fm{pp}", "gws"], writes=[gbk])
                op("scalar", lambda e, j=j, gb=gb: [
                    e.copy(scb[:, j * 8:(j + 1) * 8], gb[:, j * 8:(j + 1) * 8])],
                    reads=[gbk], writes=["scb"])
            for j in range(NSUB):
                op("vector", lambda e, j=j: [
                    e.reduce_max(gmx[:, j:j + 1], scb[:, j * 8:(j + 1) * 8], axis=AX.X)],
                    reads=["scb"], writes=["gmx"])
            op("scalar", lambda e: [
                e.mul(gmx2[:], gmx[:], -1.0)],
                reads=["gmx"], writes=["gmx2"])
            for j, (ta, tb) in enumerate(((t8a, t8b), (t8c, t8d))):
                ka, kb = f"t8{j}a", f"t8{j}b"
                op("scalar", lambda e, j=j, ta=ta: [
                    e.activation(ta[:], scb[:, j * 8:(j + 1) * 8], AF.Identity,
                                 bias=gmx2[:, j:j + 1])],
                    reads=["scb", "gmx2"], writes=[ka])
                op("scalar", lambda e, ta=ta, tb=tb: [
                    e.activation(tb[:], ta[:], AF.Sign)],
                    reads=[ka], writes=[kb])
                op("scalar", lambda e, ta=ta, tb=tb: [
                    e.activation(ta[:], tb[:], AF.Abs)],
                    reads=[kb], writes=[ka])
                op("scalar", lambda e, ta=ta, tb=tb: [
                    e.activation(tb[:], ta[:], AF.Identity, bias=cst[:, 2:3], scale=-1.0)],
                    reads=[ka, "cst"], writes=[kb])
                op("vector", lambda e, ta=ta, tb=tb: [
                    e.tensor_tensor(ta[:], tb[:], io8r[:], ALU.mult)],
                    reads=[kb, "io8r"], writes=[ka])
            op("scalar", lambda e: [
                e.copy(m3g[:, 0:8], t8a[:]),
                e.copy(m3g[:, 8:16], t8c[:])],
                reads=["t80a", "t81a"], writes=["m3g"])
            for j in range(NSUB):
                op("vector", lambda e, j=j: [
                    e.reduce_max(mx3[:, j:j + 1], m3g[:, j * 8:(j + 1) * 8], axis=AX.X)],
                    reads=["m3g"], writes=["mx3"])
            op("scalar", lambda e: [
                e.activation(idxc[:, j:j + 1], mx3[:, j:j + 1], AF.Identity,
                             bias=cst[:, 3:4])
                for j in range(NSUB)],
                reads=["mx3", "cst"], writes=["idxc"])
            for j, (ta, tb) in enumerate(((t8a, t8b), (t8c, t8d))):
                ka, kb = f"t8{j}a", f"t8{j}b"
                op("scalar", lambda e, j=j, ta=ta: [
                    e.activation(ta[:], io8[:], AF.Identity, bias=idxc[:, j:j + 1])],
                    reads=["io8", "idxc"], writes=[ka])
                op("scalar", lambda e, ta=ta, tb=tb: [
                    e.activation(tb[:], ta[:], AF.Sign)],
                    reads=[ka], writes=[kb])
                op("scalar", lambda e, ta=ta, tb=tb: [
                    e.activation(ta[:], tb[:], AF.Abs)],
                    reads=[kb], writes=[ka])
                op("scalar", lambda e, j=j, ta=ta, ohx=ohx: [
                    e.activation(ohx[:, j * 8:(j + 1) * 8], ta[:], AF.Identity,
                                 bias=cst[:, 2:3], scale=-1.0)],
                    reads=[ka, "cst"], writes=[f"ohb{pp}"])
            if dbg:
                tap("idxc_g", idxc[:], ["idxc"])

        def emit_proj(pp, m, banks, bankkeys):
            xnf = xn_fm[pp]
            for i in range(4):
                bank = banks[i]

                def pmm(e, i=i, bank=bank, m=m, xnf=xnf):
                    ins = []
                    for ce in (2 * i, 2 * i + 1):
                        for kc in range(NCH):
                            ins.append(e.matmul(
                                bank[:, (ce % 2) * ST:(ce % 2) * ST + ST],
                                wT[m][:, kc * D + ce * 128: kc * D + (ce + 1) * 128],
                                xnf[:, kc * ST:(kc + 1) * ST],
                                start=(kc == 0), stop=(kc == NCH - 1)))
                    return ins
                op("tensor", pmm, reads=[m + "T", f"xn_fm{pp}"], writes=[bankkeys[i]])

        def emit_scan_ue(st, pp):
            iost = iot[pp]
            op("vector", lambda e, iost=iost: [
                e.tensor_scalar(sA[:, c * ST:(c + 1) * ST], iost[:],
                                nLc[:, c:c + 1], 23.025850929940457,
                                ALU.mult, ALU.min)
                for c in range(NCH)],
                reads=[f"iot{pp}", "nLc"], writes=["sA"])
            op("scalar", lambda e: [e.activation(sB[:], sA[:], AF.Exp)],
               reads=["sA"], writes=["sB"])

        def emit_scan_a(st, pp, with_scale, kb0=4):
            iost = iot[pp]
            op("vector", lambda e, kb0=kb0: [
                e.scalar_tensor_tensor(sC[:, 2 * i * ST:(2 * i + 2) * ST],
                                       sB[:, 2 * i * ST:(2 * i + 2) * ST],
                                       CLIP_INV, pb[kb0 + i][:, 0:512],
                                       ALU.min, ALU.mult)
                for i in range(4)],
                reads=["sB"] + [f"p{kb0 + i}" for i in range(4)], writes=["sC"])
            if with_scale:
                op("scalar", lambda e, iost=iost: [
                    e.activation(sA[:, c * ST:(c + 1) * ST], iost[:], AF.Exp,
                                 scale=Lc[:, c:c + 1])
                    for c in range(NCH)],
                    reads=[f"iot{pp}", "Lc"], writes=["sA"])

        def emit_scan_b(st, pp, prefix_st=None):
            if prefix_st is not None:
                op("vector", lambda e, k=prefix_st: [
                    e.scalar_tensor_tensor(sD[:, c * ST:(c + 1) * ST],
                                           pb[c // 2][:, (c % 2) * ST:(c % 2) * ST + ST],
                                           1.0,
                                           sC[:, c * ST:(c + 1) * ST],
                                           ALU.mult, ALU.mult,
                                           accum_out=tcols[:, k * 8 + c:k * 8 + c + 1])
                    for c in range(NCH)],
                    reads=["p0", "p1", "p2", "p3", "sC"], writes=["sD", "tcols"])
                return
            op("vector", lambda e: [
                e.tensor_tensor(sD[:, 2 * i * ST:(2 * i + 2) * ST],
                                pb[i][:, 0:512],
                                sC[:, 2 * i * ST:(2 * i + 2) * ST], ALU.mult)
                for i in range(4)],
                reads=["p0", "p1", "p2", "p3", "sC"], writes=["sD"])
            op("vector", lambda e: [
                e.tensor_tensor_scan(sC[:, c * ST:(c + 1) * ST],
                                     sD[:, c * ST:(c + 1) * ST], z256[:],
                                     carry[:, c:c + 1], ALU.add, ALU.add)
                for c in range(NCH)],
                reads=["sD", "z256", "carry"], writes=["sC"])
            op("scalar", lambda e: [
                e.copy(carry[:, c:c + 1], sC[:, c * ST + ST - 1: c * ST + ST])
                for c in range(NCH)],
                reads=["sC"], writes=["carry"])

        def emit_y(rb0=4):
            op("scalar", lambda e, rb0=rb0: [
                e.activation(sD[:, 2 * i * ST:(2 * i + 2) * ST],
                             pb[rb0 + i][:, 0:512], AF.Sigmoid)
                for i in range(4)],
                reads=[f"p{rb0 + i}" for i in range(4)], writes=["sD"])
            op("vector", lambda e: [e.tensor_tensor(sB[:], sD[:], sA[:], ALU.mult)],
               reads=["sD", "sA"], writes=["sB"])
            op("vector", lambda e: [e.tensor_tensor(ybf[:], sC[:], sB[:], ALU.mult)],
               reads=["sC", "sB"], writes=["ybf"])
            op("scalar", lambda e: [e.activation(y2g[:], ybf[:], AF.Square)],
               reads=["ybf"], writes=["y2g"])

        esched = []
        for _p in range(NST):
            _order = range(7, -1, -1) if _p % 2 == 0 else range(8)
            for _i, _ex in enumerate(_order):
                _buf = (_i + 1) % 2 if _p % 2 == 0 else _i % 2
                esched.append((_ex, _buf, _i >= 2))

        def emit_expert_load(g):
            if g >= len(esched):
                return
            ex, buf, needs = esched[g]
            if not needs:
                return
            op("gpsimd", lambda e, ex=ex, kbuf=ebuf[buf]: [e.dma_start(kbuf[:], escr_h[ex])],
               reads=[f"escr{ex}"], writes=[f"ebuf{buf}"], kind="dma")

        def emit_experts(st, pp, hooks=None):
            xnf = xn_fm[pp]
            for i in range(8):
                if hooks and i in hooks:
                    hooks[i]()
                ex, buf, _ = esched[st * 8 + i]
                kbuf = ebuf[buf]
                bkey = f"ebuf{buf}"
                for j in range(NSUB):
                    for nh in range(2):
                        bank = pb[4 + j * 2 + nh]
                        bk = f"p{4 + j * 2 + nh}"

                        def emm(e, j=j, nh=nh, bank=bank, kbuf=kbuf, xnf=xnf):
                            ins = []
                            for kc in range(NCH):
                                ins.append(e.matmul(
                                    bank[:, 0:512],
                                    xnf[:, kc * ST + j * 128: kc * ST + (j + 1) * 128],
                                    kbuf[:, kc * D + nh * 512: kc * D + nh * 512 + 512],
                                    start=(kc == 0), stop=(kc == NCH - 1)))
                            return ins
                        op("tensor", emm, reads=[f"xn_fm{pp}", bkey], writes=[bk])
                        dst = moe[(i + 1) % 2][:, j * D + nh * 512: j * D + nh * 512 + 512]
                        ohcol = ohb[pp][:, j * 8 + ex: j * 8 + ex + 1]
                        if i == 0:
                            op("vector", lambda e, dst=dst, bank=bank, ohcol=ohcol: [
                                e.tensor_scalar(dst, bank[:, 0:512], ohcol, None, ALU.mult)],
                                reads=[bk, f"ohb{pp}"], writes=["moe1"])
                        else:
                            src2 = moe[i % 2][:, j * D + nh * 512: j * D + nh * 512 + 512]
                            op("vector", lambda e, dst=dst, bank=bank, ohcol=ohcol, src2=src2: [
                                e.scalar_tensor_tensor(dst, bank[:, 0:512], ohcol, src2,
                                                       ALU.mult, ALU.add)],
                                reads=[bk, f"ohb{pp}", f"moe{i % 2}"], writes=[f"moe{(i + 1) % 2}"])
                emit_expert_load(st * 8 + i + 2)
            if hooks and 8 in hooks:
                hooks[8]()

        def emit_rms_wo_out(st, pp):
            xs = xb[pp]
            for j in range(NSUB):
                def s2mm(e, j=j):
                    ins = []
                    for kc in range(NCH):
                        ins.append(e.matmul(pb[5][:, 16 + j:17 + j],
                                            y2g[:, kc * ST + j * 128: kc * ST + (j + 1) * 128],
                                            onesb[:],
                                            start=(kc == 0), stop=(kc == NCH - 1)))
                    return ins
                op("tensor", s2mm, reads=["y2g", "onesb"], writes=["p5"])
                op("scalar", lambda e, j=j: [
                    e.activation(rsl[:, j:j + 1], pb[5][:, 16 + j:17 + j], AF.Ln,
                                 bias=cst[:, 1:2], scale=1.0 / D)],
                    reads=["p5", "cst"], writes=["rsl"])
                op("scalar", lambda e, j=j: [
                    e.activation(rsv[:, j:j + 1], rsl[:, j:j + 1], AF.Exp, scale=-0.5)],
                    reads=["rsl"], writes=["rsv"])
            op("vector", lambda e: [
                e.tensor_scalar(rsv2[:, 0:NSUB], rsv[:, 0:NSUB], 1.0, None, ALU.mult)],
               reads=["rsv"], writes=["rsv2"])
            wob = {0: (4, 6), 1: (7, 5)}
            for j in range(NSUB):
                for nh in range(2):
                    bank = pb[wob[j][nh]]
                    bk = f"p{wob[j][nh]}"

                    def womm(e, j=j, nh=nh, bank=bank):
                        ins = []
                        for kc in range(NCH):
                            ins.append(e.matmul(
                                bank[:, 0:512],
                                ybf[:, kc * ST + j * 128: kc * ST + (j + 1) * 128],
                                wT["wo"][:, kc * D + nh * 512: kc * D + nh * 512 + 512],
                                start=(kc == 0), stop=(kc == NCH - 1)))
                        return ins
                    op("tensor", womm, reads=["ybf", "woT"], writes=[bk])
                    op("scalar", lambda e, j=j, nh=nh, bank=bank: [
                        e.activation(s_sb[:, j * D + nh * 512: j * D + nh * 512 + 512],
                                     bank[:, 0:512], AF.Identity,
                                     scale=rsv2[:, j:j + 1])],
                        reads=[bk, "rsv2"], writes=["s_sb"])
        def emit_gpre(st, pp):
            for j in range(NSUB):
                op("vector", lambda e, j=j, pp=pp: [
                    e.tensor_tensor(stageP[j][:], moe[0][:, j * D:(j + 1) * D],
                                    xn_tm[pp][:, j * D:(j + 1) * D], ALU.add)],
                    reads=["moe0", f"xn_tm{pp}"], writes=[f"stageP{j}"])

        def emit_xt_load(st):
            def ldxt(e, st=st):
                return [e.dma_start(stageQ[j][:],
                                    xmain_h[st * ST + j * 128: st * ST + (j + 1) * 128, :])
                        for j in range(NSUB)]
            op("sync", ldxt, writes=["stageQ0", "stageQ1"], kind="dma", ndma=NSUB)

        def emit_tail2a(st, pp):
            op("scalar", lambda e: [
                e.activation(sD[:, 0:D], stageP[0][:], AF.Sigmoid),
                e.activation(sD[:, D:2 * D], stageP[1][:], AF.Sigmoid)],
               reads=["stageP0", "stageP1"], writes=["sD"])
            for j in range(NSUB):
                op("vector", lambda e, j=j: [
                    e.tensor_tensor(sB[:, j * D:(j + 1) * D], stageP[j][:],
                                    sD[:, j * D:(j + 1) * D], ALU.mult)],
                    reads=[f"stageP{j}", "sD"], writes=["sB"])

        def emit_tail2b(st, pp):
            for j in range(NSUB):
                op("vector", lambda e, j=j: [
                    e.tensor_tensor(sD[:, j * D:(j + 1) * D], s_sb[:, j * D:(j + 1) * D],
                                    sB[:, j * D:(j + 1) * D], ALU.mult)],
                    reads=["s_sb", "sB"], writes=["sD"])
                op("vector", lambda e, j=j: [
                    e.tensor_tensor(s_sb[:, j * D:(j + 1) * D], stageQ[j][:],
                                    sD[:, j * D:(j + 1) * D], ALU.add)],
                    reads=[f"stageQ{j}", "sD"], writes=["s_sb"])
                op("sync", lambda e, st=st, j=j: [
                    e.dma_start(out_h[st * ST + j * 128: st * ST + (j + 1) * 128, :],
                                s_sb[:, j * D:(j + 1) * D])],
                    reads=["s_sb"], writes=["outhbm"], kind="dma")

        def emit_tail2(st, pp):
            emit_tail2a(st, pp)
            emit_tail2b(st, pp)

        quantize_matrix("wk")
        quantize_matrix("wv")
        emit_ln(0, 0, xpre_h, prefix=True)
        emit_scan_ue(0, 0)
        for st in range(NST):
            pp = st % 2
            emit_transposes(pp)
            emit_proj(pp, "wk", [pb[4 + i] for i in range(4)], [f"p{4 + i}" for i in range(4)])
            emit_proj(pp, "wv", [pb[i] for i in range(4)], [f"p{i}" for i in range(4)])
            if st + 1 < NST:
                emit_ln(st + 1, (st + 1) % 2, xpre_h, prefix=True)
            emit_scan_a(st, pp, with_scale=False, kb0=4)
            if st + 1 < NST:
                emit_scan_ue(st + 1, (st + 1) % 2)
            emit_scan_b(st, pp, prefix_st=st)
            emit_quant_round("wr" if st < 4 else "wo", 2 * (st % 4))
            emit_quant_round("wr" if st < 4 else "wo", 2 * (st % 4) + 1)
            emit_prep_expert(st)
        op("vector", lambda e: [
            e.tensor_tensor(tc2[:], tcols[:, 0:32], tcols[:, 32:64], ALU.add)],
            reads=["tcols"], writes=["tc2"])
        op("vector", lambda e: [
            e.tensor_tensor(tcols[:, 0:16], tc2[:, 0:16], tc2[:, 16:32], ALU.add)],
            reads=["tc2"], writes=["tcols"])
        op("vector", lambda e: [
            e.tensor_tensor(carry[:], tcols[:, 0:8], tcols[:, 8:16], ALU.add)],
            reads=["tcols"], writes=["carry"])
        if debug_taps:
            for m in ("wk", "wv", "wr", "wo"):
                tap(m + "T", wT[m][:], [m + "T"])

        def front_A(st):
            pp = st % 2
            emit_transposes(pp)
            emit_gate(pp)

        def front_B(st):
            pp = st % 2
            emit_proj(pp, "wk", [pb[i] for i in range(4)], [f"p{i}" for i in range(4)])
            emit_scan_a(st, pp, with_scale=True, kb0=0)

        def front_C(st):
            pp = st % 2
            emit_proj(pp, "wv", [pb[i] for i in range(4)], [f"p{i}" for i in range(4)])
            emit_scan_b(st, pp)

        def front_D(st):
            pp = st % 2
            emit_proj(pp, "wr", [pb[i] for i in range(4)], [f"p{i}" for i in range(4)])

        emit_ln(0, 0, xmain_h, prefix=False)
        emit_scan_ue(0, 0)
        front_A(0)
        front_B(0)
        front_C(0)
        front_D(0)
        def mk_hooks(st):
            pp0 = (st - 1) % 2

            def h1():
                emit_y(rb0=0)
                if st > 0:
                    emit_gpre(st - 1, pp0)
                if st + 1 < NST:
                    emit_ln_load(st + 1, (st + 1) % 2, xmain_h, prefix=False)
                    emit_ln_math(st + 1, (st + 1) % 2, xmain_h, prefix=False)

            def h2():
                if st + 1 < NST:
                    emit_scan_ue(st + 1, (st + 1) % 2)

            def h3():
                if st + 1 < NST:
                    front_A(st + 1)

            def h4():
                if st > 0:
                    emit_tail2a(st - 1, pp0)

            def h5():
                if st + 1 < NST:
                    front_B(st + 1)
                if st > 0:
                    emit_tail2b(st - 1, pp0)
                emit_xt_load(st)

            def h6():
                if st + 1 < NST:
                    front_C(st + 1)

            def h8():
                if st + 1 < NST:
                    front_D(st + 1)
            return {1: h1, 2: h2, 3: h3, 4: h4, 5: h5, 6: h6, 8: h8}

        for st in range(NST):
            pp = st % 2
            emit_experts(st, pp, mk_hooks(st))
            emit_rms_wo_out(st, pp)
        emit_gpre(NST - 1, (NST - 1) % 2)
        emit_tail2(NST - 1, (NST - 1) % 2)

        P.finalize()
        for k in P.dma_semkeys:
            sems[k] = stack.enter_context(nc.semaphore("sd_" + k))
        block = stack.enter_context(nc.Block())

        @block.sync
        def _(sync):
            P.emit("sync", sync, sems)
            sync.wait_ge(sems["outhbm"], P.totals["outhbm"])
            if debug_taps and "taphbm" in sems:
                sync.wait_ge(sems["taphbm"], P.totals["taphbm"])

        @block.gpsimd
        def _(gpsimd):
            P.emit("gpsimd", gpsimd, sems)

        @block.vector
        def _(vector):
            P.emit("vector", vector, sems)

        @block.scalar
        def _(scalar):
            P.emit("scalar", scalar, sems)

        @block.tensor
        def _(tensor):
            P.emit("tensor", tensor, sems)

    return nc


_NC_CACHE = {}


def _get_nc():
    if "nc" not in _NC_CACHE:
        _NC_CACHE["nc"] = build_nc()
    return _NC_CACHE["nc"]


def kernel(L0_attn_x, L0_attn_Wr, L0_attn_Wk, L0_attn_Wv, L0_attn_Wo,
           L0_attn_decay, L0_attn_ln_w, L0_ffn_gate_w, L0_ffn_expert_w):
    x = np.ascontiguousarray(np.asarray(L0_attn_x, dtype=np.float32))
    Wr = np.ascontiguousarray(np.asarray(L0_attn_Wr, dtype=np.float32))
    Wk = np.ascontiguousarray(np.asarray(L0_attn_Wk, dtype=np.float32))
    Wv = np.ascontiguousarray(np.asarray(L0_attn_Wv, dtype=np.float32))
    Wo = np.ascontiguousarray(np.asarray(L0_attn_Wo, dtype=np.float32))
    decay = np.asarray(L0_attn_decay, dtype=np.float32)
    ln_w = np.asarray(L0_attn_ln_w, dtype=np.float32)
    gate_w = np.asarray(L0_ffn_gate_w, dtype=np.float32)
    expert_w = np.asarray(L0_ffn_expert_w, dtype=np.float32)

    ident = np.eye(128, dtype=np.float32).astype(ml_dtypes.bfloat16)
    io8 = np.broadcast_to(np.arange(8, dtype=np.float32), (128, 8)).copy()
    io8r = (8.0 - io8).astype(np.float32)
    dec8 = np.ascontiguousarray(decay.reshape(8, 128).T)
    lnw8 = np.ascontiguousarray(ln_w.reshape(8, 128).T)
    gwt = np.ascontiguousarray(gate_w.T)
    ewt = np.ascontiguousarray(expert_w.transpose(0, 2, 1))
    ar = np.arange(HALF, dtype=np.float32)
    io_lo = np.broadcast_to(ar, (128, HALF)).copy()
    io_hi = np.broadcast_to(ar + HALF, (128, HALF)).copy()
    zeros_pre = np.zeros((HALF, D), dtype=np.float32)

    in_maps = []
    for c in range(8):
        b, half = c // 2, c % 2
        in_maps.append({
            "xmain": np.ascontiguousarray(x[b, half * HALF:(half + 1) * HALF]),
            "xpre": (np.ascontiguousarray(x[b, 0:HALF]) if half else zeros_pre),
            "wk": Wk, "wv": Wv, "wr": Wr, "wo": Wo,
            "ewt": ewt, "gwt": gwt,
            "dec8": dec8, "lnw8": lnw8,
            "iomain": io_hi if half else io_lo,
            "iopre": io_lo,
            "ident": ident, "io8": io8, "io8r": io8r,
        })

    nc = _get_nc()
    res = run_bass_kernel_spmd(nc, in_maps, core_ids=list(range(8)))
    out = np.empty((B, T, D), dtype=np.float32)
    for c in range(8):
        b, half = c // 2, c % 2
        out[b, half * HALF:(half + 1) * HALF] = res.results[c]["out"]
    return out


# version 36
# speedup vs baseline: 1.0124x; 1.0079x over previous
import numpy as np
import ml_dtypes
import concourse.bass as bass
import concourse.mybir as mybir
from concourse.bass_utils import run_bass_kernel_spmd
from contextlib import ExitStack

F32 = mybir.dt.float32
BF16 = mybir.dt.bfloat16
AF = mybir.ActivationFunctionType
ALU = mybir.AluOpType
AX = mybir.AxisListType

B, T, D = 4, 4096, 1024
HALF = T // 2
ST = 256
NST = HALF // ST
NSUB = ST // 128
NCH = D // 128
CLIP_INV = 1.0e10
LN10E = 23.025850929940457

ENGINES = ("sync", "gpsimd", "vector", "scalar", "tensor")


class Op:
    __slots__ = ("engine", "fn", "deps", "kind", "ndma", "awaited", "cnt", "idx",
                 "semkey")

    def __init__(self, engine, fn, deps, kind, ndma, idx):
        self.engine = engine
        self.fn = fn
        self.deps = deps
        self.kind = kind
        self.ndma = ndma
        self.awaited = False
        self.cnt = 0
        self.idx = idx


class Prog:
    def __init__(self):
        self.ops = []
        self.writer = {}
        self.readers = {}

    def op(self, engine, fn, reads=(), writes=(), kind="comp", ndma=1):
        if kind == "dma":
            assert writes, "dma ops must name their destination buffer"
        deps = []
        for b in reads:
            w = self.writer.get(b)
            if w is not None:
                deps.append(w)
        for b in writes:
            deps.extend(self.readers.get(b, ()))
            w = self.writer.get(b)
            if w is not None:
                deps.append(w)
        seen = set()
        ud = []
        for d in deps:
            if id(d) not in seen:
                seen.add(id(d))
                ud.append(d)
        h = Op(engine, fn, ud, kind, ndma, len(self.ops))
        h.semkey = writes[0] if kind == "dma" else engine
        self.ops.append(h)
        for b in reads:
            self.readers.setdefault(b, []).append(h)
        for b in writes:
            self.writer[b] = h
            self.readers[b] = []
        return h

    def finalize(self):
        for h in self.ops:
            for d in h.deps:
                d.awaited = True
        cnt = {e: 0 for e in ENGINES}
        for h in self.ops:
            if h.kind == "dma":
                cnt.setdefault(h.semkey, 0)
                cnt[h.semkey] += 16 * h.ndma
            elif h.awaited:
                cnt[h.engine] += 1
            h.cnt = cnt[h.semkey]
        self.totals = cnt
        self.dma_semkeys = sorted(k for k in cnt if k not in ENGINES)

    def emit(self, eng_name, eng, sems):
        last = {}
        for h in self.ops:
            if h.engine != eng_name:
                continue
            for d in h.deps:
                if (h.engine == "tensor" and d.engine == "tensor"
                        and d.kind != "dma"):
                    continue
                if not d.awaited or d.cnt <= last.get(d.semkey, 0):
                    continue
                eng.wait_ge(sems[d.semkey], d.cnt)
                last[d.semkey] = d.cnt
            insts = h.fn(eng)
            if h.kind == "dma":
                for i in insts:
                    i.then_inc(sems[h.semkey], 16)
            elif h.awaited:
                insts[-1].then_inc(sems[h.engine], 1)


def build_nc(debug_taps=False):
    nc = bass.Bass("TRN2", target_bir_lowering=False)

    xmain_h = nc.dram_tensor("xmain", [HALF, D], F32, kind="ExternalInput")
    xpre_h = nc.dram_tensor("xpre", [HALF, D], F32, kind="ExternalInput")
    w_h = {m: nc.dram_tensor(m, [D, D], F32, kind="ExternalInput")
           for m in ("wk", "wv", "wr", "wo")}
    ewt_h = nc.dram_tensor("ewt", [8, D, D], F32, kind="ExternalInput")
    gwt_h = nc.dram_tensor("gwt", [D, 8], F32, kind="ExternalInput")
    dec_h = nc.dram_tensor("dec8", [128, 8], F32, kind="ExternalInput")
    lnw_h = nc.dram_tensor("lnw8", [128, 8], F32, kind="ExternalInput")
    iom_h = nc.dram_tensor("iomain", [128, HALF], F32, kind="ExternalInput")
    iop_h = nc.dram_tensor("iopre", [128, HALF], F32, kind="ExternalInput")
    id_h = nc.dram_tensor("ident", [128, 128], BF16, kind="ExternalInput")
    io8_h = nc.dram_tensor("io8", [128, 8], F32, kind="ExternalInput")
    io8r_h = nc.dram_tensor("io8r", [128, 8], F32, kind="ExternalInput")
    out_h = nc.dram_tensor("out", [HALF, D], F32, kind="ExternalOutput")
    escr_h = nc.dram_tensor("escr", [8, 128, NCH * D], BF16, kind="Internal")

    P = Prog()
    tap_handles = {}

    with ExitStack() as stack:
        def sb(name, shape, dt=F32):
            return stack.enter_context(nc.sbuf_tensor(name, shape, dt))

        wT = {m: sb(m + "T", [128, NCH * D], BF16) for m in ("wk", "wv", "wr", "wo")}
        ebuf = [sb(f"ebuf{i}", [128, NCH * D], BF16) for i in range(2)]
        gws = sb("gws", [128, NCH * 8], BF16)
        gwsf = sb("gwsf", [128, NCH * 8], F32)
        xb = [sb(f"xb{i}", [128, NSUB * D]) for i in range(2)]
        iot = [sb(f"iot{i}", [128, ST]) for i in range(2)]
        xn_tm = [sb(f"xn_tm{i}", [128, NSUB * D], BF16) for i in range(2)]
        xn_fm = [sb(f"xn_fm{i}", [128, NCH * ST], BF16) for i in range(2)]
        ybf = sb("ybf", [128, NCH * ST], BF16)
        y2g = sb("y2g", [128, NCH * ST], BF16)
        moe = [sb(f"moe{i}", [128, NSUB * D], BF16) for i in range(2)]
        sA = sb("sA", [128, NCH * ST])
        sB = sb("sB", [128, NCH * ST])
        sC = sb("sC", [128, NCH * ST])
        sD = sb("sD", [128, NCH * ST])
        s_sb = sb("s_sb", [128, NSUB * D])
        stageQ = [sb(f"stageQ{i}", [128, D]) for i in range(2)]
        stageP = [sb(f"stageP{i}", [128, D]) for i in range(2)]
        z256 = sb("z256", [128, ST])
        ident = sb("identb", [128, 128], BF16)
        onesb = sb("onesb", [128, 1], BF16)
        io8 = sb("io8b", [128, 8])
        io8r = sb("io8rb", [128, 8])
        decs = sb("decs", [128, 8])
        sgd = sb("sgd", [128, 8])
        dmx = sb("dmx", [128, 8])
        Lc = sb("Lc", [128, 8])
        nLc = sb("nLc", [128, 8])
        lnw = sb("lnwb", [128, 8])
        carry = sb("carryb", [128, NCH])
        tcols = sb("tcols", [128, NST * NCH])
        tc2 = sb("tc2", [128, 4 * NCH])
        scb = sb("scb", [128, 2 * 8])
        ohb = [sb(f"ohb{i}", [128, 2 * 8]) for i in range(2)]
        t8a = sb("t8a", [128, 8])
        t8b = sb("t8b", [128, 8])
        t8c = sb("t8c", [128, 8])
        t8d = sb("t8d", [128, 8])
        gmx = sb("gmx", [128, 2])
        gmx2 = sb("gmx2", [128, 2])
        m3g = sb("m3g", [128, 16])
        mx3 = sb("mx3", [128, 2])
        idxc = sb("idxc", [128, 2])
        mcol = sb("mcol", [128, 2])
        sq2 = sb("sq2", [128, 2])
        t1c = sb("t1c", [128, 2])
        s2c = sb("s2c", [128, 2])
        nvar = sb("nvar", [128, 2])
        lnc = sb("lncol", [128, 2])
        acol = sb("acol", [128, 2])
        acol2 = sb("acol2", [128, 2])
        nbc = sb("nbc", [128, 2])
        nbc3 = sb("nbc3", [128, 2])
        nm = sb("nm", [128, 2])
        rsv = sb("rsv", [128, 2])
        rsv2 = sb("rsv2", [128, 2])
        rsl = sb("rsl", [128, 2])
        cst = sb("cstb", [128, 4])
        qa = sb("qa", [128, 1])
        qs = sb("qs", [128, 1])
        qsh = sb("qsh", [128, 1])
        qnsh = sb("qnsh", [128, 1])
        qwb = sb("qwb", [128, D], BF16)

        pb = [stack.enter_context(nc.psum_tensor(f"pb{i}", [128, 512], F32))
              for i in range(8)]
        sems = {e: stack.enter_context(nc.semaphore("sem_" + e)) for e in ENGINES}
        block = None

        op = P.op

        def tap(name, ap, readkeys):
            if not debug_taps:
                return
            th = nc.dram_tensor("tap_" + name, list(ap.shape), ap.dtype,
                                kind="ExternalOutput")
            tap_handles[name] = th
            op("sync", lambda e, th=th, ap=ap: [e.dma_start(th[:], ap)],
               reads=list(readkeys), writes=["taphbm"], kind="dma")

        op("sync", lambda e: [e.dma_start(ident[:], id_h[:])], writes=["ident"], kind="dma")
        op("sync", lambda e: [e.dma_start(io8[:], io8_h[:])], writes=["io8"], kind="dma")
        op("sync", lambda e: [e.dma_start(io8r[:], io8r_h[:])], writes=["io8r"], kind="dma")
        op("sync", lambda e: [e.dma_start(decs[:], dec_h[:])], writes=["decs"], kind="dma")
        op("sync", lambda e: [e.dma_start(lnw[:], lnw_h[:])], writes=["lnw"], kind="dma")
        op("sync", lambda e: [e.dma_start(gwsf[:, kc * 8:(kc + 1) * 8],
                                          gwt_h[kc * 128:(kc + 1) * 128, :])
                              for kc in range(NCH)],
           writes=["gwsf"], kind="dma", ndma=NCH)
        op("vector", lambda e: [e.memset(z256[:], 0.0)], writes=["z256"])
        op("vector", lambda e: [e.memset(onesb[:], 1.0)], writes=["onesb"])
        op("vector", lambda e: [e.memset(carry[:], 0.0)], writes=["carry"])
        op("vector", lambda e: [e.memset(cst[:, 0:1], 1e-5)], writes=["cst"])
        op("vector", lambda e: [e.memset(cst[:, 1:2], 1e-6)], writes=["cst"])
        op("vector", lambda e: [e.memset(cst[:, 2:3], 1.0)], writes=["cst"])
        op("vector", lambda e: [e.memset(cst[:, 3:4], -8.0)], writes=["cst"])
        op("vector", lambda e: [e.tensor_copy(gws[:], gwsf[:])], reads=["gwsf"], writes=["gws"])
        op("scalar", lambda e: [e.activation(sgd[:], decs[:], AF.Sigmoid)],
           reads=["decs"], writes=["sgd"])
        op("vector", lambda e: [e.tensor_scalar(dmx[:], sgd[:], 1e-7, None, ALU.max)],
           reads=["sgd"], writes=["dmx"])
        op("scalar", lambda e: [e.activation(Lc[:], dmx[:], AF.Ln)],
           reads=["dmx"], writes=["Lc"])
        op("vector", lambda e: [e.tensor_scalar(nLc[:], Lc[:], -1.0, None, ALU.mult)],
           reads=["Lc"], writes=["nLc"])

        import numpy as _np
        _eps = float(_np.float32(1e-5))

        def emit_quant_round(m, c):
            wTm = wT[m]
            sg = stageQ[c % 2]
            skey = f"stageQ{c % 2}"
            op("sync", lambda e, m=m, c=c, sg=sg: [
                e.dma_start(sg[:], w_h[m][c * 128:(c + 1) * 128, :])],
                writes=[skey], kind="dma")
            qta = sA[:, 0:D]
            qtb = sB[:, 0:D]
            qwq = sC[:, 0:D]
            qwbf = qwb[:]
            op("scalar", lambda e, sg=sg, qta=qta: [
                e.activation(qta, sg[:], AF.Abs, accum_out=qa[:, 0:1])],
               reads=[skey], writes=["sA", "qa"])
            op("gpsimd", lambda e: [
                e.tensor_scalar(qs[:], qa[:], 1.0 / D, _eps, ALU.mult, ALU.max),
                e.tensor_scalar(qsh[:], qa[:], 0.5 / D, 0.5 * _eps, ALU.mult, ALU.max)],
               reads=["qa"], writes=["qs", "qsh"])
            op("scalar", lambda e, sg=sg, qtb=qtb: [
                e.activation(qtb, sg[:], AF.Sign)],
               reads=[skey], writes=["sB"])
            op("vector", lambda e, qwq=qwq, qta=qta: [
                e.tensor_scalar(qwq, qta, qsh[:, 0:1], qs[:, 0:1],
                                ALU.is_ge, ALU.mult)],
               reads=["sA", "qsh", "qs"], writes=["sC"])
            op("vector", lambda e, qwbf=qwbf, qwq=qwq, qtb=qtb: [
                e.tensor_tensor(qwbf, qwq, qtb, ALU.mult)],
               reads=["sC", "sB"], writes=["qwb"])
            for q in range(2):
                bank = pb[q]

                def trq(e, q=q, bank=bank, qwbf=qwbf):
                    return [e.matmul(bank[:, i2 * 128:(i2 + 1) * 128],
                                     qwbf[:, (4 * q + i2) * 128:(4 * q + i2 + 1) * 128],
                                     ident[:], start=True, stop=True)
                            for i2 in range(4)]
                op("tensor", trq, reads=["qwb", "ident"], writes=[f"p{q}"])

                def evq(e, q=q, bank=bank, m=m, c=c, wTm=wTm):
                    ins = []
                    for i2 in range(4):
                        kc = 4 * q + i2
                        dst = wTm[:, kc * D + c * 128: kc * D + (c + 1) * 128]
                        src = bank[:, i2 * 128:(i2 + 1) * 128]
                        if m == "wo":
                            ins.append(e.tensor_scalar(dst, src, lnw[:, kc:kc + 1], None, ALU.mult))
                        else:
                            ins.append(e.tensor_copy(dst, src))
                    return ins
                op("vector", evq, reads=[f"p{q}", "lnw"], writes=[m + "T"])

        def quantize_matrix(m):
            for c in range(8):
                emit_quant_round(m, c)

        quantize_matrix("wk")
        quantize_matrix("wv")

        def emit_prep_expert(ex):
            kb = ebuf[ex % 2]
            bkey = f"ebuf{ex % 2}"

            def ldcast(e, ex=ex, kb=kb):
                return [e.dma_start(kb[:, hh * D:(hh + 1) * D],
                                    ewt_h[ex, hh * 128:(hh + 1) * 128, :])
                        for hh in range(8)]
            op("gpsimd", ldcast, writes=[bkey], kind="dma", ndma=8)
            op("gpsimd", lambda e, ex=ex, kb=kb: [e.dma_start(escr_h[ex], kb[:])],
               reads=[bkey], writes=[f"escr{ex}"], kind="dma")

        MAGICBIG = None

        def emit_ln_load(st, pp, x_hbm, prefix):
            xs = xb[pp]
            io_hbm = iop_h if prefix else iom_h

            def ldx(e, st=st, xs=xs, x_hbm=x_hbm):
                return [e.dma_start(xs[:, j * D:(j + 1) * D],
                                    x_hbm[st * ST + j * 128: st * ST + (j + 1) * 128, :])
                        for j in range(NSUB)]
            op("sync", ldx, writes=[f"x{pp}"], kind="dma", ndma=NSUB)
            op("sync", lambda e, st=st: [e.dma_start(iot[pp][:], io_hbm[:, st * ST:(st + 1) * ST])],
               writes=[f"iot{pp}"], kind="dma")

        def emit_ln_math(st, pp, x_hbm, prefix):
            xs = xb[pp]
            xnt = xn_tm[pp]
            if prefix:
                op("vector", lambda e, xs=xs: [
                    e.reduce_sum(t1c[:, j:j + 1], xs[:, j * D:(j + 1) * D], axis=AX.X)
                    for j in range(NSUB)],
                    reads=[f"x{pp}"], writes=["t1c"])
            else:
                op("scalar", lambda e, xs=xs: [
                    e.activation(sA[:, j * D:(j + 1) * D], xs[:, j * D:(j + 1) * D],
                                 AF.Identity, accum_out=t1c[:, j:j + 1])
                    for j in range(NSUB)],
                    reads=[f"x{pp}"], writes=["sA", "t1c"])
            op("scalar", lambda e: [
                e.mul(mcol[:, j:j + 1], t1c[:, j:j + 1], 1.0 / D)
                for j in range(NSUB)],
                reads=["t1c"], writes=["mcol"])
            op("scalar", lambda e, xs=xs: [
                e.activation(sA[:, j * D:(j + 1) * D], xs[:, j * D:(j + 1) * D],
                             AF.Square, accum_out=sq2[:, j:j + 1])
                for j in range(NSUB)],
                reads=[f"x{pp}"], writes=["sA", "sq2"])
            op("scalar", lambda e: [
                e.mul(s2c[:, j:j + 1], sq2[:, j:j + 1], 1.0 / D)
                for j in range(NSUB)],
                reads=["sq2"], writes=["s2c"])
            op("vector", lambda e: [
                e.scalar_tensor_tensor(nvar[:, j:j + 1], mcol[:, j:j + 1],
                                       mcol[:, j:j + 1], s2c[:, j:j + 1],
                                       ALU.mult, ALU.subtract)
                for j in range(NSUB)],
                reads=["mcol", "s2c"], writes=["nvar"])
            op("scalar", lambda e: [
                e.activation(lnc[:, j:j + 1], nvar[:, j:j + 1], AF.Ln,
                             bias=cst[:, 0:1], scale=-1.0)
                for j in range(NSUB)],
                reads=["nvar", "cst"], writes=["lncol"])
            op("scalar", lambda e: [
                e.activation(acol[:, j:j + 1], lnc[:, j:j + 1], AF.Exp, scale=-0.5)
                for j in range(NSUB)],
                reads=["lncol"], writes=["acol"])
            if prefix:
                op("vector", lambda e, xs=xs, xnt=xnt: [
                    e.tensor_scalar(xnt[:, j * D:(j + 1) * D], xs[:, j * D:(j + 1) * D],
                                    mcol[:, j:j + 1], acol[:, j:j + 1],
                                    ALU.subtract, ALU.mult)
                    for j in range(NSUB)],
                    reads=[f"x{pp}", "mcol", "acol"], writes=[f"xn_tm{pp}"])
            else:
                op("gpsimd", lambda e: [
                    e.tensor_scalar(nm[:, 0:NSUB], t1c[:, 0:NSUB], -1.0 / D,
                                    None, ALU.mult)],
                   reads=["t1c"], writes=["nm"])
                op("gpsimd", lambda e: [
                    e.tensor_copy(acol2[:, 0:NSUB], acol[:, 0:NSUB])],
                   reads=["acol"], writes=["acol2"])
                op("scalar", lambda e: [
                    e.activation(nbc[:, j:j + 1], acol[:, j:j + 1], AF.Identity,
                                 scale=nm[:, j:j + 1])
                    for j in range(NSUB)],
                   reads=["acol", "nm"], writes=["nbc"])
                op("gpsimd", lambda e: [
                    e.tensor_copy(nbc3[:, 0:NSUB], nbc[:, 0:NSUB])],
                   reads=["nbc"], writes=["nbc3"])
                op("scalar", lambda e, xs=xs, xnt=xnt: [
                    e.activation(xnt[:, j * D:(j + 1) * D], xs[:, j * D:(j + 1) * D],
                                 AF.Identity, bias=nbc3[:, j:j + 1],
                                 scale=acol2[:, j:j + 1])
                    for j in range(NSUB)],
                    reads=[f"x{pp}", "nbc3", "acol2"], writes=[f"xn_tm{pp}"])

        def emit_ln(st, pp, x_hbm, prefix):
            emit_ln_load(st, pp, x_hbm, prefix)
            emit_ln_math(st, pp, x_hbm, prefix)

        def emit_transposes(pp):
            xnt, xnf = xn_tm[pp], xn_fm[pp]
            for g in range(4):
                bank = pb[g % 2]

                def trg(e, g=g, bank=bank, xnt=xnt):
                    ins = []
                    for u2 in range(4):
                        c = 2 * g + u2 // 2
                        j = u2 % 2
                        ins.append(e.matmul(bank[:, u2 * 128:(u2 + 1) * 128],
                                            xnt[:, j * D + c * 128: j * D + (c + 1) * 128],
                                            ident[:], start=True, stop=True))
                    return ins
                op("tensor", trg, reads=[f"xn_tm{pp}", "ident"], writes=[f"p{g % 2}"])
                op("scalar", lambda e, g=g, bank=bank, xnf=xnf: [
                    e.copy(xnf[:, (2 * g) * ST:(2 * g + 2) * ST], bank[:, 0:512])],
                    reads=[f"p{g % 2}"], writes=[f"xn_fm{pp}"])

        def emit_gate(pp, dbg=False):
            xnf = xn_fm[pp]
            ohx = ohb[pp]
            for j in range(NSUB):
                gb = pb[2 + j]
                gbk = f"p{2 + j}"

                def gmm(e, j=j, xnf=xnf, gb=gb):
                    ins = []
                    for kc in range(NCH):
                        ins.append(e.matmul(gb[:, j * 8:(j + 1) * 8],
                                            xnf[:, kc * ST + j * 128: kc * ST + (j + 1) * 128],
                                            gws[:, kc * 8:(kc + 1) * 8],
                                            start=(kc == 0), stop=(kc == NCH - 1)))
                    return ins
                op("tensor", gmm, reads=[f"xn_fm{pp}", "gws"], writes=[gbk])
                op("scalar", lambda e, j=j, gb=gb: [
                    e.copy(scb[:, j * 8:(j + 1) * 8], gb[:, j * 8:(j + 1) * 8])],
                    reads=[gbk], writes=["scb"])
            for j in range(NSUB):
                op("vector", lambda e, j=j: [
                    e.reduce_max(gmx[:, j:j + 1], scb[:, j * 8:(j + 1) * 8], axis=AX.X)],
                    reads=["scb"], writes=["gmx"])
            op("scalar", lambda e: [
                e.mul(gmx2[:], gmx[:], -1.0)],
                reads=["gmx"], writes=["gmx2"])
            for j, (ta, tb) in enumerate(((t8a, t8b), (t8c, t8d))):
                ka, kb = f"t8{j}a", f"t8{j}b"
                op("scalar", lambda e, j=j, ta=ta: [
                    e.activation(ta[:], scb[:, j * 8:(j + 1) * 8], AF.Identity,
                                 bias=gmx2[:, j:j + 1])],
                    reads=["scb", "gmx2"], writes=[ka])
                op("scalar", lambda e, ta=ta, tb=tb: [
                    e.activation(tb[:], ta[:], AF.Sign)],
                    reads=[ka], writes=[kb])
                op("scalar", lambda e, ta=ta, tb=tb: [
                    e.activation(ta[:], tb[:], AF.Abs)],
                    reads=[kb], writes=[ka])
                op("scalar", lambda e, ta=ta, tb=tb: [
                    e.activation(tb[:], ta[:], AF.Identity, bias=cst[:, 2:3], scale=-1.0)],
                    reads=[ka, "cst"], writes=[kb])
                op("vector", lambda e, ta=ta, tb=tb: [
                    e.tensor_tensor(ta[:], tb[:], io8r[:], ALU.mult)],
                    reads=[kb, "io8r"], writes=[ka])
            op("scalar", lambda e: [
                e.copy(m3g[:, 0:8], t8a[:]),
                e.copy(m3g[:, 8:16], t8c[:])],
                reads=["t80a", "t81a"], writes=["m3g"])
            for j in range(NSUB):
                op("vector", lambda e, j=j: [
                    e.reduce_max(mx3[:, j:j + 1], m3g[:, j * 8:(j + 1) * 8], axis=AX.X)],
                    reads=["m3g"], writes=["mx3"])
            op("scalar", lambda e: [
                e.activation(idxc[:, j:j + 1], mx3[:, j:j + 1], AF.Identity,
                             bias=cst[:, 3:4])
                for j in range(NSUB)],
                reads=["mx3", "cst"], writes=["idxc"])
            for j, (ta, tb) in enumerate(((t8a, t8b), (t8c, t8d))):
                ka, kb = f"t8{j}a", f"t8{j}b"
                op("scalar", lambda e, j=j, ta=ta: [
                    e.activation(ta[:], io8[:], AF.Identity, bias=idxc[:, j:j + 1])],
                    reads=["io8", "idxc"], writes=[ka])
                op("scalar", lambda e, ta=ta, tb=tb: [
                    e.activation(tb[:], ta[:], AF.Sign)],
                    reads=[ka], writes=[kb])
                op("scalar", lambda e, ta=ta, tb=tb: [
                    e.activation(ta[:], tb[:], AF.Abs)],
                    reads=[kb], writes=[ka])
                op("scalar", lambda e, j=j, ta=ta, ohx=ohx: [
                    e.activation(ohx[:, j * 8:(j + 1) * 8], ta[:], AF.Identity,
                                 bias=cst[:, 2:3], scale=-1.0)],
                    reads=[ka, "cst"], writes=[f"ohb{pp}"])
            if dbg:
                tap("idxc_g", idxc[:], ["idxc"])

        def emit_proj(pp, m, banks, bankkeys):
            xnf = xn_fm[pp]
            for i in range(4):
                bank = banks[i]

                def pmm(e, i=i, bank=bank, m=m, xnf=xnf):
                    ins = []
                    for ce in (2 * i, 2 * i + 1):
                        for kc in range(NCH):
                            ins.append(e.matmul(
                                bank[:, (ce % 2) * ST:(ce % 2) * ST + ST],
                                wT[m][:, kc * D + ce * 128: kc * D + (ce + 1) * 128],
                                xnf[:, kc * ST:(kc + 1) * ST],
                                start=(kc == 0), stop=(kc == NCH - 1)))
                    return ins
                op("tensor", pmm, reads=[m + "T", f"xn_fm{pp}"], writes=[bankkeys[i]])

        def emit_scan_ue(st, pp):
            iost = iot[pp]
            op("vector", lambda e, iost=iost: [
                e.tensor_scalar(sA[:, c * ST:(c + 1) * ST], iost[:],
                                nLc[:, c:c + 1], 23.025850929940457,
                                ALU.mult, ALU.min)
                for c in range(NCH)],
                reads=[f"iot{pp}", "nLc"], writes=["sA"])
            op("scalar", lambda e: [e.activation(sB[:], sA[:], AF.Exp)],
               reads=["sA"], writes=["sB"])

        def emit_scan_a(st, pp, with_scale, kb0=4):
            iost = iot[pp]
            op("vector", lambda e, kb0=kb0: [
                e.scalar_tensor_tensor(sC[:, 2 * i * ST:(2 * i + 2) * ST],
                                       sB[:, 2 * i * ST:(2 * i + 2) * ST],
                                       CLIP_INV, pb[kb0 + i][:, 0:512],
                                       ALU.min, ALU.mult)
                for i in range(4)],
                reads=["sB"] + [f"p{kb0 + i}" for i in range(4)], writes=["sC"])
            if with_scale:
                op("scalar", lambda e, iost=iost: [
                    e.activation(sA[:, c * ST:(c + 1) * ST], iost[:], AF.Exp,
                                 scale=Lc[:, c:c + 1])
                    for c in range(NCH)],
                    reads=[f"iot{pp}", "Lc"], writes=["sA"])

        def emit_scan_b(st, pp, prefix_st=None):
            if prefix_st is not None:
                op("vector", lambda e, k=prefix_st: [
                    e.scalar_tensor_tensor(sD[:, c * ST:(c + 1) * ST],
                                           pb[c // 2][:, (c % 2) * ST:(c % 2) * ST + ST],
                                           1.0,
                                           sC[:, c * ST:(c + 1) * ST],
                                           ALU.mult, ALU.mult,
                                           accum_out=tcols[:, k * 8 + c:k * 8 + c + 1])
                    for c in range(NCH)],
                    reads=["p0", "p1", "p2", "p3", "sC"], writes=["sD", "tcols"])
                return
            op("vector", lambda e: [
                e.tensor_tensor(sD[:, 2 * i * ST:(2 * i + 2) * ST],
                                pb[i][:, 0:512],
                                sC[:, 2 * i * ST:(2 * i + 2) * ST], ALU.mult)
                for i in range(4)],
                reads=["p0", "p1", "p2", "p3", "sC"], writes=["sD"])
            op("vector", lambda e: [
                e.tensor_tensor_scan(sC[:, c * ST:(c + 1) * ST],
                                     sD[:, c * ST:(c + 1) * ST], z256[:],
                                     carry[:, c:c + 1], ALU.add, ALU.add)
                for c in range(NCH)],
                reads=["sD", "z256", "carry"], writes=["sC"])
            op("scalar", lambda e: [
                e.copy(carry[:, c:c + 1], sC[:, c * ST + ST - 1: c * ST + ST])
                for c in range(NCH)],
                reads=["sC"], writes=["carry"])

        def emit_y(rb0=4):
            op("scalar", lambda e, rb0=rb0: [
                e.activation(sD[:, 2 * i * ST:(2 * i + 2) * ST],
                             pb[rb0 + i][:, 0:512], AF.Sigmoid)
                for i in range(4)],
                reads=[f"p{rb0 + i}" for i in range(4)], writes=["sD"])
            op("vector", lambda e: [e.tensor_tensor(sB[:], sD[:], sA[:], ALU.mult)],
               reads=["sD", "sA"], writes=["sB"])
            op("vector", lambda e: [e.tensor_tensor(ybf[:], sC[:], sB[:], ALU.mult)],
               reads=["sC", "sB"], writes=["ybf"])
            op("scalar", lambda e: [e.activation(y2g[:], ybf[:], AF.Square)],
               reads=["ybf"], writes=["y2g"])

        esched = []
        for _p in range(NST):
            _order = range(7, -1, -1) if _p % 2 == 0 else range(8)
            for _i, _ex in enumerate(_order):
                _buf = (_i + 1) % 2 if _p % 2 == 0 else _i % 2
                esched.append((_ex, _buf, _i >= 2))

        def emit_expert_load(g):
            if g >= len(esched):
                return
            ex, buf, needs = esched[g]
            if not needs:
                return
            op("gpsimd", lambda e, ex=ex, kbuf=ebuf[buf]: [e.dma_start(kbuf[:], escr_h[ex])],
               reads=[f"escr{ex}"], writes=[f"ebuf{buf}"], kind="dma")

        def emit_experts(st, pp, hooks=None):
            xnf = xn_fm[pp]
            for i in range(8):
                if hooks and i in hooks:
                    hooks[i]()
                ex, buf, _ = esched[st * 8 + i]
                kbuf = ebuf[buf]
                bkey = f"ebuf{buf}"
                for j in range(NSUB):
                    for nh in range(2):
                        bank = pb[4 + j * 2 + nh]
                        bk = f"p{4 + j * 2 + nh}"

                        def emm(e, j=j, nh=nh, bank=bank, kbuf=kbuf, xnf=xnf):
                            ins = []
                            for kc in range(NCH):
                                ins.append(e.matmul(
                                    bank[:, 0:512],
                                    xnf[:, kc * ST + j * 128: kc * ST + (j + 1) * 128],
                                    kbuf[:, kc * D + nh * 512: kc * D + nh * 512 + 512],
                                    start=(kc == 0), stop=(kc == NCH - 1)))
                            return ins
                        op("tensor", emm, reads=[f"xn_fm{pp}", bkey], writes=[bk])
                        dst = moe[(i + 1) % 2][:, j * D + nh * 512: j * D + nh * 512 + 512]
                        ohcol = ohb[pp][:, j * 8 + ex: j * 8 + ex + 1]
                        if i == 0:
                            op("vector", lambda e, dst=dst, bank=bank, ohcol=ohcol: [
                                e.tensor_scalar(dst, bank[:, 0:512], ohcol, None, ALU.mult)],
                                reads=[bk, f"ohb{pp}"], writes=["moe1"])
                        else:
                            src2 = moe[i % 2][:, j * D + nh * 512: j * D + nh * 512 + 512]
                            op("vector", lambda e, dst=dst, bank=bank, ohcol=ohcol, src2=src2: [
                                e.scalar_tensor_tensor(dst, bank[:, 0:512], ohcol, src2,
                                                       ALU.mult, ALU.add)],
                                reads=[bk, f"ohb{pp}", f"moe{i % 2}"], writes=[f"moe{(i + 1) % 2}"])
                emit_expert_load(st * 8 + i + 2)
            if hooks and 8 in hooks:
                hooks[8]()

        def emit_rms_wo_out(st, pp):
            xs = xb[pp]
            for j in range(NSUB):
                def s2mm(e, j=j):
                    ins = []
                    for kc in range(NCH):
                        ins.append(e.matmul(pb[5][:, 16 + j:17 + j],
                                            y2g[:, kc * ST + j * 128: kc * ST + (j + 1) * 128],
                                            onesb[:],
                                            start=(kc == 0), stop=(kc == NCH - 1)))
                    return ins
                op("tensor", s2mm, reads=["y2g", "onesb"], writes=["p5"])
                op("scalar", lambda e, j=j: [
                    e.activation(rsl[:, j:j + 1], pb[5][:, 16 + j:17 + j], AF.Ln,
                                 bias=cst[:, 1:2], scale=1.0 / D)],
                    reads=["p5", "cst"], writes=["rsl"])
                op("scalar", lambda e, j=j: [
                    e.activation(rsv[:, j:j + 1], rsl[:, j:j + 1], AF.Exp, scale=-0.5)],
                    reads=["rsl"], writes=["rsv"])
            op("vector", lambda e: [
                e.tensor_scalar(rsv2[:, 0:NSUB], rsv[:, 0:NSUB], 1.0, None, ALU.mult)],
               reads=["rsv"], writes=["rsv2"])
            wob = {0: (4, 6), 1: (7, 5)}
            for j in range(NSUB):
                for nh in range(2):
                    bank = pb[wob[j][nh]]
                    bk = f"p{wob[j][nh]}"

                    def womm(e, j=j, nh=nh, bank=bank):
                        ins = []
                        for kc in range(NCH):
                            ins.append(e.matmul(
                                bank[:, 0:512],
                                ybf[:, kc * ST + j * 128: kc * ST + (j + 1) * 128],
                                wT["wo"][:, kc * D + nh * 512: kc * D + nh * 512 + 512],
                                start=(kc == 0), stop=(kc == NCH - 1)))
                        return ins
                    op("tensor", womm, reads=["ybf", "woT"], writes=[bk])
                    op("scalar", lambda e, j=j, nh=nh, bank=bank: [
                        e.activation(s_sb[:, j * D + nh * 512: j * D + nh * 512 + 512],
                                     bank[:, 0:512], AF.Identity,
                                     scale=rsv2[:, j:j + 1])],
                        reads=[bk, "rsv2"], writes=["s_sb"])
        def emit_gpre(st, pp):
            for j in range(NSUB):
                op("vector", lambda e, j=j, pp=pp: [
                    e.tensor_tensor(stageP[j][:], moe[0][:, j * D:(j + 1) * D],
                                    xn_tm[pp][:, j * D:(j + 1) * D], ALU.add)],
                    reads=["moe0", f"xn_tm{pp}"], writes=[f"stageP{j}"])

        def emit_xt_load(st):
            def ldxt(e, st=st):
                return [e.dma_start(stageQ[j][:],
                                    xmain_h[st * ST + j * 128: st * ST + (j + 1) * 128, :])
                        for j in range(NSUB)]
            op("sync", ldxt, writes=["stageQ0", "stageQ1"], kind="dma", ndma=NSUB)

        def emit_tail2a(st, pp):
            op("scalar", lambda e: [
                e.activation(sD[:, 0:D], stageP[0][:], AF.Sigmoid),
                e.activation(sD[:, D:2 * D], stageP[1][:], AF.Sigmoid)],
               reads=["stageP0", "stageP1"], writes=["sD"])
            for j in range(NSUB):
                op("vector", lambda e, j=j: [
                    e.tensor_tensor(sB[:, j * D:(j + 1) * D], stageP[j][:],
                                    sD[:, j * D:(j + 1) * D], ALU.mult)],
                    reads=[f"stageP{j}", "sD"], writes=["sB"])

        def emit_tail2b(st, pp):
            for j in range(NSUB):
                op("vector", lambda e, j=j: [
                    e.tensor_tensor(sD[:, j * D:(j + 1) * D], s_sb[:, j * D:(j + 1) * D],
                                    sB[:, j * D:(j + 1) * D], ALU.mult)],
                    reads=["s_sb", "sB"], writes=["sD"])
                op("vector", lambda e, j=j: [
                    e.tensor_tensor(s_sb[:, j * D:(j + 1) * D], stageQ[j][:],
                                    sD[:, j * D:(j + 1) * D], ALU.add)],
                    reads=[f"stageQ{j}", "sD"], writes=["s_sb"])
                op("sync", lambda e, st=st, j=j: [
                    e.dma_start(out_h[st * ST + j * 128: st * ST + (j + 1) * 128, :],
                                s_sb[:, j * D:(j + 1) * D])],
                    reads=["s_sb"], writes=["outhbm"], kind="dma")

        def emit_tail2(st, pp):
            emit_tail2a(st, pp)
            emit_tail2b(st, pp)

        quantize_matrix("wk")
        quantize_matrix("wv")
        emit_ln(0, 0, xpre_h, prefix=True)
        emit_scan_ue(0, 0)
        for st in range(NST):
            pp = st % 2
            emit_transposes(pp)
            emit_proj(pp, "wk", [pb[4 + i] for i in range(4)], [f"p{4 + i}" for i in range(4)])
            emit_proj(pp, "wv", [pb[i] for i in range(4)], [f"p{i}" for i in range(4)])
            if st + 1 < NST:
                emit_ln(st + 1, (st + 1) % 2, xpre_h, prefix=True)
            emit_scan_a(st, pp, with_scale=False, kb0=4)
            if st + 1 < NST:
                emit_scan_ue(st + 1, (st + 1) % 2)
            emit_scan_b(st, pp, prefix_st=st)
            emit_quant_round("wr" if st < 4 else "wo", 2 * (st % 4))
            emit_quant_round("wr" if st < 4 else "wo", 2 * (st % 4) + 1)
            emit_prep_expert(st)
        op("vector", lambda e: [
            e.tensor_tensor(tc2[:], tcols[:, 0:32], tcols[:, 32:64], ALU.add)],
            reads=["tcols"], writes=["tc2"])
        op("vector", lambda e: [
            e.tensor_tensor(tcols[:, 0:16], tc2[:, 0:16], tc2[:, 16:32], ALU.add)],
            reads=["tc2"], writes=["tcols"])
        op("vector", lambda e: [
            e.tensor_tensor(carry[:], tcols[:, 0:8], tcols[:, 8:16], ALU.add)],
            reads=["tcols"], writes=["carry"])
        if debug_taps:
            for m in ("wk", "wv", "wr", "wo"):
                tap(m + "T", wT[m][:], [m + "T"])

        def front_A(st):
            pp = st % 2
            emit_transposes(pp)
            emit_gate(pp)

        def front_B(st):
            pp = st % 2
            emit_proj(pp, "wk", [pb[i] for i in range(4)], [f"p{i}" for i in range(4)])
            emit_scan_a(st, pp, with_scale=True, kb0=0)

        def front_C(st):
            pp = st % 2
            emit_proj(pp, "wv", [pb[i] for i in range(4)], [f"p{i}" for i in range(4)])
            emit_scan_b(st, pp)

        def front_D(st):
            pp = st % 2
            emit_proj(pp, "wr", [pb[i] for i in range(4)], [f"p{i}" for i in range(4)])

        emit_ln(0, 0, xmain_h, prefix=False)
        emit_scan_ue(0, 0)
        front_A(0)
        front_B(0)
        front_C(0)
        front_D(0)
        def mk_hooks(st):
            pp0 = (st - 1) % 2

            def h1():
                emit_y(rb0=0)
                if st > 0:
                    emit_gpre(st - 1, pp0)
                if st + 1 < NST:
                    emit_ln_load(st + 1, (st + 1) % 2, xmain_h, prefix=False)

            def h2():
                if st + 1 < NST:
                    emit_ln_math(st + 1, (st + 1) % 2, xmain_h, prefix=False)
                    emit_scan_ue(st + 1, (st + 1) % 2)

            def h3():
                if st + 1 < NST:
                    front_A(st + 1)

            def h4():
                if st > 0:
                    emit_tail2a(st - 1, pp0)

            def h5():
                if st + 1 < NST:
                    front_B(st + 1)
                if st > 0:
                    emit_tail2b(st - 1, pp0)
                emit_xt_load(st)

            def h6():
                if st + 1 < NST:
                    front_C(st + 1)

            def h8():
                if st + 1 < NST:
                    front_D(st + 1)
            return {1: h1, 2: h2, 3: h3, 4: h4, 5: h5, 6: h6, 8: h8}

        for st in range(NST):
            pp = st % 2
            emit_experts(st, pp, mk_hooks(st))
            emit_rms_wo_out(st, pp)
        emit_gpre(NST - 1, (NST - 1) % 2)
        emit_tail2(NST - 1, (NST - 1) % 2)

        P.finalize()
        for k in P.dma_semkeys:
            sems[k] = stack.enter_context(nc.semaphore("sd_" + k))
        block = stack.enter_context(nc.Block())

        @block.sync
        def _(sync):
            P.emit("sync", sync, sems)
            sync.wait_ge(sems["outhbm"], P.totals["outhbm"])
            if debug_taps and "taphbm" in sems:
                sync.wait_ge(sems["taphbm"], P.totals["taphbm"])

        @block.gpsimd
        def _(gpsimd):
            P.emit("gpsimd", gpsimd, sems)

        @block.vector
        def _(vector):
            P.emit("vector", vector, sems)

        @block.scalar
        def _(scalar):
            P.emit("scalar", scalar, sems)

        @block.tensor
        def _(tensor):
            P.emit("tensor", tensor, sems)

    return nc


_NC_CACHE = {}


def _get_nc():
    if "nc" not in _NC_CACHE:
        _NC_CACHE["nc"] = build_nc()
    return _NC_CACHE["nc"]


def kernel(L0_attn_x, L0_attn_Wr, L0_attn_Wk, L0_attn_Wv, L0_attn_Wo,
           L0_attn_decay, L0_attn_ln_w, L0_ffn_gate_w, L0_ffn_expert_w):
    x = np.ascontiguousarray(np.asarray(L0_attn_x, dtype=np.float32))
    Wr = np.ascontiguousarray(np.asarray(L0_attn_Wr, dtype=np.float32))
    Wk = np.ascontiguousarray(np.asarray(L0_attn_Wk, dtype=np.float32))
    Wv = np.ascontiguousarray(np.asarray(L0_attn_Wv, dtype=np.float32))
    Wo = np.ascontiguousarray(np.asarray(L0_attn_Wo, dtype=np.float32))
    decay = np.asarray(L0_attn_decay, dtype=np.float32)
    ln_w = np.asarray(L0_attn_ln_w, dtype=np.float32)
    gate_w = np.asarray(L0_ffn_gate_w, dtype=np.float32)
    expert_w = np.asarray(L0_ffn_expert_w, dtype=np.float32)

    ident = np.eye(128, dtype=np.float32).astype(ml_dtypes.bfloat16)
    io8 = np.broadcast_to(np.arange(8, dtype=np.float32), (128, 8)).copy()
    io8r = (8.0 - io8).astype(np.float32)
    dec8 = np.ascontiguousarray(decay.reshape(8, 128).T)
    lnw8 = np.ascontiguousarray(ln_w.reshape(8, 128).T)
    gwt = np.ascontiguousarray(gate_w.T)
    ewt = np.ascontiguousarray(expert_w.transpose(0, 2, 1))
    ar = np.arange(HALF, dtype=np.float32)
    io_lo = np.broadcast_to(ar, (128, HALF)).copy()
    io_hi = np.broadcast_to(ar + HALF, (128, HALF)).copy()
    zeros_pre = np.zeros((HALF, D), dtype=np.float32)

    in_maps = []
    for c in range(8):
        b, half = c // 2, c % 2
        in_maps.append({
            "xmain": np.ascontiguousarray(x[b, half * HALF:(half + 1) * HALF]),
            "xpre": (np.ascontiguousarray(x[b, 0:HALF]) if half else zeros_pre),
            "wk": Wk, "wv": Wv, "wr": Wr, "wo": Wo,
            "ewt": ewt, "gwt": gwt,
            "dec8": dec8, "lnw8": lnw8,
            "iomain": io_hi if half else io_lo,
            "iopre": io_lo,
            "ident": ident, "io8": io8, "io8r": io8r,
        })

    nc = _get_nc()
    res = run_bass_kernel_spmd(nc, in_maps, core_ids=list(range(8)))
    out = np.empty((B, T, D), dtype=np.float32)
    for c in range(8):
        b, half = c // 2, c % 2
        out[b, half * HALF:(half + 1) * HALF] = res.results[c]["out"]
    return out


# version 37
# speedup vs baseline: 1.0326x; 1.0199x over previous
import numpy as np
import ml_dtypes
import concourse.bass as bass
import concourse.mybir as mybir
from concourse.bass_utils import run_bass_kernel_spmd
from contextlib import ExitStack

F32 = mybir.dt.float32
BF16 = mybir.dt.bfloat16
AF = mybir.ActivationFunctionType
ALU = mybir.AluOpType
AX = mybir.AxisListType

B, T, D = 4, 4096, 1024
HALF = T // 2
ST = 256
NST = HALF // ST
NSUB = ST // 128
NCH = D // 128
CLIP_INV = 1.0e10
LN10E = 23.025850929940457

ENGINES = ("sync", "gpsimd", "vector", "scalar", "tensor")


class Op:
    __slots__ = ("engine", "fn", "deps", "kind", "ndma", "awaited", "cnt", "idx",
                 "semkey")

    def __init__(self, engine, fn, deps, kind, ndma, idx):
        self.engine = engine
        self.fn = fn
        self.deps = deps
        self.kind = kind
        self.ndma = ndma
        self.awaited = False
        self.cnt = 0
        self.idx = idx


class Prog:
    def __init__(self):
        self.ops = []
        self.writer = {}
        self.readers = {}

    def op(self, engine, fn, reads=(), writes=(), kind="comp", ndma=1):
        if kind == "dma":
            assert writes, "dma ops must name their destination buffer"
        deps = []
        for b in reads:
            w = self.writer.get(b)
            if w is not None:
                deps.append(w)
        for b in writes:
            deps.extend(self.readers.get(b, ()))
            w = self.writer.get(b)
            if w is not None:
                deps.append(w)
        seen = set()
        ud = []
        for d in deps:
            if id(d) not in seen:
                seen.add(id(d))
                ud.append(d)
        h = Op(engine, fn, ud, kind, ndma, len(self.ops))
        h.semkey = writes[0] if kind == "dma" else engine
        self.ops.append(h)
        for b in reads:
            self.readers.setdefault(b, []).append(h)
        for b in writes:
            self.writer[b] = h
            self.readers[b] = []
        return h

    def finalize(self):
        for h in self.ops:
            for d in h.deps:
                d.awaited = True
        cnt = {e: 0 for e in ENGINES}
        for h in self.ops:
            if h.kind == "dma":
                cnt.setdefault(h.semkey, 0)
                cnt[h.semkey] += 16 * h.ndma
            elif h.awaited:
                cnt[h.engine] += 1
            h.cnt = cnt[h.semkey]
        self.totals = cnt
        self.dma_semkeys = sorted(k for k in cnt if k not in ENGINES)

    def emit(self, eng_name, eng, sems):
        last = {}
        for h in self.ops:
            if h.engine != eng_name:
                continue
            for d in h.deps:
                if (h.engine == "tensor" and d.engine == "tensor"
                        and d.kind != "dma"):
                    continue
                if not d.awaited or d.cnt <= last.get(d.semkey, 0):
                    continue
                eng.wait_ge(sems[d.semkey], d.cnt)
                last[d.semkey] = d.cnt
            insts = h.fn(eng)
            if h.kind == "dma":
                for i in insts:
                    i.then_inc(sems[h.semkey], 16)
            elif h.awaited:
                insts[-1].then_inc(sems[h.engine], 1)


def build_nc(debug_taps=False):
    nc = bass.Bass("TRN2", target_bir_lowering=False)

    xmain_h = nc.dram_tensor("xmain", [HALF, D], F32, kind="ExternalInput")
    xpre_h = nc.dram_tensor("xpre", [HALF, D], F32, kind="ExternalInput")
    w_h = {m: nc.dram_tensor(m, [D, D], F32, kind="ExternalInput")
           for m in ("wk", "wv", "wr", "wo")}
    ewt_h = nc.dram_tensor("ewt", [8, D, D], F32, kind="ExternalInput")
    gwt_h = nc.dram_tensor("gwt", [D, 8], F32, kind="ExternalInput")
    dec_h = nc.dram_tensor("dec8", [128, 8], F32, kind="ExternalInput")
    lnw_h = nc.dram_tensor("lnw8", [128, 8], F32, kind="ExternalInput")
    iom_h = nc.dram_tensor("iomain", [128, HALF], F32, kind="ExternalInput")
    iop_h = nc.dram_tensor("iopre", [128, HALF], F32, kind="ExternalInput")
    id_h = nc.dram_tensor("ident", [128, 128], BF16, kind="ExternalInput")
    io8_h = nc.dram_tensor("io8", [128, 8], F32, kind="ExternalInput")
    io8r_h = nc.dram_tensor("io8r", [128, 8], F32, kind="ExternalInput")
    out_h = nc.dram_tensor("out", [HALF, D], F32, kind="ExternalOutput")
    escr_h = nc.dram_tensor("escr", [8, 128, NCH * D], BF16, kind="Internal")

    P = Prog()
    tap_handles = {}

    with ExitStack() as stack:
        def sb(name, shape, dt=F32):
            return stack.enter_context(nc.sbuf_tensor(name, shape, dt))

        wT = {m: sb(m + "T", [128, NCH * D], BF16) for m in ("wk", "wv", "wr", "wo")}
        ebuf = [sb(f"ebuf{i}", [128, NCH * D], BF16) for i in range(2)]
        gws = sb("gws", [128, NCH * 8], BF16)
        gwsf = sb("gwsf", [128, NCH * 8], F32)
        xb = [sb(f"xb{i}", [128, NSUB * D]) for i in range(2)]
        iot = [sb(f"iot{i}", [128, ST]) for i in range(2)]
        xn_tm = [sb(f"xn_tm{i}", [128, NSUB * D], BF16) for i in range(2)]
        xn_fm = [sb(f"xn_fm{i}", [128, NCH * ST], BF16) for i in range(2)]
        ybf = sb("ybf", [128, NCH * ST], BF16)
        y2g = sb("y2g", [128, NCH * ST], BF16)
        moe = [sb(f"moe{i}", [128, NSUB * D], BF16) for i in range(2)]
        sA = sb("sA", [128, NCH * ST])
        sB = sb("sB", [128, NCH * ST])
        sC = sb("sC", [128, NCH * ST])
        sD = sb("sD", [128, NCH * ST])
        s_sb = sb("s_sb", [128, NSUB * D])
        stageQ = [sb(f"stageQ{i}", [128, D]) for i in range(2)]
        stageP = [sb(f"stageP{i}", [128, D]) for i in range(2)]
        z256 = sb("z256", [128, ST])
        ident = sb("identb", [128, 128], BF16)
        onesb = sb("onesb", [128, 1], BF16)
        io8 = sb("io8b", [128, 8])
        io8r = sb("io8rb", [128, 8])
        decs = sb("decs", [128, 8])
        sgd = sb("sgd", [128, 8])
        dmx = sb("dmx", [128, 8])
        Lc = sb("Lc", [128, 8])
        nLc = sb("nLc", [128, 8])
        lnw = sb("lnwb", [128, 8])
        carry = sb("carryb", [128, NCH])
        tcols = sb("tcols", [128, NST * NCH])
        tc2 = sb("tc2", [128, 4 * NCH])
        scb = sb("scb", [128, 2 * 8])
        ohb = [sb(f"ohb{i}", [128, 2 * 8]) for i in range(2)]
        t8a = sb("t8a", [128, 8])
        t8b = sb("t8b", [128, 8])
        t8c = sb("t8c", [128, 8])
        t8d = sb("t8d", [128, 8])
        gmx = sb("gmx", [128, 2])
        gmx2 = sb("gmx2", [128, 2])
        m3g = sb("m3g", [128, 16])
        mx3 = sb("mx3", [128, 2])
        idxc = sb("idxc", [128, 2])
        mcol = sb("mcol", [128, 2])
        sq2 = sb("sq2", [128, 2])
        t1c = sb("t1c", [128, 2])
        s2c = sb("s2c", [128, 2])
        nvar = sb("nvar", [128, 2])
        lnc = sb("lncol", [128, 2])
        acol = sb("acol", [128, 2])
        acol2 = sb("acol2", [128, 2])
        nbc = sb("nbc", [128, 2])
        rsv = sb("rsv", [128, 2])
        rsv2 = sb("rsv2", [128, 2])
        rsl = sb("rsl", [128, 2])
        cst = sb("cstb", [128, 4])
        qa = sb("qa", [128, 1])
        qs = sb("qs", [128, 1])
        qsh = sb("qsh", [128, 1])
        qnsh = sb("qnsh", [128, 1])
        qwb = sb("qwb", [128, D], BF16)

        pb = [stack.enter_context(nc.psum_tensor(f"pb{i}", [128, 512], F32))
              for i in range(8)]
        sems = {e: stack.enter_context(nc.semaphore("sem_" + e)) for e in ENGINES}
        block = None

        op = P.op

        def tap(name, ap, readkeys):
            if not debug_taps:
                return
            th = nc.dram_tensor("tap_" + name, list(ap.shape), ap.dtype,
                                kind="ExternalOutput")
            tap_handles[name] = th
            op("sync", lambda e, th=th, ap=ap: [e.dma_start(th[:], ap)],
               reads=list(readkeys), writes=["taphbm"], kind="dma")

        op("sync", lambda e: [e.dma_start(ident[:], id_h[:])], writes=["ident"], kind="dma")
        op("sync", lambda e: [e.dma_start(io8[:], io8_h[:])], writes=["io8"], kind="dma")
        op("sync", lambda e: [e.dma_start(io8r[:], io8r_h[:])], writes=["io8r"], kind="dma")
        op("sync", lambda e: [e.dma_start(decs[:], dec_h[:])], writes=["decs"], kind="dma")
        op("sync", lambda e: [e.dma_start(lnw[:], lnw_h[:])], writes=["lnw"], kind="dma")
        op("sync", lambda e: [e.dma_start(gwsf[:, kc * 8:(kc + 1) * 8],
                                          gwt_h[kc * 128:(kc + 1) * 128, :])
                              for kc in range(NCH)],
           writes=["gwsf"], kind="dma", ndma=NCH)
        op("vector", lambda e: [e.memset(z256[:], 0.0)], writes=["z256"])
        op("vector", lambda e: [e.memset(onesb[:], 1.0)], writes=["onesb"])
        op("vector", lambda e: [e.memset(carry[:], 0.0)], writes=["carry"])
        op("vector", lambda e: [e.memset(cst[:, 0:1], 1e-5)], writes=["cst"])
        op("vector", lambda e: [e.memset(cst[:, 1:2], 1e-6)], writes=["cst"])
        op("vector", lambda e: [e.memset(cst[:, 2:3], 1.0)], writes=["cst"])
        op("vector", lambda e: [e.memset(cst[:, 3:4], -8.0)], writes=["cst"])
        op("vector", lambda e: [e.tensor_copy(gws[:], gwsf[:])], reads=["gwsf"], writes=["gws"])
        op("scalar", lambda e: [e.activation(sgd[:], decs[:], AF.Sigmoid)],
           reads=["decs"], writes=["sgd"])
        op("vector", lambda e: [e.tensor_scalar(dmx[:], sgd[:], 1e-7, None, ALU.max)],
           reads=["sgd"], writes=["dmx"])
        op("scalar", lambda e: [e.activation(Lc[:], dmx[:], AF.Ln)],
           reads=["dmx"], writes=["Lc"])
        op("vector", lambda e: [e.tensor_scalar(nLc[:], Lc[:], -1.0, None, ALU.mult)],
           reads=["Lc"], writes=["nLc"])

        import numpy as _np
        _eps = float(_np.float32(1e-5))

        def emit_quant_round(m, c):
            wTm = wT[m]
            sg = (stageQ + stageP)[c % 4]
            skey = ("stageQ0", "stageQ1", "stageP0", "stageP1")[c % 4]
            op("sync", lambda e, m=m, c=c, sg=sg: [
                e.dma_start(sg[:], w_h[m][c * 128:(c + 1) * 128, :])],
                writes=[skey], kind="dma")
            qta = sA[:, 0:D]
            qtb = sB[:, 0:D]
            qwq = sC[:, 0:D]
            qwbf = qwb[:]
            op("scalar", lambda e, sg=sg, qta=qta: [
                e.activation(qta, sg[:], AF.Abs, accum_out=qa[:, 0:1])],
               reads=[skey], writes=["sA", "qa"])
            op("gpsimd", lambda e: [
                e.tensor_scalar(qs[:], qa[:], 1.0 / D, _eps, ALU.mult, ALU.max),
                e.tensor_scalar(qsh[:], qa[:], 0.5 / D, 0.5 * _eps, ALU.mult, ALU.max)],
               reads=["qa"], writes=["qs", "qsh"])
            op("scalar", lambda e, sg=sg, qtb=qtb: [
                e.activation(qtb, sg[:], AF.Sign)],
               reads=[skey], writes=["sB"])
            op("vector", lambda e, qwq=qwq, qta=qta: [
                e.tensor_scalar(qwq, qta, qsh[:, 0:1], qs[:, 0:1],
                                ALU.is_ge, ALU.mult)],
               reads=["sA", "qsh", "qs"], writes=["sC"])
            op("vector", lambda e, qwbf=qwbf, qwq=qwq, qtb=qtb: [
                e.tensor_tensor(qwbf, qwq, qtb, ALU.mult)],
               reads=["sC", "sB"], writes=["qwb"])
            for q in range(2):
                bank = pb[q]

                def trq(e, q=q, bank=bank, qwbf=qwbf):
                    return [e.matmul(bank[:, i2 * 128:(i2 + 1) * 128],
                                     qwbf[:, (4 * q + i2) * 128:(4 * q + i2 + 1) * 128],
                                     ident[:], start=True, stop=True)
                            for i2 in range(4)]
                op("tensor", trq, reads=["qwb", "ident"], writes=[f"p{q}"])

                def evq(e, q=q, bank=bank, m=m, c=c, wTm=wTm):
                    ins = []
                    for i2 in range(4):
                        kc = 4 * q + i2
                        dst = wTm[:, kc * D + c * 128: kc * D + (c + 1) * 128]
                        src = bank[:, i2 * 128:(i2 + 1) * 128]
                        if m == "wo":
                            ins.append(e.tensor_scalar(dst, src, lnw[:, kc:kc + 1], None, ALU.mult))
                        else:
                            ins.append(e.tensor_copy(dst, src))
                    return ins
                op("vector", evq, reads=[f"p{q}", "lnw"], writes=[m + "T"])

        def quantize_matrix(m):
            for c in range(8):
                emit_quant_round(m, c)

        quantize_matrix("wk")
        quantize_matrix("wv")

        def emit_prep_expert(ex):
            kb = ebuf[ex % 2]
            bkey = f"ebuf{ex % 2}"

            def ldcast(e, ex=ex, kb=kb):
                return [e.dma_start(kb[:, hh * D:(hh + 1) * D],
                                    ewt_h[ex, hh * 128:(hh + 1) * 128, :])
                        for hh in range(8)]
            op("gpsimd", ldcast, writes=[bkey], kind="dma", ndma=8)
            op("gpsimd", lambda e, ex=ex, kb=kb: [e.dma_start(escr_h[ex], kb[:])],
               reads=[bkey], writes=[f"escr{ex}"], kind="dma")

        MAGICBIG = None

        def emit_ln_load(st, pp, x_hbm, prefix):
            xs = xb[pp]
            io_hbm = iop_h if prefix else iom_h

            def ldx(e, st=st, xs=xs, x_hbm=x_hbm):
                return [e.dma_start(xs[:, j * D:(j + 1) * D],
                                    x_hbm[st * ST + j * 128: st * ST + (j + 1) * 128, :])
                        for j in range(NSUB)]
            op("sync", ldx, writes=[f"x{pp}"], kind="dma", ndma=NSUB)
            op("sync", lambda e, st=st: [e.dma_start(iot[pp][:], io_hbm[:, st * ST:(st + 1) * ST])],
               writes=[f"iot{pp}"], kind="dma")

        def emit_ln_math(st, pp, x_hbm, prefix):
            xs = xb[pp]
            xnt = xn_tm[pp]
            if prefix:
                op("vector", lambda e, xs=xs: [
                    e.reduce_sum(t1c[:, j:j + 1], xs[:, j * D:(j + 1) * D], axis=AX.X)
                    for j in range(NSUB)],
                    reads=[f"x{pp}"], writes=["t1c"])
            else:
                op("scalar", lambda e, xs=xs: [
                    e.activation(sA[:, j * D:(j + 1) * D], xs[:, j * D:(j + 1) * D],
                                 AF.Identity, accum_out=t1c[:, j:j + 1])
                    for j in range(NSUB)],
                    reads=[f"x{pp}"], writes=["sA", "t1c"])
            op("scalar", lambda e: [
                e.mul(mcol[:, j:j + 1], t1c[:, j:j + 1], 1.0 / D)
                for j in range(NSUB)],
                reads=["t1c"], writes=["mcol"])
            op("scalar", lambda e, xs=xs: [
                e.activation(sA[:, j * D:(j + 1) * D], xs[:, j * D:(j + 1) * D],
                             AF.Square, accum_out=sq2[:, j:j + 1])
                for j in range(NSUB)],
                reads=[f"x{pp}"], writes=["sA", "sq2"])
            op("scalar", lambda e: [
                e.mul(s2c[:, j:j + 1], sq2[:, j:j + 1], 1.0 / D)
                for j in range(NSUB)],
                reads=["sq2"], writes=["s2c"])
            op("vector", lambda e: [
                e.scalar_tensor_tensor(nvar[:, j:j + 1], mcol[:, j:j + 1],
                                       mcol[:, j:j + 1], s2c[:, j:j + 1],
                                       ALU.mult, ALU.subtract)
                for j in range(NSUB)],
                reads=["mcol", "s2c"], writes=["nvar"])
            op("scalar", lambda e: [
                e.activation(lnc[:, j:j + 1], nvar[:, j:j + 1], AF.Ln,
                             bias=cst[:, 0:1], scale=-1.0)
                for j in range(NSUB)],
                reads=["nvar", "cst"], writes=["lncol"])
            op("scalar", lambda e: [
                e.activation(acol[:, j:j + 1], lnc[:, j:j + 1], AF.Exp, scale=-0.5)
                for j in range(NSUB)],
                reads=["lncol"], writes=["acol"])
            if prefix:
                op("vector", lambda e, xs=xs, xnt=xnt: [
                    e.tensor_scalar(xnt[:, j * D:(j + 1) * D], xs[:, j * D:(j + 1) * D],
                                    mcol[:, j:j + 1], acol[:, j:j + 1],
                                    ALU.subtract, ALU.mult)
                    for j in range(NSUB)],
                    reads=[f"x{pp}", "mcol", "acol"], writes=[f"xn_tm{pp}"])
            else:
                op("vector", lambda e: [
                    e.tensor_scalar(acol2[:, 0:NSUB], acol[:, 0:NSUB], 1.0,
                                    None, ALU.mult)],
                   reads=["acol"], writes=["acol2"])
                op("vector", lambda e: [
                    e.scalar_tensor_tensor(nbc[:, 0:NSUB], mcol[:, 0:NSUB], -1.0,
                                           acol[:, 0:NSUB], ALU.mult, ALU.mult)],
                   reads=["mcol", "acol"], writes=["nbc"])
                op("scalar", lambda e, xs=xs, xnt=xnt: [
                    e.activation(xnt[:, j * D:(j + 1) * D], xs[:, j * D:(j + 1) * D],
                                 AF.Identity, bias=nbc[:, j:j + 1],
                                 scale=acol2[:, j:j + 1])
                    for j in range(NSUB)],
                    reads=[f"x{pp}", "nbc", "acol2"], writes=[f"xn_tm{pp}"])

        def emit_ln(st, pp, x_hbm, prefix):
            emit_ln_load(st, pp, x_hbm, prefix)
            emit_ln_math(st, pp, x_hbm, prefix)

        def emit_transposes(pp):
            xnt, xnf = xn_tm[pp], xn_fm[pp]
            for g in range(4):
                bank = pb[g % 2]

                def trg(e, g=g, bank=bank, xnt=xnt):
                    ins = []
                    for u2 in range(4):
                        c = 2 * g + u2 // 2
                        j = u2 % 2
                        ins.append(e.matmul(bank[:, u2 * 128:(u2 + 1) * 128],
                                            xnt[:, j * D + c * 128: j * D + (c + 1) * 128],
                                            ident[:], start=True, stop=True))
                    return ins
                op("tensor", trg, reads=[f"xn_tm{pp}", "ident"], writes=[f"p{g % 2}"])
                op("scalar", lambda e, g=g, bank=bank, xnf=xnf: [
                    e.copy(xnf[:, (2 * g) * ST:(2 * g + 2) * ST], bank[:, 0:512])],
                    reads=[f"p{g % 2}"], writes=[f"xn_fm{pp}"])

        def emit_gate(pp, dbg=False):
            xnf = xn_fm[pp]
            ohx = ohb[pp]
            for j in range(NSUB):
                gb = pb[2 + j]
                gbk = f"p{2 + j}"

                def gmm(e, j=j, xnf=xnf, gb=gb):
                    ins = []
                    for kc in range(NCH):
                        ins.append(e.matmul(gb[:, j * 8:(j + 1) * 8],
                                            xnf[:, kc * ST + j * 128: kc * ST + (j + 1) * 128],
                                            gws[:, kc * 8:(kc + 1) * 8],
                                            start=(kc == 0), stop=(kc == NCH - 1)))
                    return ins
                op("tensor", gmm, reads=[f"xn_fm{pp}", "gws"], writes=[gbk])
                op("scalar", lambda e, j=j, gb=gb: [
                    e.copy(scb[:, j * 8:(j + 1) * 8], gb[:, j * 8:(j + 1) * 8])],
                    reads=[gbk], writes=["scb"])
            for j in range(NSUB):
                op("vector", lambda e, j=j: [
                    e.reduce_max(gmx[:, j:j + 1], scb[:, j * 8:(j + 1) * 8], axis=AX.X)],
                    reads=["scb"], writes=["gmx"])
            op("scalar", lambda e: [
                e.mul(gmx2[:], gmx[:], -1.0)],
                reads=["gmx"], writes=["gmx2"])
            for j, (ta, tb) in enumerate(((t8a, t8b), (t8c, t8d))):
                ka, kb = f"t8{j}a", f"t8{j}b"
                op("scalar", lambda e, j=j, ta=ta: [
                    e.activation(ta[:], scb[:, j * 8:(j + 1) * 8], AF.Identity,
                                 bias=gmx2[:, j:j + 1])],
                    reads=["scb", "gmx2"], writes=[ka])
                op("scalar", lambda e, ta=ta, tb=tb: [
                    e.activation(tb[:], ta[:], AF.Sign)],
                    reads=[ka], writes=[kb])
                op("scalar", lambda e, ta=ta, tb=tb: [
                    e.activation(ta[:], tb[:], AF.Abs)],
                    reads=[kb], writes=[ka])
                op("scalar", lambda e, ta=ta, tb=tb: [
                    e.activation(tb[:], ta[:], AF.Identity, bias=cst[:, 2:3], scale=-1.0)],
                    reads=[ka, "cst"], writes=[kb])
                op("vector", lambda e, ta=ta, tb=tb: [
                    e.tensor_tensor(ta[:], tb[:], io8r[:], ALU.mult)],
                    reads=[kb, "io8r"], writes=[ka])
            op("scalar", lambda e: [
                e.copy(m3g[:, 0:8], t8a[:]),
                e.copy(m3g[:, 8:16], t8c[:])],
                reads=["t80a", "t81a"], writes=["m3g"])
            for j in range(NSUB):
                op("vector", lambda e, j=j: [
                    e.reduce_max(mx3[:, j:j + 1], m3g[:, j * 8:(j + 1) * 8], axis=AX.X)],
                    reads=["m3g"], writes=["mx3"])
            op("scalar", lambda e: [
                e.activation(idxc[:, j:j + 1], mx3[:, j:j + 1], AF.Identity,
                             bias=cst[:, 3:4])
                for j in range(NSUB)],
                reads=["mx3", "cst"], writes=["idxc"])
            for j, (ta, tb) in enumerate(((t8a, t8b), (t8c, t8d))):
                ka, kb = f"t8{j}a", f"t8{j}b"
                op("scalar", lambda e, j=j, ta=ta: [
                    e.activation(ta[:], io8[:], AF.Identity, bias=idxc[:, j:j + 1])],
                    reads=["io8", "idxc"], writes=[ka])
                op("scalar", lambda e, ta=ta, tb=tb: [
                    e.activation(tb[:], ta[:], AF.Sign)],
                    reads=[ka], writes=[kb])
                op("scalar", lambda e, ta=ta, tb=tb: [
                    e.activation(ta[:], tb[:], AF.Abs)],
                    reads=[kb], writes=[ka])
                op("scalar", lambda e, j=j, ta=ta, ohx=ohx: [
                    e.activation(ohx[:, j * 8:(j + 1) * 8], ta[:], AF.Identity,
                                 bias=cst[:, 2:3], scale=-1.0)],
                    reads=[ka, "cst"], writes=[f"ohb{pp}"])
            if dbg:
                tap("idxc_g", idxc[:], ["idxc"])

        def emit_proj(pp, m, banks, bankkeys):
            xnf = xn_fm[pp]
            for i in range(4):
                bank = banks[i]

                def pmm(e, i=i, bank=bank, m=m, xnf=xnf):
                    ins = []
                    for ce in (2 * i, 2 * i + 1):
                        for kc in range(NCH):
                            ins.append(e.matmul(
                                bank[:, (ce % 2) * ST:(ce % 2) * ST + ST],
                                wT[m][:, kc * D + ce * 128: kc * D + (ce + 1) * 128],
                                xnf[:, kc * ST:(kc + 1) * ST],
                                start=(kc == 0), stop=(kc == NCH - 1)))
                    return ins
                op("tensor", pmm, reads=[m + "T", f"xn_fm{pp}"], writes=[bankkeys[i]])

        def emit_scan_ue(st, pp):
            iost = iot[pp]
            op("vector", lambda e, iost=iost: [
                e.tensor_scalar(sA[:, c * ST:(c + 1) * ST], iost[:],
                                nLc[:, c:c + 1], 23.025850929940457,
                                ALU.mult, ALU.min)
                for c in range(NCH)],
                reads=[f"iot{pp}", "nLc"], writes=["sA"])
            op("scalar", lambda e: [e.activation(sB[:], sA[:], AF.Exp)],
               reads=["sA"], writes=["sB"])

        def emit_scan_a(st, pp, with_scale, kb0=4):
            iost = iot[pp]
            op("vector", lambda e, kb0=kb0: [
                e.scalar_tensor_tensor(sC[:, 2 * i * ST:(2 * i + 2) * ST],
                                       sB[:, 2 * i * ST:(2 * i + 2) * ST],
                                       CLIP_INV, pb[kb0 + i][:, 0:512],
                                       ALU.min, ALU.mult)
                for i in range(4)],
                reads=["sB"] + [f"p{kb0 + i}" for i in range(4)], writes=["sC"])
            if with_scale:
                op("scalar", lambda e, iost=iost: [
                    e.activation(sA[:, c * ST:(c + 1) * ST], iost[:], AF.Exp,
                                 scale=Lc[:, c:c + 1])
                    for c in range(NCH)],
                    reads=[f"iot{pp}", "Lc"], writes=["sA"])

        def emit_scan_b(st, pp, prefix_st=None):
            if prefix_st is not None:
                op("vector", lambda e, k=prefix_st: [
                    e.scalar_tensor_tensor(sD[:, c * ST:(c + 1) * ST],
                                           pb[c // 2][:, (c % 2) * ST:(c % 2) * ST + ST],
                                           1.0,
                                           sC[:, c * ST:(c + 1) * ST],
                                           ALU.mult, ALU.mult,
                                           accum_out=tcols[:, k * 8 + c:k * 8 + c + 1])
                    for c in range(NCH)],
                    reads=["p0", "p1", "p2", "p3", "sC"], writes=["sD", "tcols"])
                return
            op("vector", lambda e: [
                e.tensor_tensor(sD[:, 2 * i * ST:(2 * i + 2) * ST],
                                pb[i][:, 0:512],
                                sC[:, 2 * i * ST:(2 * i + 2) * ST], ALU.mult)
                for i in range(4)],
                reads=["p0", "p1", "p2", "p3", "sC"], writes=["sD"])
            op("vector", lambda e: [
                e.tensor_tensor_scan(sC[:, c * ST:(c + 1) * ST],
                                     sD[:, c * ST:(c + 1) * ST], z256[:],
                                     carry[:, c:c + 1], ALU.add, ALU.add)
                for c in range(NCH)],
                reads=["sD", "z256", "carry"], writes=["sC"])
            op("scalar", lambda e: [
                e.copy(carry[:, c:c + 1], sC[:, c * ST + ST - 1: c * ST + ST])
                for c in range(NCH)],
                reads=["sC"], writes=["carry"])

        def emit_y(rb0=4):
            op("scalar", lambda e, rb0=rb0: [
                e.activation(sD[:, 2 * i * ST:(2 * i + 2) * ST],
                             pb[rb0 + i][:, 0:512], AF.Sigmoid)
                for i in range(4)],
                reads=[f"p{rb0 + i}" for i in range(4)], writes=["sD"])
            op("vector", lambda e: [e.tensor_tensor(sB[:], sD[:], sA[:], ALU.mult)],
               reads=["sD", "sA"], writes=["sB"])
            op("vector", lambda e: [e.tensor_tensor(ybf[:], sC[:], sB[:], ALU.mult)],
               reads=["sC", "sB"], writes=["ybf"])
            op("scalar", lambda e: [e.activation(y2g[:], ybf[:], AF.Square)],
               reads=["ybf"], writes=["y2g"])

        esched = []
        for _p in range(NST):
            _order = range(7, -1, -1) if _p % 2 == 0 else range(8)
            for _i, _ex in enumerate(_order):
                _buf = (_i + 1) % 2 if _p % 2 == 0 else _i % 2
                esched.append((_ex, _buf, _i >= 2))

        def emit_expert_load(g):
            if g >= len(esched):
                return
            ex, buf, needs = esched[g]
            if not needs:
                return
            op("gpsimd", lambda e, ex=ex, kbuf=ebuf[buf]: [e.dma_start(kbuf[:], escr_h[ex])],
               reads=[f"escr{ex}"], writes=[f"ebuf{buf}"], kind="dma")

        def emit_experts(st, pp, hooks=None):
            xnf = xn_fm[pp]
            for i in range(8):
                if hooks and i in hooks:
                    hooks[i]()
                ex, buf, _ = esched[st * 8 + i]
                kbuf = ebuf[buf]
                bkey = f"ebuf{buf}"
                for j in range(NSUB):
                    for nh in range(2):
                        bank = pb[4 + j * 2 + nh]
                        bk = f"p{4 + j * 2 + nh}"

                        def emm(e, j=j, nh=nh, bank=bank, kbuf=kbuf, xnf=xnf):
                            ins = []
                            for kc in range(NCH):
                                ins.append(e.matmul(
                                    bank[:, 0:512],
                                    xnf[:, kc * ST + j * 128: kc * ST + (j + 1) * 128],
                                    kbuf[:, kc * D + nh * 512: kc * D + nh * 512 + 512],
                                    start=(kc == 0), stop=(kc == NCH - 1)))
                            return ins
                        op("tensor", emm, reads=[f"xn_fm{pp}", bkey], writes=[bk])
                        dst = moe[(i + 1) % 2][:, j * D + nh * 512: j * D + nh * 512 + 512]
                        ohcol = ohb[pp][:, j * 8 + ex: j * 8 + ex + 1]
                        if i == 0:
                            op("vector", lambda e, dst=dst, bank=bank, ohcol=ohcol: [
                                e.tensor_scalar(dst, bank[:, 0:512], ohcol, None, ALU.mult)],
                                reads=[bk, f"ohb{pp}"], writes=["moe1"])
                        else:
                            src2 = moe[i % 2][:, j * D + nh * 512: j * D + nh * 512 + 512]
                            op("vector", lambda e, dst=dst, bank=bank, ohcol=ohcol, src2=src2: [
                                e.scalar_tensor_tensor(dst, bank[:, 0:512], ohcol, src2,
                                                       ALU.mult, ALU.add)],
                                reads=[bk, f"ohb{pp}", f"moe{i % 2}"], writes=[f"moe{(i + 1) % 2}"])
                emit_expert_load(st * 8 + i + 2)
            if hooks and 8 in hooks:
                hooks[8]()

        def emit_rms_wo_out(st, pp):
            xs = xb[pp]
            for j in range(NSUB):
                def s2mm(e, j=j):
                    ins = []
                    for kc in range(NCH):
                        ins.append(e.matmul(pb[5][:, 16 + j:17 + j],
                                            y2g[:, kc * ST + j * 128: kc * ST + (j + 1) * 128],
                                            onesb[:],
                                            start=(kc == 0), stop=(kc == NCH - 1)))
                    return ins
                op("tensor", s2mm, reads=["y2g", "onesb"], writes=["p5"])
                op("scalar", lambda e, j=j: [
                    e.activation(rsl[:, j:j + 1], pb[5][:, 16 + j:17 + j], AF.Ln,
                                 bias=cst[:, 1:2], scale=1.0 / D)],
                    reads=["p5", "cst"], writes=["rsl"])
                op("scalar", lambda e, j=j: [
                    e.activation(rsv[:, j:j + 1], rsl[:, j:j + 1], AF.Exp, scale=-0.5)],
                    reads=["rsl"], writes=["rsv"])
            op("vector", lambda e: [
                e.tensor_scalar(rsv2[:, 0:NSUB], rsv[:, 0:NSUB], 1.0, None, ALU.mult)],
               reads=["rsv"], writes=["rsv2"])
            wob = {0: (4, 6), 1: (7, 5)}
            for j in range(NSUB):
                for nh in range(2):
                    bank = pb[wob[j][nh]]
                    bk = f"p{wob[j][nh]}"

                    def womm(e, j=j, nh=nh, bank=bank):
                        ins = []
                        for kc in range(NCH):
                            ins.append(e.matmul(
                                bank[:, 0:512],
                                ybf[:, kc * ST + j * 128: kc * ST + (j + 1) * 128],
                                wT["wo"][:, kc * D + nh * 512: kc * D + nh * 512 + 512],
                                start=(kc == 0), stop=(kc == NCH - 1)))
                        return ins
                    op("tensor", womm, reads=["ybf", "woT"], writes=[bk])
                    op("scalar", lambda e, j=j, nh=nh, bank=bank: [
                        e.activation(s_sb[:, j * D + nh * 512: j * D + nh * 512 + 512],
                                     bank[:, 0:512], AF.Identity,
                                     scale=rsv2[:, j:j + 1])],
                        reads=[bk, "rsv2"], writes=["s_sb"])
        def emit_gpre(st, pp):
            for j in range(NSUB):
                op("vector", lambda e, j=j, pp=pp: [
                    e.tensor_tensor(stageP[j][:], moe[0][:, j * D:(j + 1) * D],
                                    xn_tm[pp][:, j * D:(j + 1) * D], ALU.add)],
                    reads=["moe0", f"xn_tm{pp}"], writes=[f"stageP{j}"])

        def emit_xt_load(st):
            def ldxt(e, st=st):
                return [e.dma_start(stageQ[j][:],
                                    xmain_h[st * ST + j * 128: st * ST + (j + 1) * 128, :])
                        for j in range(NSUB)]
            op("sync", ldxt, writes=["stageQ0", "stageQ1"], kind="dma", ndma=NSUB)

        def emit_tail2a(st, pp):
            op("scalar", lambda e: [
                e.activation(sD[:, 0:D], stageP[0][:], AF.Sigmoid),
                e.activation(sD[:, D:2 * D], stageP[1][:], AF.Sigmoid)],
               reads=["stageP0", "stageP1"], writes=["sD"])
            for j in range(NSUB):
                op("vector", lambda e, j=j: [
                    e.tensor_tensor(sB[:, j * D:(j + 1) * D], stageP[j][:],
                                    sD[:, j * D:(j + 1) * D], ALU.mult)],
                    reads=[f"stageP{j}", "sD"], writes=["sB"])

        def emit_tail2b(st, pp):
            for j in range(NSUB):
                op("vector", lambda e, j=j: [
                    e.tensor_tensor(sD[:, j * D:(j + 1) * D], s_sb[:, j * D:(j + 1) * D],
                                    sB[:, j * D:(j + 1) * D], ALU.mult)],
                    reads=["s_sb", "sB"], writes=["sD"])
                op("vector", lambda e, j=j: [
                    e.tensor_tensor(s_sb[:, j * D:(j + 1) * D], stageQ[j][:],
                                    sD[:, j * D:(j + 1) * D], ALU.add)],
                    reads=[f"stageQ{j}", "sD"], writes=["s_sb"])
                op("sync", lambda e, st=st, j=j: [
                    e.dma_start(out_h[st * ST + j * 128: st * ST + (j + 1) * 128, :],
                                s_sb[:, j * D:(j + 1) * D])],
                    reads=["s_sb"], writes=["outhbm"], kind="dma")

        def emit_tail2(st, pp):
            emit_tail2a(st, pp)
            emit_tail2b(st, pp)

        quantize_matrix("wk")
        quantize_matrix("wv")
        emit_ln(0, 0, xpre_h, prefix=True)
        emit_scan_ue(0, 0)
        for st in range(NST):
            pp = st % 2
            emit_transposes(pp)
            emit_proj(pp, "wk", [pb[4 + i] for i in range(4)], [f"p{4 + i}" for i in range(4)])
            emit_proj(pp, "wv", [pb[i] for i in range(4)], [f"p{i}" for i in range(4)])
            if st + 1 < NST:
                emit_ln(st + 1, (st + 1) % 2, xpre_h, prefix=True)
            emit_scan_a(st, pp, with_scale=False, kb0=4)
            if st + 1 < NST:
                emit_scan_ue(st + 1, (st + 1) % 2)
            emit_scan_b(st, pp, prefix_st=st)
            emit_quant_round("wr" if st < 4 else "wo", 2 * (st % 4))
            emit_quant_round("wr" if st < 4 else "wo", 2 * (st % 4) + 1)
            emit_prep_expert(st)
        op("vector", lambda e: [
            e.tensor_tensor(tc2[:], tcols[:, 0:32], tcols[:, 32:64], ALU.add)],
            reads=["tcols"], writes=["tc2"])
        op("vector", lambda e: [
            e.tensor_tensor(tcols[:, 0:16], tc2[:, 0:16], tc2[:, 16:32], ALU.add)],
            reads=["tc2"], writes=["tcols"])
        op("vector", lambda e: [
            e.tensor_tensor(carry[:], tcols[:, 0:8], tcols[:, 8:16], ALU.add)],
            reads=["tcols"], writes=["carry"])
        if debug_taps:
            for m in ("wk", "wv", "wr", "wo"):
                tap(m + "T", wT[m][:], [m + "T"])

        def front_A(st):
            pp = st % 2
            emit_transposes(pp)
            emit_gate(pp)

        def front_B(st):
            pp = st % 2
            emit_proj(pp, "wk", [pb[i] for i in range(4)], [f"p{i}" for i in range(4)])
            emit_scan_a(st, pp, with_scale=True, kb0=0)

        def front_C(st):
            pp = st % 2
            emit_proj(pp, "wv", [pb[i] for i in range(4)], [f"p{i}" for i in range(4)])
            emit_scan_b(st, pp)

        def front_D(st):
            pp = st % 2
            emit_proj(pp, "wr", [pb[i] for i in range(4)], [f"p{i}" for i in range(4)])

        emit_ln(0, 0, xmain_h, prefix=False)
        emit_scan_ue(0, 0)
        front_A(0)
        front_B(0)
        front_C(0)
        front_D(0)
        def mk_hooks(st):
            pp0 = (st - 1) % 2

            def h1():
                emit_y(rb0=0)
                if st > 0:
                    emit_gpre(st - 1, pp0)
                if st + 1 < NST:
                    emit_ln_load(st + 1, (st + 1) % 2, xmain_h, prefix=False)

            def h2():
                if st + 1 < NST:
                    emit_ln_math(st + 1, (st + 1) % 2, xmain_h, prefix=False)
                    emit_scan_ue(st + 1, (st + 1) % 2)

            def h3():
                if st + 1 < NST:
                    front_A(st + 1)

            def h4():
                if st > 0:
                    emit_tail2a(st - 1, pp0)

            def h5():
                if st + 1 < NST:
                    front_B(st + 1)
                if st > 0:
                    emit_tail2b(st - 1, pp0)
                emit_xt_load(st)

            def h6():
                if st + 1 < NST:
                    front_C(st + 1)

            def h8():
                if st + 1 < NST:
                    front_D(st + 1)
            return {1: h1, 2: h2, 3: h3, 4: h4, 5: h5, 6: h6, 8: h8}

        for st in range(NST):
            pp = st % 2
            emit_experts(st, pp, mk_hooks(st))
            emit_rms_wo_out(st, pp)
        emit_gpre(NST - 1, (NST - 1) % 2)
        emit_tail2(NST - 1, (NST - 1) % 2)

        P.finalize()
        for k in P.dma_semkeys:
            sems[k] = stack.enter_context(nc.semaphore("sd_" + k))
        block = stack.enter_context(nc.Block())

        @block.sync
        def _(sync):
            P.emit("sync", sync, sems)
            sync.wait_ge(sems["outhbm"], P.totals["outhbm"])
            if debug_taps and "taphbm" in sems:
                sync.wait_ge(sems["taphbm"], P.totals["taphbm"])

        @block.gpsimd
        def _(gpsimd):
            P.emit("gpsimd", gpsimd, sems)

        @block.vector
        def _(vector):
            P.emit("vector", vector, sems)

        @block.scalar
        def _(scalar):
            P.emit("scalar", scalar, sems)

        @block.tensor
        def _(tensor):
            P.emit("tensor", tensor, sems)

    return nc


_NC_CACHE = {}


def _get_nc():
    if "nc" not in _NC_CACHE:
        _NC_CACHE["nc"] = build_nc()
    return _NC_CACHE["nc"]


def kernel(L0_attn_x, L0_attn_Wr, L0_attn_Wk, L0_attn_Wv, L0_attn_Wo,
           L0_attn_decay, L0_attn_ln_w, L0_ffn_gate_w, L0_ffn_expert_w):
    x = np.ascontiguousarray(np.asarray(L0_attn_x, dtype=np.float32))
    Wr = np.ascontiguousarray(np.asarray(L0_attn_Wr, dtype=np.float32))
    Wk = np.ascontiguousarray(np.asarray(L0_attn_Wk, dtype=np.float32))
    Wv = np.ascontiguousarray(np.asarray(L0_attn_Wv, dtype=np.float32))
    Wo = np.ascontiguousarray(np.asarray(L0_attn_Wo, dtype=np.float32))
    decay = np.asarray(L0_attn_decay, dtype=np.float32)
    ln_w = np.asarray(L0_attn_ln_w, dtype=np.float32)
    gate_w = np.asarray(L0_ffn_gate_w, dtype=np.float32)
    expert_w = np.asarray(L0_ffn_expert_w, dtype=np.float32)

    ident = np.eye(128, dtype=np.float32).astype(ml_dtypes.bfloat16)
    io8 = np.broadcast_to(np.arange(8, dtype=np.float32), (128, 8)).copy()
    io8r = (8.0 - io8).astype(np.float32)
    dec8 = np.ascontiguousarray(decay.reshape(8, 128).T)
    lnw8 = np.ascontiguousarray(ln_w.reshape(8, 128).T)
    gwt = np.ascontiguousarray(gate_w.T)
    ewt = np.ascontiguousarray(expert_w.transpose(0, 2, 1))
    ar = np.arange(HALF, dtype=np.float32)
    io_lo = np.broadcast_to(ar, (128, HALF)).copy()
    io_hi = np.broadcast_to(ar + HALF, (128, HALF)).copy()
    zeros_pre = np.zeros((HALF, D), dtype=np.float32)

    in_maps = []
    for c in range(8):
        b, half = c // 2, c % 2
        in_maps.append({
            "xmain": np.ascontiguousarray(x[b, half * HALF:(half + 1) * HALF]),
            "xpre": (np.ascontiguousarray(x[b, 0:HALF]) if half else zeros_pre),
            "wk": Wk, "wv": Wv, "wr": Wr, "wo": Wo,
            "ewt": ewt, "gwt": gwt,
            "dec8": dec8, "lnw8": lnw8,
            "iomain": io_hi if half else io_lo,
            "iopre": io_lo,
            "ident": ident, "io8": io8, "io8r": io8r,
        })

    nc = _get_nc()
    res = run_bass_kernel_spmd(nc, in_maps, core_ids=list(range(8)))
    out = np.empty((B, T, D), dtype=np.float32)
    for c in range(8):
        b, half = c // 2, c % 2
        out[b, half * HALF:(half + 1) * HALF] = res.results[c]["out"]
    return out


# version 38
# speedup vs baseline: 1.0545x; 1.0212x over previous
import numpy as np
import ml_dtypes
import concourse.bass as bass
import concourse.mybir as mybir
from concourse.bass_utils import run_bass_kernel_spmd
from contextlib import ExitStack

F32 = mybir.dt.float32
BF16 = mybir.dt.bfloat16
AF = mybir.ActivationFunctionType
ALU = mybir.AluOpType
AX = mybir.AxisListType

B, T, D = 4, 4096, 1024
HALF = T // 2
ST = 256
NST = HALF // ST
NSUB = ST // 128
NCH = D // 128
CLIP_INV = 1.0e10
LN10E = 23.025850929940457

ENGINES = ("sync", "gpsimd", "vector", "scalar", "tensor")


class Op:
    __slots__ = ("engine", "fn", "deps", "kind", "ndma", "awaited", "cnt", "idx",
                 "semkey")

    def __init__(self, engine, fn, deps, kind, ndma, idx):
        self.engine = engine
        self.fn = fn
        self.deps = deps
        self.kind = kind
        self.ndma = ndma
        self.awaited = False
        self.cnt = 0
        self.idx = idx


class Prog:
    def __init__(self):
        self.ops = []
        self.writer = {}
        self.readers = {}

    def op(self, engine, fn, reads=(), writes=(), kind="comp", ndma=1):
        if kind == "dma":
            assert writes, "dma ops must name their destination buffer"
        deps = []
        for b in reads:
            w = self.writer.get(b)
            if w is not None:
                deps.append(w)
        for b in writes:
            deps.extend(self.readers.get(b, ()))
            w = self.writer.get(b)
            if w is not None:
                deps.append(w)
        seen = set()
        ud = []
        for d in deps:
            if id(d) not in seen:
                seen.add(id(d))
                ud.append(d)
        h = Op(engine, fn, ud, kind, ndma, len(self.ops))
        h.semkey = writes[0] if kind == "dma" else engine
        self.ops.append(h)
        for b in reads:
            self.readers.setdefault(b, []).append(h)
        for b in writes:
            self.writer[b] = h
            self.readers[b] = []
        return h

    def finalize(self):
        for h in self.ops:
            for d in h.deps:
                d.awaited = True
        cnt = {e: 0 for e in ENGINES}
        for h in self.ops:
            if h.kind == "dma":
                cnt.setdefault(h.semkey, 0)
                cnt[h.semkey] += 16 * h.ndma
            elif h.awaited:
                cnt[h.engine] += 1
            h.cnt = cnt[h.semkey]
        self.totals = cnt
        self.dma_semkeys = sorted(k for k in cnt if k not in ENGINES)

    def emit(self, eng_name, eng, sems):
        last = {}
        for h in self.ops:
            if h.engine != eng_name:
                continue
            for d in h.deps:
                if (h.engine == "tensor" and d.engine == "tensor"
                        and d.kind != "dma"):
                    continue
                if not d.awaited or d.cnt <= last.get(d.semkey, 0):
                    continue
                eng.wait_ge(sems[d.semkey], d.cnt)
                last[d.semkey] = d.cnt
            insts = h.fn(eng)
            if h.kind == "dma":
                for i in insts:
                    i.then_inc(sems[h.semkey], 16)
            elif h.awaited:
                insts[-1].then_inc(sems[h.engine], 1)


def build_nc(debug_taps=False):
    nc = bass.Bass("TRN2", target_bir_lowering=False)

    xmain_h = nc.dram_tensor("xmain", [HALF, D], F32, kind="ExternalInput")
    xpre_h = nc.dram_tensor("xpre", [HALF, D], F32, kind="ExternalInput")
    w_h = {m: nc.dram_tensor(m, [D, D], F32, kind="ExternalInput")
           for m in ("wk", "wv", "wr", "wo")}
    ewt_h = nc.dram_tensor("ewt", [8, D, D], F32, kind="ExternalInput")
    gwt_h = nc.dram_tensor("gwt", [D, 8], F32, kind="ExternalInput")
    dec_h = nc.dram_tensor("dec8", [128, 8], F32, kind="ExternalInput")
    lnw_h = nc.dram_tensor("lnw8", [128, 8], F32, kind="ExternalInput")
    iom_h = nc.dram_tensor("iomain", [128, HALF], F32, kind="ExternalInput")
    iop_h = nc.dram_tensor("iopre", [128, HALF], F32, kind="ExternalInput")
    id_h = nc.dram_tensor("ident", [128, 128], BF16, kind="ExternalInput")
    io8_h = nc.dram_tensor("io8", [128, 8], F32, kind="ExternalInput")
    io8r_h = nc.dram_tensor("io8r", [128, 8], F32, kind="ExternalInput")
    out_h = nc.dram_tensor("out", [HALF, D], F32, kind="ExternalOutput")
    escr_h = nc.dram_tensor("escr", [8, 128, NCH * D], BF16, kind="Internal")

    P = Prog()
    tap_handles = {}

    with ExitStack() as stack:
        def sb(name, shape, dt=F32):
            return stack.enter_context(nc.sbuf_tensor(name, shape, dt))

        wT = {m: sb(m + "T", [128, NCH * D], BF16) for m in ("wk", "wv", "wr", "wo")}
        ebuf = [sb(f"ebuf{i}", [128, NCH * D], BF16) for i in range(2)]
        gws = sb("gws", [128, NCH * 8], BF16)
        gwsf = sb("gwsf", [128, NCH * 8], F32)
        xb = [sb(f"xb{i}", [128, NSUB * D]) for i in range(2)]
        iot = [sb(f"iot{i}", [128, ST]) for i in range(2)]
        xn_tm = [sb(f"xn_tm{i}", [128, NSUB * D], BF16) for i in range(2)]
        xn_fm = [sb(f"xn_fm{i}", [128, NCH * ST], BF16) for i in range(2)]
        ybf = sb("ybf", [128, NCH * ST], BF16)
        y2g = sb("y2g", [128, NCH * ST], BF16)
        moe = [sb(f"moe{i}", [128, NSUB * D], BF16) for i in range(2)]
        sA = sb("sA", [128, NCH * ST])
        sB = sb("sB", [128, NCH * ST])
        sC = sb("sC", [128, NCH * ST])
        sD = sb("sD", [128, NCH * ST])
        s_sb = sb("s_sb", [128, NSUB * D])
        stageQ = [sb(f"stageQ{i}", [128, D]) for i in range(2)]
        stageP = [sb(f"stageP{i}", [128, D]) for i in range(2)]
        z256 = sb("z256", [128, ST])
        ident = sb("identb", [128, 128], BF16)
        onesb = sb("onesb", [128, 1], BF16)
        io8 = sb("io8b", [128, 8])
        io8r = sb("io8rb", [128, 8])
        decs = sb("decs", [128, 8])
        sgd = sb("sgd", [128, 8])
        dmx = sb("dmx", [128, 8])
        Lc = sb("Lc", [128, 8])
        nLc = sb("nLc", [128, 8])
        lnw = sb("lnwb", [128, 8])
        carry = sb("carryb", [128, NCH])
        tcols = sb("tcols", [128, NST * NCH])
        tc2 = sb("tc2", [128, 4 * NCH])
        scb = sb("scb", [128, 2 * 8])
        ohb = [sb(f"ohb{i}", [128, 2 * 8]) for i in range(2)]
        t8a = sb("t8a", [128, 8])
        t8b = sb("t8b", [128, 8])
        t8c = sb("t8c", [128, 8])
        t8d = sb("t8d", [128, 8])
        gmx = sb("gmx", [128, 2])
        gmx2 = sb("gmx2", [128, 2])
        m3g = sb("m3g", [128, 16])
        mx3 = sb("mx3", [128, 2])
        idxc = sb("idxc", [128, 2])
        mcol = sb("mcol", [128, 2])
        sq2 = sb("sq2", [128, 2])
        t1c = sb("t1c", [128, 2])
        s2c = sb("s2c", [128, 2])
        nvar = sb("nvar", [128, 2])
        lnc = sb("lncol", [128, 2])
        acol = sb("acol", [128, 2])
        acol2 = sb("acol2", [128, 2])
        nbc = sb("nbc", [128, 2])
        rsv = sb("rsv", [128, 2])
        rsv2 = sb("rsv2", [128, 2])
        rsl = sb("rsl", [128, 2])
        cst = sb("cstb", [128, 4])
        qa = sb("qa", [128, 1])
        qs = sb("qs", [128, 1])
        qsh = sb("qsh", [128, 1])
        qnsh = sb("qnsh", [128, 1])
        qwb = sb("qwb", [128, D], BF16)

        pb = [stack.enter_context(nc.psum_tensor(f"pb{i}", [128, 512], F32))
              for i in range(8)]
        sems = {e: stack.enter_context(nc.semaphore("sem_" + e)) for e in ENGINES}
        block = None

        op = P.op

        def tap(name, ap, readkeys):
            if not debug_taps:
                return
            th = nc.dram_tensor("tap_" + name, list(ap.shape), ap.dtype,
                                kind="ExternalOutput")
            tap_handles[name] = th
            op("sync", lambda e, th=th, ap=ap: [e.dma_start(th[:], ap)],
               reads=list(readkeys), writes=["taphbm"], kind="dma")

        op("sync", lambda e: [e.dma_start(ident[:], id_h[:])], writes=["ident"], kind="dma")
        op("sync", lambda e: [e.dma_start(io8[:], io8_h[:])], writes=["io8"], kind="dma")
        op("sync", lambda e: [e.dma_start(io8r[:], io8r_h[:])], writes=["io8r"], kind="dma")
        op("sync", lambda e: [e.dma_start(decs[:], dec_h[:])], writes=["decs"], kind="dma")
        op("sync", lambda e: [e.dma_start(lnw[:], lnw_h[:])], writes=["lnw"], kind="dma")
        op("sync", lambda e: [e.dma_start(gwsf[:, kc * 8:(kc + 1) * 8],
                                          gwt_h[kc * 128:(kc + 1) * 128, :])
                              for kc in range(NCH)],
           writes=["gwsf"], kind="dma", ndma=NCH)
        op("vector", lambda e: [e.memset(z256[:], 0.0)], writes=["z256"])
        op("vector", lambda e: [e.memset(onesb[:], 1.0)], writes=["onesb"])
        op("vector", lambda e: [e.memset(carry[:], 0.0)], writes=["carry"])
        op("vector", lambda e: [e.memset(cst[:, 0:1], 1e-5)], writes=["cst"])
        op("vector", lambda e: [e.memset(cst[:, 1:2], 1e-6)], writes=["cst"])
        op("vector", lambda e: [e.memset(cst[:, 2:3], 1.0)], writes=["cst"])
        op("vector", lambda e: [e.memset(cst[:, 3:4], -8.0)], writes=["cst"])
        op("vector", lambda e: [e.tensor_copy(gws[:], gwsf[:])], reads=["gwsf"], writes=["gws"])
        op("scalar", lambda e: [e.activation(sgd[:], decs[:], AF.Sigmoid)],
           reads=["decs"], writes=["sgd"])
        op("vector", lambda e: [e.tensor_scalar(dmx[:], sgd[:], 1e-7, None, ALU.max)],
           reads=["sgd"], writes=["dmx"])
        op("scalar", lambda e: [e.activation(Lc[:], dmx[:], AF.Ln)],
           reads=["dmx"], writes=["Lc"])
        op("vector", lambda e: [e.tensor_scalar(nLc[:], Lc[:], -1.0, None, ALU.mult)],
           reads=["Lc"], writes=["nLc"])

        import numpy as _np
        _eps = float(_np.float32(1e-5))

        def emit_quant_round(m, c):
            wTm = wT[m]
            sg = stageQ[c % 2]
            skey = f"stageQ{c % 2}"
            op("sync", lambda e, m=m, c=c, sg=sg: [
                e.dma_start(sg[:], w_h[m][c * 128:(c + 1) * 128, :])],
                writes=[skey], kind="dma")
            qta = sA[:, 0:D]
            qtb = sB[:, 0:D]
            qwq = sC[:, 0:D]
            qwbf = qwb[:]
            op("scalar", lambda e, sg=sg, qta=qta: [
                e.activation(qta, sg[:], AF.Abs, accum_out=qa[:, 0:1])],
               reads=[skey], writes=["sA", "qa"])
            op("gpsimd", lambda e: [
                e.tensor_scalar(qs[:], qa[:], 1.0 / D, _eps, ALU.mult, ALU.max),
                e.tensor_scalar(qsh[:], qa[:], 0.5 / D, 0.5 * _eps, ALU.mult, ALU.max)],
               reads=["qa"], writes=["qs", "qsh"])
            op("scalar", lambda e, sg=sg, qtb=qtb: [
                e.activation(qtb, sg[:], AF.Sign)],
               reads=[skey], writes=["sB"])
            op("vector", lambda e, qwq=qwq, qta=qta: [
                e.tensor_scalar(qwq, qta, qsh[:, 0:1], qs[:, 0:1],
                                ALU.is_ge, ALU.mult)],
               reads=["sA", "qsh", "qs"], writes=["sC"])
            op("vector", lambda e, qwbf=qwbf, qwq=qwq, qtb=qtb: [
                e.tensor_tensor(qwbf, qwq, qtb, ALU.mult)],
               reads=["sC", "sB"], writes=["qwb"])
            for q in range(2):
                bank = pb[q]

                def trq(e, q=q, bank=bank, qwbf=qwbf):
                    return [e.matmul(bank[:, i2 * 128:(i2 + 1) * 128],
                                     qwbf[:, (4 * q + i2) * 128:(4 * q + i2 + 1) * 128],
                                     ident[:], start=True, stop=True)
                            for i2 in range(4)]
                op("tensor", trq, reads=["qwb", "ident"], writes=[f"p{q}"])

                def evq(e, q=q, bank=bank, m=m, c=c, wTm=wTm):
                    ins = []
                    for i2 in range(4):
                        kc = 4 * q + i2
                        dst = wTm[:, kc * D + c * 128: kc * D + (c + 1) * 128]
                        src = bank[:, i2 * 128:(i2 + 1) * 128]
                        if m == "wo":
                            ins.append(e.tensor_scalar(dst, src, lnw[:, kc:kc + 1], None, ALU.mult))
                        else:
                            ins.append(e.tensor_copy(dst, src))
                    return ins
                op("vector", evq, reads=[f"p{q}", "lnw"], writes=[m + "T"])

        def quantize_matrix(m):
            for c in range(8):
                emit_quant_round(m, c)

        quantize_matrix("wk")
        quantize_matrix("wv")

        def emit_prep_expert(ex):
            kb = ebuf[ex % 2]
            bkey = f"ebuf{ex % 2}"

            def ldcast(e, ex=ex, kb=kb):
                return [e.dma_start(kb[:, hh * D:(hh + 1) * D],
                                    ewt_h[ex, hh * 128:(hh + 1) * 128, :])
                        for hh in range(8)]
            op("gpsimd", ldcast, writes=[bkey], kind="dma", ndma=8)
            op("gpsimd", lambda e, ex=ex, kb=kb: [e.dma_start(escr_h[ex], kb[:])],
               reads=[bkey], writes=[f"escr{ex}"], kind="dma")

        MAGICBIG = None

        def emit_ln_load(st, pp, x_hbm, prefix):
            xs = xb[pp]
            io_hbm = iop_h if prefix else iom_h

            def ldx(e, st=st, xs=xs, x_hbm=x_hbm):
                return [e.dma_start(xs[:, j * D:(j + 1) * D],
                                    x_hbm[st * ST + j * 128: st * ST + (j + 1) * 128, :])
                        for j in range(NSUB)]
            op("sync", ldx, writes=[f"x{pp}"], kind="dma", ndma=NSUB)
            op("sync", lambda e, st=st: [e.dma_start(iot[pp][:], io_hbm[:, st * ST:(st + 1) * ST])],
               writes=[f"iot{pp}"], kind="dma")

        def emit_ln_math(st, pp, x_hbm, prefix):
            xs = xb[pp]
            xnt = xn_tm[pp]
            if prefix:
                op("vector", lambda e, xs=xs: [
                    e.reduce_sum(t1c[:, j:j + 1], xs[:, j * D:(j + 1) * D], axis=AX.X)
                    for j in range(NSUB)],
                    reads=[f"x{pp}"], writes=["t1c"])
            else:
                op("scalar", lambda e, xs=xs: [
                    e.activation(sA[:, j * D:(j + 1) * D], xs[:, j * D:(j + 1) * D],
                                 AF.Identity, accum_out=t1c[:, j:j + 1])
                    for j in range(NSUB)],
                    reads=[f"x{pp}"], writes=["sA", "t1c"])
            op("scalar", lambda e: [
                e.mul(mcol[:, j:j + 1], t1c[:, j:j + 1], 1.0 / D)
                for j in range(NSUB)],
                reads=["t1c"], writes=["mcol"])
            op("scalar", lambda e, xs=xs: [
                e.activation(sA[:, j * D:(j + 1) * D], xs[:, j * D:(j + 1) * D],
                             AF.Square, accum_out=sq2[:, j:j + 1])
                for j in range(NSUB)],
                reads=[f"x{pp}"], writes=["sA", "sq2"])
            op("scalar", lambda e: [
                e.mul(s2c[:, j:j + 1], sq2[:, j:j + 1], 1.0 / D)
                for j in range(NSUB)],
                reads=["sq2"], writes=["s2c"])
            op("vector", lambda e: [
                e.scalar_tensor_tensor(nvar[:, j:j + 1], mcol[:, j:j + 1],
                                       mcol[:, j:j + 1], s2c[:, j:j + 1],
                                       ALU.mult, ALU.subtract)
                for j in range(NSUB)],
                reads=["mcol", "s2c"], writes=["nvar"])
            op("scalar", lambda e: [
                e.activation(lnc[:, j:j + 1], nvar[:, j:j + 1], AF.Ln,
                             bias=cst[:, 0:1], scale=-1.0)
                for j in range(NSUB)],
                reads=["nvar", "cst"], writes=["lncol"])
            op("scalar", lambda e: [
                e.activation(acol[:, j:j + 1], lnc[:, j:j + 1], AF.Exp, scale=-0.5)
                for j in range(NSUB)],
                reads=["lncol"], writes=["acol"])
            if prefix:
                op("vector", lambda e, xs=xs, xnt=xnt: [
                    e.tensor_scalar(xnt[:, j * D:(j + 1) * D], xs[:, j * D:(j + 1) * D],
                                    mcol[:, j:j + 1], acol[:, j:j + 1],
                                    ALU.subtract, ALU.mult)
                    for j in range(NSUB)],
                    reads=[f"x{pp}", "mcol", "acol"], writes=[f"xn_tm{pp}"])
            else:
                op("vector", lambda e: [
                    e.tensor_scalar(acol2[:, 0:NSUB], acol[:, 0:NSUB], 1.0,
                                    None, ALU.mult)],
                   reads=["acol"], writes=["acol2"])
                op("vector", lambda e: [
                    e.scalar_tensor_tensor(nbc[:, 0:NSUB], mcol[:, 0:NSUB], -1.0,
                                           acol[:, 0:NSUB], ALU.mult, ALU.mult)],
                   reads=["mcol", "acol"], writes=["nbc"])
                op("scalar", lambda e, xs=xs, xnt=xnt: [
                    e.activation(xnt[:, j * D:(j + 1) * D], xs[:, j * D:(j + 1) * D],
                                 AF.Identity, bias=nbc[:, j:j + 1],
                                 scale=acol2[:, j:j + 1])
                    for j in range(NSUB)],
                    reads=[f"x{pp}", "nbc", "acol2"], writes=[f"xn_tm{pp}"])

        def emit_ln(st, pp, x_hbm, prefix):
            emit_ln_load(st, pp, x_hbm, prefix)
            emit_ln_math(st, pp, x_hbm, prefix)

        def emit_transposes(pp):
            xnt, xnf = xn_tm[pp], xn_fm[pp]
            for g in range(4):
                bank = pb[g % 2]

                def trg(e, g=g, bank=bank, xnt=xnt):
                    ins = []
                    for u2 in range(4):
                        c = 2 * g + u2 // 2
                        j = u2 % 2
                        ins.append(e.matmul(bank[:, u2 * 128:(u2 + 1) * 128],
                                            xnt[:, j * D + c * 128: j * D + (c + 1) * 128],
                                            ident[:], start=True, stop=True))
                    return ins
                op("tensor", trg, reads=[f"xn_tm{pp}", "ident"], writes=[f"p{g % 2}"])
                op("scalar", lambda e, g=g, bank=bank, xnf=xnf: [
                    e.copy(xnf[:, (2 * g) * ST:(2 * g + 2) * ST], bank[:, 0:512])],
                    reads=[f"p{g % 2}"], writes=[f"xn_fm{pp}"])

        def emit_gate(pp, dbg=False):
            xnf = xn_fm[pp]
            ohx = ohb[pp]
            for j in range(NSUB):
                gb = pb[2 + j]
                gbk = f"p{2 + j}"

                def gmm(e, j=j, xnf=xnf, gb=gb):
                    ins = []
                    for kc in range(NCH):
                        ins.append(e.matmul(gb[:, j * 8:(j + 1) * 8],
                                            xnf[:, kc * ST + j * 128: kc * ST + (j + 1) * 128],
                                            gws[:, kc * 8:(kc + 1) * 8],
                                            start=(kc == 0), stop=(kc == NCH - 1)))
                    return ins
                op("tensor", gmm, reads=[f"xn_fm{pp}", "gws"], writes=[gbk])
                op("scalar", lambda e, j=j, gb=gb: [
                    e.copy(scb[:, j * 8:(j + 1) * 8], gb[:, j * 8:(j + 1) * 8])],
                    reads=[gbk], writes=["scb"])
            for j in range(NSUB):
                op("vector", lambda e, j=j: [
                    e.reduce_max(gmx[:, j:j + 1], scb[:, j * 8:(j + 1) * 8], axis=AX.X)],
                    reads=["scb"], writes=["gmx"])
            op("scalar", lambda e: [
                e.mul(gmx2[:], gmx[:], -1.0)],
                reads=["gmx"], writes=["gmx2"])
            for j, (ta, tb) in enumerate(((t8a, t8b), (t8c, t8d))):
                ka, kb = f"t8{j}a", f"t8{j}b"
                op("scalar", lambda e, j=j, ta=ta: [
                    e.activation(ta[:], scb[:, j * 8:(j + 1) * 8], AF.Identity,
                                 bias=gmx2[:, j:j + 1])],
                    reads=["scb", "gmx2"], writes=[ka])
                op("scalar", lambda e, ta=ta, tb=tb: [
                    e.activation(tb[:], ta[:], AF.Sign)],
                    reads=[ka], writes=[kb])
                op("scalar", lambda e, ta=ta, tb=tb: [
                    e.activation(ta[:], tb[:], AF.Abs)],
                    reads=[kb], writes=[ka])
                op("scalar", lambda e, ta=ta, tb=tb: [
                    e.activation(tb[:], ta[:], AF.Identity, bias=cst[:, 2:3], scale=-1.0)],
                    reads=[ka, "cst"], writes=[kb])
                op("vector", lambda e, ta=ta, tb=tb: [
                    e.tensor_tensor(ta[:], tb[:], io8r[:], ALU.mult)],
                    reads=[kb, "io8r"], writes=[ka])
            op("scalar", lambda e: [
                e.copy(m3g[:, 0:8], t8a[:]),
                e.copy(m3g[:, 8:16], t8c[:])],
                reads=["t80a", "t81a"], writes=["m3g"])
            for j in range(NSUB):
                op("vector", lambda e, j=j: [
                    e.reduce_max(mx3[:, j:j + 1], m3g[:, j * 8:(j + 1) * 8], axis=AX.X)],
                    reads=["m3g"], writes=["mx3"])
            op("scalar", lambda e: [
                e.activation(idxc[:, j:j + 1], mx3[:, j:j + 1], AF.Identity,
                             bias=cst[:, 3:4])
                for j in range(NSUB)],
                reads=["mx3", "cst"], writes=["idxc"])
            for j, (ta, tb) in enumerate(((t8a, t8b), (t8c, t8d))):
                ka, kb = f"t8{j}a", f"t8{j}b"
                op("scalar", lambda e, j=j, ta=ta: [
                    e.activation(ta[:], io8[:], AF.Identity, bias=idxc[:, j:j + 1])],
                    reads=["io8", "idxc"], writes=[ka])
                op("scalar", lambda e, ta=ta, tb=tb: [
                    e.activation(tb[:], ta[:], AF.Sign)],
                    reads=[ka], writes=[kb])
                op("scalar", lambda e, ta=ta, tb=tb: [
                    e.activation(ta[:], tb[:], AF.Abs)],
                    reads=[kb], writes=[ka])
                op("scalar", lambda e, j=j, ta=ta, ohx=ohx: [
                    e.activation(ohx[:, j * 8:(j + 1) * 8], ta[:], AF.Identity,
                                 bias=cst[:, 2:3], scale=-1.0)],
                    reads=[ka, "cst"], writes=[f"ohb{pp}"])
            if dbg:
                tap("idxc_g", idxc[:], ["idxc"])

        def emit_proj(pp, m, banks, bankkeys):
            xnf = xn_fm[pp]
            for i in range(4):
                bank = banks[i]

                def pmm(e, i=i, bank=bank, m=m, xnf=xnf):
                    ins = []
                    for ce in (2 * i, 2 * i + 1):
                        for kc in range(NCH):
                            ins.append(e.matmul(
                                bank[:, (ce % 2) * ST:(ce % 2) * ST + ST],
                                wT[m][:, kc * D + ce * 128: kc * D + (ce + 1) * 128],
                                xnf[:, kc * ST:(kc + 1) * ST],
                                start=(kc == 0), stop=(kc == NCH - 1)))
                    return ins
                op("tensor", pmm, reads=[m + "T", f"xn_fm{pp}"], writes=[bankkeys[i]])

        def emit_scan_ue(st, pp):
            iost = iot[pp]
            op("vector", lambda e, iost=iost: [
                e.tensor_scalar(sA[:, c * ST:(c + 1) * ST], iost[:],
                                nLc[:, c:c + 1], 23.025850929940457,
                                ALU.mult, ALU.min)
                for c in range(NCH)],
                reads=[f"iot{pp}", "nLc"], writes=["sA"])
            op("scalar", lambda e: [e.activation(sB[:], sA[:], AF.Exp)],
               reads=["sA"], writes=["sB"])

        def emit_scan_a(st, pp, with_scale, kb0=4):
            iost = iot[pp]
            op("vector", lambda e, kb0=kb0: [
                e.scalar_tensor_tensor(sC[:, 2 * i * ST:(2 * i + 2) * ST],
                                       sB[:, 2 * i * ST:(2 * i + 2) * ST],
                                       CLIP_INV, pb[kb0 + i][:, 0:512],
                                       ALU.min, ALU.mult)
                for i in range(4)],
                reads=["sB"] + [f"p{kb0 + i}" for i in range(4)], writes=["sC"])
            if with_scale:
                op("scalar", lambda e, iost=iost: [
                    e.activation(sA[:, c * ST:(c + 1) * ST], iost[:], AF.Exp,
                                 scale=Lc[:, c:c + 1])
                    for c in range(NCH)],
                    reads=[f"iot{pp}", "Lc"], writes=["sA"])

        def emit_scan_b(st, pp, prefix_st=None):
            if prefix_st is not None:
                op("vector", lambda e, k=prefix_st: [
                    e.scalar_tensor_tensor(sD[:, c * ST:(c + 1) * ST],
                                           pb[c // 2][:, (c % 2) * ST:(c % 2) * ST + ST],
                                           1.0,
                                           sC[:, c * ST:(c + 1) * ST],
                                           ALU.mult, ALU.mult,
                                           accum_out=tcols[:, k * 8 + c:k * 8 + c + 1])
                    for c in range(NCH)],
                    reads=["p0", "p1", "p2", "p3", "sC"], writes=["sD", "tcols"])
                return
            op("vector", lambda e: [
                e.tensor_tensor(sD[:, 2 * i * ST:(2 * i + 2) * ST],
                                pb[i][:, 0:512],
                                sC[:, 2 * i * ST:(2 * i + 2) * ST], ALU.mult)
                for i in range(4)],
                reads=["p0", "p1", "p2", "p3", "sC"], writes=["sD"])
            op("vector", lambda e: [
                e.tensor_tensor_scan(sC[:, c * ST:(c + 1) * ST],
                                     sD[:, c * ST:(c + 1) * ST], z256[:],
                                     carry[:, c:c + 1], ALU.add, ALU.add)
                for c in range(NCH)],
                reads=["sD", "z256", "carry"], writes=["sC"])
            op("scalar", lambda e: [
                e.copy(carry[:, c:c + 1], sC[:, c * ST + ST - 1: c * ST + ST])
                for c in range(NCH)],
                reads=["sC"], writes=["carry"])

        def emit_y(rb0=4):
            op("scalar", lambda e, rb0=rb0: [
                e.activation(sD[:, 2 * i * ST:(2 * i + 2) * ST],
                             pb[rb0 + i][:, 0:512], AF.Sigmoid)
                for i in range(4)],
                reads=[f"p{rb0 + i}" for i in range(4)], writes=["sD"])
            op("vector", lambda e: [e.tensor_tensor(sB[:], sD[:], sA[:], ALU.mult)],
               reads=["sD", "sA"], writes=["sB"])
            op("vector", lambda e: [e.tensor_tensor(ybf[:], sC[:], sB[:], ALU.mult)],
               reads=["sC", "sB"], writes=["ybf"])

        esched = []
        for _p in range(NST):
            _order = range(7, -1, -1) if _p % 2 == 0 else range(8)
            for _i, _ex in enumerate(_order):
                _buf = (_i + 1) % 2 if _p % 2 == 0 else _i % 2
                esched.append((_ex, _buf, _i >= 2))

        def emit_expert_load(g):
            if g >= len(esched):
                return
            ex, buf, needs = esched[g]
            if not needs:
                return
            op("gpsimd", lambda e, ex=ex, kbuf=ebuf[buf]: [e.dma_start(kbuf[:], escr_h[ex])],
               reads=[f"escr{ex}"], writes=[f"ebuf{buf}"], kind="dma")

        def emit_experts(st, pp, hooks=None):
            xnf = xn_fm[pp]
            for i in range(8):
                if hooks and i in hooks:
                    hooks[i]()
                ex, buf, _ = esched[st * 8 + i]
                kbuf = ebuf[buf]
                bkey = f"ebuf{buf}"
                for j in range(NSUB):
                    for nh in range(2):
                        bank = pb[4 + j * 2 + nh]
                        bk = f"p{4 + j * 2 + nh}"

                        def emm(e, j=j, nh=nh, bank=bank, kbuf=kbuf, xnf=xnf):
                            ins = []
                            for kc in range(NCH):
                                ins.append(e.matmul(
                                    bank[:, 0:512],
                                    xnf[:, kc * ST + j * 128: kc * ST + (j + 1) * 128],
                                    kbuf[:, kc * D + nh * 512: kc * D + nh * 512 + 512],
                                    start=(kc == 0), stop=(kc == NCH - 1)))
                            return ins
                        op("tensor", emm, reads=[f"xn_fm{pp}", bkey], writes=[bk])
                        dst = moe[(i + 1) % 2][:, j * D + nh * 512: j * D + nh * 512 + 512]
                        ohcol = ohb[pp][:, j * 8 + ex: j * 8 + ex + 1]
                        if i == 0:
                            op("vector", lambda e, dst=dst, bank=bank, ohcol=ohcol: [
                                e.tensor_scalar(dst, bank[:, 0:512], ohcol, None, ALU.mult)],
                                reads=[bk, f"ohb{pp}"], writes=["moe1"])
                        else:
                            src2 = moe[i % 2][:, j * D + nh * 512: j * D + nh * 512 + 512]
                            op("vector", lambda e, dst=dst, bank=bank, ohcol=ohcol, src2=src2: [
                                e.scalar_tensor_tensor(dst, bank[:, 0:512], ohcol, src2,
                                                       ALU.mult, ALU.add)],
                                reads=[bk, f"ohb{pp}", f"moe{i % 2}"], writes=[f"moe{(i + 1) % 2}"])
                emit_expert_load(st * 8 + i + 2)
            if hooks and 8 in hooks:
                hooks[8]()

        def emit_rms_wo_out(st, pp):
            xs = xb[pp]
            for j in range(NSUB):
                def s2mm(e, j=j):
                    ins = []
                    for kc in range(NCH):
                        ins.append(e.matmul(pb[5][:, 16 + j:17 + j],
                                            y2g[:, kc * ST + j * 128: kc * ST + (j + 1) * 128],
                                            onesb[:],
                                            start=(kc == 0), stop=(kc == NCH - 1)))
                    return ins
                op("tensor", s2mm, reads=["y2g", "onesb"], writes=["p5"])
                op("scalar", lambda e, j=j: [
                    e.activation(rsl[:, j:j + 1], pb[5][:, 16 + j:17 + j], AF.Ln,
                                 bias=cst[:, 1:2], scale=1.0 / D)],
                    reads=["p5", "cst"], writes=["rsl"])
                op("scalar", lambda e, j=j: [
                    e.activation(rsv[:, j:j + 1], rsl[:, j:j + 1], AF.Exp, scale=-0.5)],
                    reads=["rsl"], writes=["rsv"])
            op("vector", lambda e: [
                e.tensor_scalar(rsv2[:, 0:NSUB], rsv[:, 0:NSUB], 1.0, None, ALU.mult)],
               reads=["rsv"], writes=["rsv2"])
            wob = {0: (4, 6), 1: (7, 5)}
            for j in range(NSUB):
                for nh in range(2):
                    bank = pb[wob[j][nh]]
                    bk = f"p{wob[j][nh]}"

                    def womm(e, j=j, nh=nh, bank=bank):
                        ins = []
                        for kc in range(NCH):
                            ins.append(e.matmul(
                                bank[:, 0:512],
                                ybf[:, kc * ST + j * 128: kc * ST + (j + 1) * 128],
                                wT["wo"][:, kc * D + nh * 512: kc * D + nh * 512 + 512],
                                start=(kc == 0), stop=(kc == NCH - 1)))
                        return ins
                    op("tensor", womm, reads=["ybf", "woT"], writes=[bk])
                    op("scalar", lambda e, j=j, nh=nh, bank=bank: [
                        e.activation(s_sb[:, j * D + nh * 512: j * D + nh * 512 + 512],
                                     bank[:, 0:512], AF.Identity,
                                     scale=rsv2[:, j:j + 1])],
                        reads=[bk, "rsv2"], writes=["s_sb"])
        def emit_gpre(st, pp):
            for j in range(NSUB):
                op("vector", lambda e, j=j, pp=pp: [
                    e.tensor_tensor(stageP[j][:], moe[0][:, j * D:(j + 1) * D],
                                    xn_tm[pp][:, j * D:(j + 1) * D], ALU.add)],
                    reads=["moe0", f"xn_tm{pp}"], writes=[f"stageP{j}"])

        def emit_xt_load(st):
            def ldxt(e, st=st):
                return [e.dma_start(stageQ[j][:],
                                    xmain_h[st * ST + j * 128: st * ST + (j + 1) * 128, :])
                        for j in range(NSUB)]
            op("sync", ldxt, writes=["stageQ0", "stageQ1"], kind="dma", ndma=NSUB)

        def emit_tail2a(st, pp):
            op("scalar", lambda e: [
                e.activation(sD[:, 0:D], stageP[0][:], AF.Sigmoid),
                e.activation(sD[:, D:2 * D], stageP[1][:], AF.Sigmoid)],
               reads=["stageP0", "stageP1"], writes=["sD"])
            for j in range(NSUB):
                op("vector", lambda e, j=j: [
                    e.tensor_tensor(sB[:, j * D:(j + 1) * D], stageP[j][:],
                                    sD[:, j * D:(j + 1) * D], ALU.mult)],
                    reads=[f"stageP{j}", "sD"], writes=["sB"])

        def emit_tail2b(st, pp):
            for j in range(NSUB):
                op("vector", lambda e, j=j: [
                    e.tensor_tensor(sD[:, j * D:(j + 1) * D], s_sb[:, j * D:(j + 1) * D],
                                    sB[:, j * D:(j + 1) * D], ALU.mult)],
                    reads=["s_sb", "sB"], writes=["sD"])
                op("vector", lambda e, j=j: [
                    e.tensor_tensor(s_sb[:, j * D:(j + 1) * D], stageQ[j][:],
                                    sD[:, j * D:(j + 1) * D], ALU.add)],
                    reads=[f"stageQ{j}", "sD"], writes=["s_sb"])
                op("sync", lambda e, st=st, j=j: [
                    e.dma_start(out_h[st * ST + j * 128: st * ST + (j + 1) * 128, :],
                                s_sb[:, j * D:(j + 1) * D])],
                    reads=["s_sb"], writes=["outhbm"], kind="dma")

        def emit_tail2(st, pp):
            emit_tail2a(st, pp)
            emit_tail2b(st, pp)

        quantize_matrix("wk")
        quantize_matrix("wv")
        emit_ln(0, 0, xpre_h, prefix=True)
        emit_scan_ue(0, 0)
        for st in range(NST):
            pp = st % 2
            emit_transposes(pp)
            emit_proj(pp, "wk", [pb[4 + i] for i in range(4)], [f"p{4 + i}" for i in range(4)])
            emit_proj(pp, "wv", [pb[i] for i in range(4)], [f"p{i}" for i in range(4)])
            if st + 1 < NST:
                emit_ln(st + 1, (st + 1) % 2, xpre_h, prefix=True)
            emit_scan_a(st, pp, with_scale=False, kb0=4)
            if st + 1 < NST:
                emit_scan_ue(st + 1, (st + 1) % 2)
            emit_scan_b(st, pp, prefix_st=st)
            emit_quant_round("wr" if st < 4 else "wo", 2 * (st % 4))
            emit_quant_round("wr" if st < 4 else "wo", 2 * (st % 4) + 1)
            emit_prep_expert(st)
        op("vector", lambda e: [
            e.tensor_tensor(tc2[:], tcols[:, 0:32], tcols[:, 32:64], ALU.add)],
            reads=["tcols"], writes=["tc2"])
        op("vector", lambda e: [
            e.tensor_tensor(tcols[:, 0:16], tc2[:, 0:16], tc2[:, 16:32], ALU.add)],
            reads=["tc2"], writes=["tcols"])
        op("vector", lambda e: [
            e.tensor_tensor(carry[:], tcols[:, 0:8], tcols[:, 8:16], ALU.add)],
            reads=["tcols"], writes=["carry"])
        if debug_taps:
            for m in ("wk", "wv", "wr", "wo"):
                tap(m + "T", wT[m][:], [m + "T"])

        def front_A(st):
            pp = st % 2
            emit_transposes(pp)
            emit_gate(pp)

        def front_B(st):
            pp = st % 2
            emit_proj(pp, "wk", [pb[i] for i in range(4)], [f"p{i}" for i in range(4)])
            emit_scan_a(st, pp, with_scale=True, kb0=0)

        def front_C(st):
            pp = st % 2
            emit_proj(pp, "wv", [pb[i] for i in range(4)], [f"p{i}" for i in range(4)])
            emit_scan_b(st, pp)

        def front_D(st):
            pp = st % 2
            emit_proj(pp, "wr", [pb[i] for i in range(4)], [f"p{i}" for i in range(4)])

        emit_ln(0, 0, xmain_h, prefix=False)
        emit_scan_ue(0, 0)
        front_A(0)
        front_B(0)
        front_C(0)
        front_D(0)
        def mk_hooks(st):
            pp0 = (st - 1) % 2

            def h1():
                emit_y(rb0=0)
                if st > 0:
                    emit_gpre(st - 1, pp0)
                if st + 1 < NST:
                    emit_ln_load(st + 1, (st + 1) % 2, xmain_h, prefix=False)

            def h2():
                if st + 1 < NST:
                    emit_ln_math(st + 1, (st + 1) % 2, xmain_h, prefix=False)
                    emit_scan_ue(st + 1, (st + 1) % 2)

            def h3():
                if st + 1 < NST:
                    front_A(st + 1)

            def h4():
                if st > 0:
                    emit_tail2a(st - 1, pp0)

            def h5():
                if st + 1 < NST:
                    front_B(st + 1)
                if st > 0:
                    emit_tail2b(st - 1, pp0)
                emit_xt_load(st)
                op("scalar", lambda e: [e.activation(y2g[:], ybf[:], AF.Square)],
                   reads=["ybf"], writes=["y2g"])

            def h6():
                if st + 1 < NST:
                    front_C(st + 1)

            def h8():
                if st + 1 < NST:
                    front_D(st + 1)
            return {1: h1, 2: h2, 3: h3, 4: h4, 5: h5, 6: h6, 8: h8}

        for st in range(NST):
            pp = st % 2
            emit_experts(st, pp, mk_hooks(st))
            emit_rms_wo_out(st, pp)
        emit_gpre(NST - 1, (NST - 1) % 2)
        emit_tail2(NST - 1, (NST - 1) % 2)

        P.finalize()
        for k in P.dma_semkeys:
            sems[k] = stack.enter_context(nc.semaphore("sd_" + k))
        block = stack.enter_context(nc.Block())

        @block.sync
        def _(sync):
            P.emit("sync", sync, sems)
            sync.wait_ge(sems["outhbm"], P.totals["outhbm"])
            if debug_taps and "taphbm" in sems:
                sync.wait_ge(sems["taphbm"], P.totals["taphbm"])

        @block.gpsimd
        def _(gpsimd):
            P.emit("gpsimd", gpsimd, sems)

        @block.vector
        def _(vector):
            P.emit("vector", vector, sems)

        @block.scalar
        def _(scalar):
            P.emit("scalar", scalar, sems)

        @block.tensor
        def _(tensor):
            P.emit("tensor", tensor, sems)

    return nc


_NC_CACHE = {}


def _get_nc():
    if "nc" not in _NC_CACHE:
        _NC_CACHE["nc"] = build_nc()
    return _NC_CACHE["nc"]


def kernel(L0_attn_x, L0_attn_Wr, L0_attn_Wk, L0_attn_Wv, L0_attn_Wo,
           L0_attn_decay, L0_attn_ln_w, L0_ffn_gate_w, L0_ffn_expert_w):
    x = np.ascontiguousarray(np.asarray(L0_attn_x, dtype=np.float32))
    Wr = np.ascontiguousarray(np.asarray(L0_attn_Wr, dtype=np.float32))
    Wk = np.ascontiguousarray(np.asarray(L0_attn_Wk, dtype=np.float32))
    Wv = np.ascontiguousarray(np.asarray(L0_attn_Wv, dtype=np.float32))
    Wo = np.ascontiguousarray(np.asarray(L0_attn_Wo, dtype=np.float32))
    decay = np.asarray(L0_attn_decay, dtype=np.float32)
    ln_w = np.asarray(L0_attn_ln_w, dtype=np.float32)
    gate_w = np.asarray(L0_ffn_gate_w, dtype=np.float32)
    expert_w = np.asarray(L0_ffn_expert_w, dtype=np.float32)

    ident = np.eye(128, dtype=np.float32).astype(ml_dtypes.bfloat16)
    io8 = np.broadcast_to(np.arange(8, dtype=np.float32), (128, 8)).copy()
    io8r = (8.0 - io8).astype(np.float32)
    dec8 = np.ascontiguousarray(decay.reshape(8, 128).T)
    lnw8 = np.ascontiguousarray(ln_w.reshape(8, 128).T)
    gwt = np.ascontiguousarray(gate_w.T)
    ewt = np.ascontiguousarray(expert_w.transpose(0, 2, 1))
    ar = np.arange(HALF, dtype=np.float32)
    io_lo = np.broadcast_to(ar, (128, HALF)).copy()
    io_hi = np.broadcast_to(ar + HALF, (128, HALF)).copy()
    zeros_pre = np.zeros((HALF, D), dtype=np.float32)

    in_maps = []
    for c in range(8):
        b, half = c // 2, c % 2
        in_maps.append({
            "xmain": np.ascontiguousarray(x[b, half * HALF:(half + 1) * HALF]),
            "xpre": (np.ascontiguousarray(x[b, 0:HALF]) if half else zeros_pre),
            "wk": Wk, "wv": Wv, "wr": Wr, "wo": Wo,
            "ewt": ewt, "gwt": gwt,
            "dec8": dec8, "lnw8": lnw8,
            "iomain": io_hi if half else io_lo,
            "iopre": io_lo,
            "ident": ident, "io8": io8, "io8r": io8r,
        })

    nc = _get_nc()
    res = run_bass_kernel_spmd(nc, in_maps, core_ids=list(range(8)))
    out = np.empty((B, T, D), dtype=np.float32)
    for c in range(8):
        b, half = c // 2, c % 2
        out[b, half * HALF:(half + 1) * HALF] = res.results[c]["out"]
    return out


# version 39
# speedup vs baseline: 1.0613x; 1.0065x over previous
import numpy as np
import ml_dtypes
import concourse.bass as bass
import concourse.mybir as mybir
from concourse.bass_utils import run_bass_kernel_spmd
from contextlib import ExitStack

F32 = mybir.dt.float32
BF16 = mybir.dt.bfloat16
AF = mybir.ActivationFunctionType
ALU = mybir.AluOpType
AX = mybir.AxisListType

B, T, D = 4, 4096, 1024
HALF = T // 2
ST = 256
NST = HALF // ST
NSUB = ST // 128
NCH = D // 128
CLIP_INV = 1.0e10
LN10E = 23.025850929940457

ENGINES = ("sync", "gpsimd", "vector", "scalar", "tensor")


class Op:
    __slots__ = ("engine", "fn", "deps", "kind", "ndma", "awaited", "cnt", "idx",
                 "semkey")

    def __init__(self, engine, fn, deps, kind, ndma, idx):
        self.engine = engine
        self.fn = fn
        self.deps = deps
        self.kind = kind
        self.ndma = ndma
        self.awaited = False
        self.cnt = 0
        self.idx = idx


class Prog:
    def __init__(self):
        self.ops = []
        self.writer = {}
        self.readers = {}

    def op(self, engine, fn, reads=(), writes=(), kind="comp", ndma=1):
        if kind == "dma":
            assert writes, "dma ops must name their destination buffer"
        deps = []
        for b in reads:
            w = self.writer.get(b)
            if w is not None:
                deps.append(w)
        for b in writes:
            deps.extend(self.readers.get(b, ()))
            w = self.writer.get(b)
            if w is not None:
                deps.append(w)
        seen = set()
        ud = []
        for d in deps:
            if id(d) not in seen:
                seen.add(id(d))
                ud.append(d)
        h = Op(engine, fn, ud, kind, ndma, len(self.ops))
        h.semkey = writes[0] if kind == "dma" else engine
        self.ops.append(h)
        for b in reads:
            self.readers.setdefault(b, []).append(h)
        for b in writes:
            self.writer[b] = h
            self.readers[b] = []
        return h

    def finalize(self):
        for h in self.ops:
            for d in h.deps:
                d.awaited = True
        cnt = {e: 0 for e in ENGINES}
        for h in self.ops:
            if h.kind == "dma":
                cnt.setdefault(h.semkey, 0)
                cnt[h.semkey] += 16 * h.ndma
            elif h.awaited:
                cnt[h.engine] += 1
            h.cnt = cnt[h.semkey]
        self.totals = cnt
        self.dma_semkeys = sorted(k for k in cnt if k not in ENGINES)

    def emit(self, eng_name, eng, sems):
        last = {}
        for h in self.ops:
            if h.engine != eng_name:
                continue
            for d in h.deps:
                if (h.engine == "tensor" and d.engine == "tensor"
                        and d.kind != "dma"):
                    continue
                if not d.awaited or d.cnt <= last.get(d.semkey, 0):
                    continue
                eng.wait_ge(sems[d.semkey], d.cnt)
                last[d.semkey] = d.cnt
            insts = h.fn(eng)
            if h.kind == "dma":
                for i in insts:
                    i.then_inc(sems[h.semkey], 16)
            elif h.awaited:
                insts[-1].then_inc(sems[h.engine], 1)


def build_nc(debug_taps=False):
    nc = bass.Bass("TRN2", target_bir_lowering=False)

    xmain_h = nc.dram_tensor("xmain", [HALF, D], F32, kind="ExternalInput")
    xpre_h = nc.dram_tensor("xpre", [HALF, D], F32, kind="ExternalInput")
    w_h = {m: nc.dram_tensor(m, [D, D], F32, kind="ExternalInput")
           for m in ("wk", "wv", "wr", "wo")}
    ewt_h = nc.dram_tensor("ewt", [8, D, D], F32, kind="ExternalInput")
    gwt_h = nc.dram_tensor("gwt", [D, 8], F32, kind="ExternalInput")
    dec_h = nc.dram_tensor("dec8", [128, 8], F32, kind="ExternalInput")
    lnw_h = nc.dram_tensor("lnw8", [128, 8], F32, kind="ExternalInput")
    iom_h = nc.dram_tensor("iomain", [128, HALF], F32, kind="ExternalInput")
    iop_h = nc.dram_tensor("iopre", [128, HALF], F32, kind="ExternalInput")
    id_h = nc.dram_tensor("ident", [128, 128], BF16, kind="ExternalInput")
    io8_h = nc.dram_tensor("io8", [128, 8], F32, kind="ExternalInput")
    io8r_h = nc.dram_tensor("io8r", [128, 8], F32, kind="ExternalInput")
    out_h = nc.dram_tensor("out", [HALF, D], F32, kind="ExternalOutput")
    escr_h = nc.dram_tensor("escr", [8, 128, NCH * D], BF16, kind="Internal")

    P = Prog()
    tap_handles = {}

    with ExitStack() as stack:
        def sb(name, shape, dt=F32):
            return stack.enter_context(nc.sbuf_tensor(name, shape, dt))

        wT = {m: sb(m + "T", [128, NCH * D], BF16) for m in ("wk", "wv", "wr", "wo")}
        ebuf = [sb(f"ebuf{i}", [128, NCH * D], BF16) for i in range(2)]
        gws = sb("gws", [128, NCH * 8], BF16)
        gwsf = sb("gwsf", [128, NCH * 8], F32)
        xb = [sb(f"xb{i}", [128, NSUB * D]) for i in range(2)]
        iot = [sb(f"iot{i}", [128, ST]) for i in range(2)]
        xn_tm = [sb(f"xn_tm{i}", [128, NSUB * D], BF16) for i in range(2)]
        xn_fm = [sb(f"xn_fm{i}", [128, NCH * ST], BF16) for i in range(2)]
        ybf = sb("ybf", [128, NCH * ST], BF16)
        y2g = sb("y2g", [128, NCH * ST], BF16)
        moe = [sb(f"moe{i}", [128, NSUB * D], BF16) for i in range(2)]
        sA = sb("sA", [128, NCH * ST])
        sB = sb("sB", [128, NCH * ST])
        sC = sb("sC", [128, NCH * ST])
        sD = sb("sD", [128, NCH * ST])
        s_sb = sb("s_sb", [128, NSUB * D])
        stageQ = [sb(f"stageQ{i}", [128, D]) for i in range(2)]
        stageP = [sb(f"stageP{i}", [128, D]) for i in range(2)]
        z256 = sb("z256", [128, ST])
        ident = sb("identb", [128, 128], BF16)
        onesb = sb("onesb", [128, 1], BF16)
        io8 = sb("io8b", [128, 8])
        io8r = sb("io8rb", [128, 8])
        decs = sb("decs", [128, 8])
        sgd = sb("sgd", [128, 8])
        dmx = sb("dmx", [128, 8])
        Lc = sb("Lc", [128, 8])
        nLc = sb("nLc", [128, 8])
        lnw = sb("lnwb", [128, 8])
        carry = sb("carryb", [128, NCH])
        tcols = sb("tcols", [128, NST * NCH])
        tc2 = sb("tc2", [128, 4 * NCH])
        scb = sb("scb", [128, 2 * 8])
        ohb = [sb(f"ohb{i}", [128, 2 * 8]) for i in range(2)]
        t8a = sb("t8a", [128, 8])
        t8b = sb("t8b", [128, 8])
        t8c = sb("t8c", [128, 8])
        t8d = sb("t8d", [128, 8])
        gmx = sb("gmx", [128, 2])
        gmx2 = sb("gmx2", [128, 2])
        m3g = sb("m3g", [128, 16])
        mx3 = sb("mx3", [128, 2])
        idxc = sb("idxc", [128, 2])
        mcol = sb("mcol", [128, 2])
        sq2 = sb("sq2", [128, 2])
        t1c = sb("t1c", [128, 2])
        s2c = sb("s2c", [128, 2])
        nvar = sb("nvar", [128, 2])
        lnc = sb("lncol", [128, 2])
        acol = sb("acol", [128, 2])
        acol2 = sb("acol2", [128, 2])
        nbc = sb("nbc", [128, 2])
        rsv = sb("rsv", [128, 2])
        rsv2 = sb("rsv2", [128, 2])
        rsl = sb("rsl", [128, 2])
        cst = sb("cstb", [128, 4])
        qa = sb("qa", [128, 1])
        qs = sb("qs", [128, 1])
        qsh = sb("qsh", [128, 1])
        qnsh = sb("qnsh", [128, 1])
        qwb = sb("qwb", [128, D], BF16)

        pb = [stack.enter_context(nc.psum_tensor(f"pb{i}", [128, 512], F32))
              for i in range(8)]
        sems = {e: stack.enter_context(nc.semaphore("sem_" + e)) for e in ENGINES}
        block = None

        op = P.op

        def tap(name, ap, readkeys):
            if not debug_taps:
                return
            th = nc.dram_tensor("tap_" + name, list(ap.shape), ap.dtype,
                                kind="ExternalOutput")
            tap_handles[name] = th
            op("sync", lambda e, th=th, ap=ap: [e.dma_start(th[:], ap)],
               reads=list(readkeys), writes=["taphbm"], kind="dma")

        op("sync", lambda e: [e.dma_start(ident[:], id_h[:])], writes=["ident"], kind="dma")
        op("sync", lambda e: [e.dma_start(io8[:], io8_h[:])], writes=["io8"], kind="dma")
        op("sync", lambda e: [e.dma_start(io8r[:], io8r_h[:])], writes=["io8r"], kind="dma")
        op("sync", lambda e: [e.dma_start(decs[:], dec_h[:])], writes=["decs"], kind="dma")
        op("sync", lambda e: [e.dma_start(lnw[:], lnw_h[:])], writes=["lnw"], kind="dma")
        op("sync", lambda e: [e.dma_start(gwsf[:, kc * 8:(kc + 1) * 8],
                                          gwt_h[kc * 128:(kc + 1) * 128, :])
                              for kc in range(NCH)],
           writes=["gwsf"], kind="dma", ndma=NCH)
        op("vector", lambda e: [e.memset(z256[:], 0.0)], writes=["z256"])
        op("vector", lambda e: [e.memset(onesb[:], 1.0)], writes=["onesb"])
        op("vector", lambda e: [e.memset(carry[:], 0.0)], writes=["carry"])
        op("vector", lambda e: [e.memset(cst[:, 0:1], 1e-5)], writes=["cst"])
        op("vector", lambda e: [e.memset(cst[:, 1:2], 1e-6)], writes=["cst"])
        op("vector", lambda e: [e.memset(cst[:, 2:3], 1.0)], writes=["cst"])
        op("vector", lambda e: [e.memset(cst[:, 3:4], -8.0)], writes=["cst"])
        op("vector", lambda e: [e.tensor_copy(gws[:], gwsf[:])], reads=["gwsf"], writes=["gws"])
        op("scalar", lambda e: [e.activation(sgd[:], decs[:], AF.Sigmoid)],
           reads=["decs"], writes=["sgd"])
        op("vector", lambda e: [e.tensor_scalar(dmx[:], sgd[:], 1e-7, None, ALU.max)],
           reads=["sgd"], writes=["dmx"])
        op("scalar", lambda e: [e.activation(Lc[:], dmx[:], AF.Ln)],
           reads=["dmx"], writes=["Lc"])
        op("vector", lambda e: [e.tensor_scalar(nLc[:], Lc[:], -1.0, None, ALU.mult)],
           reads=["Lc"], writes=["nLc"])

        import numpy as _np
        _eps = float(_np.float32(1e-5))

        def emit_quant_round(m, c):
            wTm = wT[m]
            sg = stageQ[c % 2]
            skey = f"stageQ{c % 2}"
            op("sync", lambda e, m=m, c=c, sg=sg: [
                e.dma_start(sg[:], w_h[m][c * 128:(c + 1) * 128, :])],
                writes=[skey], kind="dma")
            qta = sA[:, 0:D]
            qtb = sB[:, 0:D]
            qwq = sC[:, 0:D]
            qwbf = qwb[:]
            op("scalar", lambda e, sg=sg, qta=qta: [
                e.activation(qta, sg[:], AF.Abs, accum_out=qa[:, 0:1])],
               reads=[skey], writes=["sA", "qa"])
            op("gpsimd", lambda e: [
                e.tensor_scalar(qs[:], qa[:], 1.0 / D, _eps, ALU.mult, ALU.max),
                e.tensor_scalar(qsh[:], qa[:], 0.5 / D, 0.5 * _eps, ALU.mult, ALU.max)],
               reads=["qa"], writes=["qs", "qsh"])
            op("scalar", lambda e, sg=sg, qtb=qtb: [
                e.activation(qtb, sg[:], AF.Sign)],
               reads=[skey], writes=["sB"])
            op("vector", lambda e, qwq=qwq, qta=qta: [
                e.tensor_scalar(qwq, qta, qsh[:, 0:1], qs[:, 0:1],
                                ALU.is_ge, ALU.mult)],
               reads=["sA", "qsh", "qs"], writes=["sC"])
            op("vector", lambda e, qwbf=qwbf, qwq=qwq, qtb=qtb: [
                e.tensor_tensor(qwbf, qwq, qtb, ALU.mult)],
               reads=["sC", "sB"], writes=["qwb"])
            for q in range(2):
                bank = pb[q]

                def trq(e, q=q, bank=bank, qwbf=qwbf):
                    return [e.matmul(bank[:, i2 * 128:(i2 + 1) * 128],
                                     qwbf[:, (4 * q + i2) * 128:(4 * q + i2 + 1) * 128],
                                     ident[:], start=True, stop=True)
                            for i2 in range(4)]
                op("tensor", trq, reads=["qwb", "ident"], writes=[f"p{q}"])

                def evq(e, q=q, bank=bank, m=m, c=c, wTm=wTm):
                    ins = []
                    for i2 in range(4):
                        kc = 4 * q + i2
                        dst = wTm[:, kc * D + c * 128: kc * D + (c + 1) * 128]
                        src = bank[:, i2 * 128:(i2 + 1) * 128]
                        if m == "wo":
                            ins.append(e.tensor_scalar(dst, src, lnw[:, kc:kc + 1], None, ALU.mult))
                        else:
                            ins.append(e.tensor_copy(dst, src))
                    return ins
                op("vector", evq, reads=[f"p{q}", "lnw"], writes=[m + "T"])

        def quantize_matrix(m):
            for c in range(8):
                emit_quant_round(m, c)

        quantize_matrix("wk")
        quantize_matrix("wv")

        def emit_prep_expert(ex):
            kb = ebuf[ex % 2]
            bkey = f"ebuf{ex % 2}"

            def ldcast(e, ex=ex, kb=kb):
                return [e.dma_start(kb[:, hh * D:(hh + 1) * D],
                                    ewt_h[ex, hh * 128:(hh + 1) * 128, :])
                        for hh in range(8)]
            op("gpsimd", ldcast, writes=[bkey], kind="dma", ndma=8)
            op("gpsimd", lambda e, ex=ex, kb=kb: [e.dma_start(escr_h[ex], kb[:])],
               reads=[bkey], writes=[f"escr{ex}"], kind="dma")

        MAGICBIG = None

        def emit_ln_load(st, pp, x_hbm, prefix):
            xs = xb[pp]
            io_hbm = iop_h if prefix else iom_h

            def ldx(e, st=st, xs=xs, x_hbm=x_hbm):
                return [e.dma_start(xs[:, j * D:(j + 1) * D],
                                    x_hbm[st * ST + j * 128: st * ST + (j + 1) * 128, :])
                        for j in range(NSUB)]
            op("sync", ldx, writes=[f"x{pp}"], kind="dma", ndma=NSUB)
            op("sync", lambda e, st=st: [e.dma_start(iot[pp][:], io_hbm[:, st * ST:(st + 1) * ST])],
               writes=[f"iot{pp}"], kind="dma")

        def emit_ln_math(st, pp, x_hbm, prefix):
            xs = xb[pp]
            xnt = xn_tm[pp]
            if prefix:
                op("vector", lambda e, xs=xs: [
                    e.reduce_sum(t1c[:, j:j + 1], xs[:, j * D:(j + 1) * D], axis=AX.X)
                    for j in range(NSUB)],
                    reads=[f"x{pp}"], writes=["t1c"])
            else:
                op("scalar", lambda e, xs=xs: [
                    e.activation(sA[:, j * D:(j + 1) * D], xs[:, j * D:(j + 1) * D],
                                 AF.Identity, accum_out=t1c[:, j:j + 1])
                    for j in range(NSUB)],
                    reads=[f"x{pp}"], writes=["sA", "t1c"])
            op("scalar", lambda e: [
                e.mul(mcol[:, j:j + 1], t1c[:, j:j + 1], 1.0 / D)
                for j in range(NSUB)],
                reads=["t1c"], writes=["mcol"])
            op("scalar", lambda e, xs=xs: [
                e.activation(sA[:, j * D:(j + 1) * D], xs[:, j * D:(j + 1) * D],
                             AF.Square, accum_out=sq2[:, j:j + 1])
                for j in range(NSUB)],
                reads=[f"x{pp}"], writes=["sA", "sq2"])
            op("scalar", lambda e: [
                e.mul(s2c[:, j:j + 1], sq2[:, j:j + 1], 1.0 / D)
                for j in range(NSUB)],
                reads=["sq2"], writes=["s2c"])
            op("vector", lambda e: [
                e.scalar_tensor_tensor(nvar[:, j:j + 1], mcol[:, j:j + 1],
                                       mcol[:, j:j + 1], s2c[:, j:j + 1],
                                       ALU.mult, ALU.subtract)
                for j in range(NSUB)],
                reads=["mcol", "s2c"], writes=["nvar"])
            op("scalar", lambda e: [
                e.activation(lnc[:, j:j + 1], nvar[:, j:j + 1], AF.Ln,
                             bias=cst[:, 0:1], scale=-1.0)
                for j in range(NSUB)],
                reads=["nvar", "cst"], writes=["lncol"])
            op("scalar", lambda e: [
                e.activation(acol[:, j:j + 1], lnc[:, j:j + 1], AF.Exp, scale=-0.5)
                for j in range(NSUB)],
                reads=["lncol"], writes=["acol"])
            if prefix:
                op("vector", lambda e, xs=xs, xnt=xnt: [
                    e.tensor_scalar(xnt[:, j * D:(j + 1) * D], xs[:, j * D:(j + 1) * D],
                                    mcol[:, j:j + 1], acol[:, j:j + 1],
                                    ALU.subtract, ALU.mult)
                    for j in range(NSUB)],
                    reads=[f"x{pp}", "mcol", "acol"], writes=[f"xn_tm{pp}"])
            else:
                op("vector", lambda e: [
                    e.tensor_scalar(acol2[:, 0:NSUB], acol[:, 0:NSUB], 1.0,
                                    None, ALU.mult)],
                   reads=["acol"], writes=["acol2"])
                op("vector", lambda e: [
                    e.scalar_tensor_tensor(nbc[:, 0:NSUB], mcol[:, 0:NSUB], -1.0,
                                           acol[:, 0:NSUB], ALU.mult, ALU.mult)],
                   reads=["mcol", "acol"], writes=["nbc"])
                op("scalar", lambda e, xs=xs, xnt=xnt: [
                    e.activation(xnt[:, j * D:(j + 1) * D], xs[:, j * D:(j + 1) * D],
                                 AF.Identity, bias=nbc[:, j:j + 1],
                                 scale=acol2[:, j:j + 1])
                    for j in range(NSUB)],
                    reads=[f"x{pp}", "nbc", "acol2"], writes=[f"xn_tm{pp}"])

        def emit_ln(st, pp, x_hbm, prefix):
            emit_ln_load(st, pp, x_hbm, prefix)
            emit_ln_math(st, pp, x_hbm, prefix)

        def emit_transposes(pp):
            xnt, xnf = xn_tm[pp], xn_fm[pp]
            for g in range(4):
                bank = pb[g % 2]

                def trg(e, g=g, bank=bank, xnt=xnt):
                    ins = []
                    for u2 in range(4):
                        c = 2 * g + u2 // 2
                        j = u2 % 2
                        ins.append(e.matmul(bank[:, u2 * 128:(u2 + 1) * 128],
                                            xnt[:, j * D + c * 128: j * D + (c + 1) * 128],
                                            ident[:], start=True, stop=True))
                    return ins
                op("tensor", trg, reads=[f"xn_tm{pp}", "ident"], writes=[f"p{g % 2}"])
                op("scalar", lambda e, g=g, bank=bank, xnf=xnf: [
                    e.copy(xnf[:, (2 * g) * ST:(2 * g + 2) * ST], bank[:, 0:512])],
                    reads=[f"p{g % 2}"], writes=[f"xn_fm{pp}"])

        def emit_gate(pp, dbg=False):
            xnf = xn_fm[pp]
            ohx = ohb[pp]
            for j in range(NSUB):
                gb = pb[2 + j]
                gbk = f"p{2 + j}"

                def gmm(e, j=j, xnf=xnf, gb=gb):
                    ins = []
                    for kc in range(NCH):
                        ins.append(e.matmul(gb[:, j * 8:(j + 1) * 8],
                                            xnf[:, kc * ST + j * 128: kc * ST + (j + 1) * 128],
                                            gws[:, kc * 8:(kc + 1) * 8],
                                            start=(kc == 0), stop=(kc == NCH - 1)))
                    return ins
                op("tensor", gmm, reads=[f"xn_fm{pp}", "gws"], writes=[gbk])
                op("scalar", lambda e, j=j, gb=gb: [
                    e.copy(scb[:, j * 8:(j + 1) * 8], gb[:, j * 8:(j + 1) * 8])],
                    reads=[gbk], writes=["scb"])
            for j in range(NSUB):
                op("vector", lambda e, j=j: [
                    e.reduce_max(gmx[:, j:j + 1], scb[:, j * 8:(j + 1) * 8], axis=AX.X)],
                    reads=["scb"], writes=["gmx"])
            op("scalar", lambda e: [
                e.mul(gmx2[:], gmx[:], -1.0)],
                reads=["gmx"], writes=["gmx2"])
            for j, (ta, tb) in enumerate(((t8a, t8b), (t8c, t8d))):
                ka, kb = f"t8{j}a", f"t8{j}b"
                op("scalar", lambda e, j=j, ta=ta: [
                    e.activation(ta[:], scb[:, j * 8:(j + 1) * 8], AF.Identity,
                                 bias=gmx2[:, j:j + 1])],
                    reads=["scb", "gmx2"], writes=[ka])
                op("scalar", lambda e, ta=ta, tb=tb: [
                    e.activation(tb[:], ta[:], AF.Sign)],
                    reads=[ka], writes=[kb])
                op("scalar", lambda e, ta=ta, tb=tb: [
                    e.activation(ta[:], tb[:], AF.Abs)],
                    reads=[kb], writes=[ka])
                op("scalar", lambda e, ta=ta, tb=tb: [
                    e.activation(tb[:], ta[:], AF.Identity, bias=cst[:, 2:3], scale=-1.0)],
                    reads=[ka, "cst"], writes=[kb])
                op("vector", lambda e, ta=ta, tb=tb: [
                    e.tensor_tensor(ta[:], tb[:], io8r[:], ALU.mult)],
                    reads=[kb, "io8r"], writes=[ka])
            op("scalar", lambda e: [
                e.copy(m3g[:, 0:8], t8a[:]),
                e.copy(m3g[:, 8:16], t8c[:])],
                reads=["t80a", "t81a"], writes=["m3g"])
            for j in range(NSUB):
                op("vector", lambda e, j=j: [
                    e.reduce_max(mx3[:, j:j + 1], m3g[:, j * 8:(j + 1) * 8], axis=AX.X)],
                    reads=["m3g"], writes=["mx3"])
            op("scalar", lambda e: [
                e.activation(idxc[:, j:j + 1], mx3[:, j:j + 1], AF.Identity,
                             bias=cst[:, 3:4])
                for j in range(NSUB)],
                reads=["mx3", "cst"], writes=["idxc"])
            for j, (ta, tb) in enumerate(((t8a, t8b), (t8c, t8d))):
                ka, kb = f"t8{j}a", f"t8{j}b"
                op("scalar", lambda e, j=j, ta=ta: [
                    e.activation(ta[:], io8[:], AF.Identity, bias=idxc[:, j:j + 1])],
                    reads=["io8", "idxc"], writes=[ka])
                op("scalar", lambda e, ta=ta, tb=tb: [
                    e.activation(tb[:], ta[:], AF.Sign)],
                    reads=[ka], writes=[kb])
                op("scalar", lambda e, ta=ta, tb=tb: [
                    e.activation(ta[:], tb[:], AF.Abs)],
                    reads=[kb], writes=[ka])
                op("scalar", lambda e, j=j, ta=ta, ohx=ohx: [
                    e.activation(ohx[:, j * 8:(j + 1) * 8], ta[:], AF.Identity,
                                 bias=cst[:, 2:3], scale=-1.0)],
                    reads=[ka, "cst"], writes=[f"ohb{pp}"])
            if dbg:
                tap("idxc_g", idxc[:], ["idxc"])

        def emit_proj(pp, m, banks, bankkeys):
            xnf = xn_fm[pp]
            for i in range(4):
                bank = banks[i]

                def pmm(e, i=i, bank=bank, m=m, xnf=xnf):
                    ins = []
                    for ce in (2 * i, 2 * i + 1):
                        for kc in range(NCH):
                            ins.append(e.matmul(
                                bank[:, (ce % 2) * ST:(ce % 2) * ST + ST],
                                wT[m][:, kc * D + ce * 128: kc * D + (ce + 1) * 128],
                                xnf[:, kc * ST:(kc + 1) * ST],
                                start=(kc == 0), stop=(kc == NCH - 1)))
                    return ins
                op("tensor", pmm, reads=[m + "T", f"xn_fm{pp}"], writes=[bankkeys[i]])

        def emit_scan_ue(st, pp):
            iost = iot[pp]
            op("vector", lambda e, iost=iost: [
                e.tensor_scalar(sA[:, c * ST:(c + 1) * ST], iost[:],
                                nLc[:, c:c + 1], 23.025850929940457,
                                ALU.mult, ALU.min)
                for c in range(NCH)],
                reads=[f"iot{pp}", "nLc"], writes=["sA"])
            op("scalar", lambda e: [e.activation(sB[:], sA[:], AF.Exp)],
               reads=["sA"], writes=["sB"])

        def emit_scan_a(st, pp, with_scale, kb0=4):
            iost = iot[pp]
            op("vector", lambda e, kb0=kb0: [
                e.scalar_tensor_tensor(sC[:, 2 * i * ST:(2 * i + 2) * ST],
                                       sB[:, 2 * i * ST:(2 * i + 2) * ST],
                                       CLIP_INV, pb[kb0 + i][:, 0:512],
                                       ALU.min, ALU.mult)
                for i in range(4)],
                reads=["sB"] + [f"p{kb0 + i}" for i in range(4)], writes=["sC"])
            if with_scale:
                op("scalar", lambda e, iost=iost: [
                    e.activation(sA[:, c * ST:(c + 1) * ST], iost[:], AF.Exp,
                                 scale=Lc[:, c:c + 1])
                    for c in range(NCH)],
                    reads=[f"iot{pp}", "Lc"], writes=["sA"])

        def emit_scan_b(st, pp, prefix_st=None):
            if prefix_st is not None:
                op("vector", lambda e, k=prefix_st: [
                    e.scalar_tensor_tensor(sD[:, c * ST:(c + 1) * ST],
                                           pb[c // 2][:, (c % 2) * ST:(c % 2) * ST + ST],
                                           1.0,
                                           sC[:, c * ST:(c + 1) * ST],
                                           ALU.mult, ALU.mult,
                                           accum_out=tcols[:, k * 8 + c:k * 8 + c + 1])
                    for c in range(NCH)],
                    reads=["p0", "p1", "p2", "p3", "sC"], writes=["sD", "tcols"])
                return
            for bk0, c0 in ((0, 0), (2, 4)):
                op("vector", lambda e, bk0=bk0: [
                    e.tensor_tensor(sD[:, 2 * i * ST:(2 * i + 2) * ST],
                                    pb[i][:, 0:512],
                                    sC[:, 2 * i * ST:(2 * i + 2) * ST], ALU.mult)
                    for i in range(bk0, bk0 + 2)],
                    reads=[f"p{bk0}", f"p{bk0 + 1}", "sC"], writes=["sD"])
                op("vector", lambda e, c0=c0: [
                    e.tensor_tensor_scan(sC[:, c * ST:(c + 1) * ST],
                                         sD[:, c * ST:(c + 1) * ST], z256[:],
                                         carry[:, c:c + 1], ALU.add, ALU.add)
                    for c in range(c0, c0 + 4)],
                    reads=["sD", "z256", "carry"], writes=["sC"])
                op("scalar", lambda e, c0=c0: [
                    e.copy(carry[:, c:c + 1], sC[:, c * ST + ST - 1: c * ST + ST])
                    for c in range(c0, c0 + 4)],
                    reads=["sC"], writes=["carry"])

        def emit_y(rb0=4):
            op("scalar", lambda e, rb0=rb0: [
                e.activation(sD[:, 2 * i * ST:(2 * i + 2) * ST],
                             pb[rb0 + i][:, 0:512], AF.Sigmoid)
                for i in range(4)],
                reads=[f"p{rb0 + i}" for i in range(4)], writes=["sD"])
            op("vector", lambda e: [e.tensor_tensor(sB[:], sD[:], sA[:], ALU.mult)],
               reads=["sD", "sA"], writes=["sB"])
            op("vector", lambda e: [e.tensor_tensor(ybf[:], sC[:], sB[:], ALU.mult)],
               reads=["sC", "sB"], writes=["ybf"])

        esched = []
        for _p in range(NST):
            _order = range(7, -1, -1) if _p % 2 == 0 else range(8)
            for _i, _ex in enumerate(_order):
                _buf = (_i + 1) % 2 if _p % 2 == 0 else _i % 2
                esched.append((_ex, _buf, _i >= 2))

        def emit_expert_load(g):
            if g >= len(esched):
                return
            ex, buf, needs = esched[g]
            if not needs:
                return
            op("gpsimd", lambda e, ex=ex, kbuf=ebuf[buf]: [e.dma_start(kbuf[:], escr_h[ex])],
               reads=[f"escr{ex}"], writes=[f"ebuf{buf}"], kind="dma")

        def emit_experts(st, pp, hooks=None):
            xnf = xn_fm[pp]
            for i in range(8):
                if hooks and i in hooks:
                    hooks[i]()
                ex, buf, _ = esched[st * 8 + i]
                kbuf = ebuf[buf]
                bkey = f"ebuf{buf}"
                for j in range(NSUB):
                    for nh in range(2):
                        bank = pb[4 + j * 2 + nh]
                        bk = f"p{4 + j * 2 + nh}"

                        def emm(e, j=j, nh=nh, bank=bank, kbuf=kbuf, xnf=xnf):
                            ins = []
                            for kc in range(NCH):
                                ins.append(e.matmul(
                                    bank[:, 0:512],
                                    xnf[:, kc * ST + j * 128: kc * ST + (j + 1) * 128],
                                    kbuf[:, kc * D + nh * 512: kc * D + nh * 512 + 512],
                                    start=(kc == 0), stop=(kc == NCH - 1)))
                            return ins
                        op("tensor", emm, reads=[f"xn_fm{pp}", bkey], writes=[bk])
                        dst = moe[(i + 1) % 2][:, j * D + nh * 512: j * D + nh * 512 + 512]
                        ohcol = ohb[pp][:, j * 8 + ex: j * 8 + ex + 1]
                        if i == 0:
                            op("vector", lambda e, dst=dst, bank=bank, ohcol=ohcol: [
                                e.tensor_scalar(dst, bank[:, 0:512], ohcol, None, ALU.mult)],
                                reads=[bk, f"ohb{pp}"], writes=["moe1"])
                        else:
                            src2 = moe[i % 2][:, j * D + nh * 512: j * D + nh * 512 + 512]
                            op("vector", lambda e, dst=dst, bank=bank, ohcol=ohcol, src2=src2: [
                                e.scalar_tensor_tensor(dst, bank[:, 0:512], ohcol, src2,
                                                       ALU.mult, ALU.add)],
                                reads=[bk, f"ohb{pp}", f"moe{i % 2}"], writes=[f"moe{(i + 1) % 2}"])
                emit_expert_load(st * 8 + i + 2)
            if hooks and 8 in hooks:
                hooks[8]()

        def emit_rms_wo_out(st, pp):
            xs = xb[pp]
            for j in range(NSUB):
                def s2mm(e, j=j):
                    ins = []
                    for kc in range(NCH):
                        ins.append(e.matmul(pb[5][:, 16 + j:17 + j],
                                            y2g[:, kc * ST + j * 128: kc * ST + (j + 1) * 128],
                                            onesb[:],
                                            start=(kc == 0), stop=(kc == NCH - 1)))
                    return ins
                op("tensor", s2mm, reads=["y2g", "onesb"], writes=["p5"])
                op("scalar", lambda e, j=j: [
                    e.activation(rsl[:, j:j + 1], pb[5][:, 16 + j:17 + j], AF.Ln,
                                 bias=cst[:, 1:2], scale=1.0 / D)],
                    reads=["p5", "cst"], writes=["rsl"])
                op("scalar", lambda e, j=j: [
                    e.activation(rsv[:, j:j + 1], rsl[:, j:j + 1], AF.Exp, scale=-0.5)],
                    reads=["rsl"], writes=["rsv"])
            op("vector", lambda e: [
                e.tensor_scalar(rsv2[:, 0:NSUB], rsv[:, 0:NSUB], 1.0, None, ALU.mult)],
               reads=["rsv"], writes=["rsv2"])
            wob = {0: (4, 6), 1: (7, 5)}
            for j in range(NSUB):
                for nh in range(2):
                    bank = pb[wob[j][nh]]
                    bk = f"p{wob[j][nh]}"

                    def womm(e, j=j, nh=nh, bank=bank):
                        ins = []
                        for kc in range(NCH):
                            ins.append(e.matmul(
                                bank[:, 0:512],
                                ybf[:, kc * ST + j * 128: kc * ST + (j + 1) * 128],
                                wT["wo"][:, kc * D + nh * 512: kc * D + nh * 512 + 512],
                                start=(kc == 0), stop=(kc == NCH - 1)))
                        return ins
                    op("tensor", womm, reads=["ybf", "woT"], writes=[bk])
                    op("scalar", lambda e, j=j, nh=nh, bank=bank: [
                        e.activation(s_sb[:, j * D + nh * 512: j * D + nh * 512 + 512],
                                     bank[:, 0:512], AF.Identity,
                                     scale=rsv2[:, j:j + 1])],
                        reads=[bk, "rsv2"], writes=["s_sb"])
        def emit_gpre(st, pp):
            for j in range(NSUB):
                op("vector", lambda e, j=j, pp=pp: [
                    e.tensor_tensor(stageP[j][:], moe[0][:, j * D:(j + 1) * D],
                                    xn_tm[pp][:, j * D:(j + 1) * D], ALU.add)],
                    reads=["moe0", f"xn_tm{pp}"], writes=[f"stageP{j}"])

        def emit_xt_load(st):
            def ldxt(e, st=st):
                return [e.dma_start(stageQ[j][:],
                                    xmain_h[st * ST + j * 128: st * ST + (j + 1) * 128, :])
                        for j in range(NSUB)]
            op("sync", ldxt, writes=["stageQ0", "stageQ1"], kind="dma", ndma=NSUB)

        def emit_tail2a(st, pp):
            op("scalar", lambda e: [
                e.activation(sD[:, 0:D], stageP[0][:], AF.Sigmoid),
                e.activation(sD[:, D:2 * D], stageP[1][:], AF.Sigmoid)],
               reads=["stageP0", "stageP1"], writes=["sD"])
            for j in range(NSUB):
                op("vector", lambda e, j=j: [
                    e.tensor_tensor(sB[:, j * D:(j + 1) * D], stageP[j][:],
                                    sD[:, j * D:(j + 1) * D], ALU.mult)],
                    reads=[f"stageP{j}", "sD"], writes=["sB"])

        def emit_tail2b(st, pp):
            for j in range(NSUB):
                op("vector", lambda e, j=j: [
                    e.tensor_tensor(sD[:, j * D:(j + 1) * D], s_sb[:, j * D:(j + 1) * D],
                                    sB[:, j * D:(j + 1) * D], ALU.mult)],
                    reads=["s_sb", "sB"], writes=["sD"])
                op("vector", lambda e, j=j: [
                    e.tensor_tensor(s_sb[:, j * D:(j + 1) * D], stageQ[j][:],
                                    sD[:, j * D:(j + 1) * D], ALU.add)],
                    reads=[f"stageQ{j}", "sD"], writes=["s_sb"])
                op("sync", lambda e, st=st, j=j: [
                    e.dma_start(out_h[st * ST + j * 128: st * ST + (j + 1) * 128, :],
                                s_sb[:, j * D:(j + 1) * D])],
                    reads=["s_sb"], writes=["outhbm"], kind="dma")

        def emit_tail2(st, pp):
            emit_tail2a(st, pp)
            emit_tail2b(st, pp)

        quantize_matrix("wk")
        quantize_matrix("wv")
        emit_ln(0, 0, xpre_h, prefix=True)
        emit_scan_ue(0, 0)
        for st in range(NST):
            pp = st % 2
            emit_transposes(pp)
            emit_proj(pp, "wk", [pb[4 + i] for i in range(4)], [f"p{4 + i}" for i in range(4)])
            emit_proj(pp, "wv", [pb[i] for i in range(4)], [f"p{i}" for i in range(4)])
            if st + 1 < NST:
                emit_ln(st + 1, (st + 1) % 2, xpre_h, prefix=True)
            emit_scan_a(st, pp, with_scale=False, kb0=4)
            if st + 1 < NST:
                emit_scan_ue(st + 1, (st + 1) % 2)
            emit_scan_b(st, pp, prefix_st=st)
            emit_quant_round("wr" if st < 4 else "wo", 2 * (st % 4))
            emit_quant_round("wr" if st < 4 else "wo", 2 * (st % 4) + 1)
            emit_prep_expert(st)
        op("vector", lambda e: [
            e.tensor_tensor(tc2[:], tcols[:, 0:32], tcols[:, 32:64], ALU.add)],
            reads=["tcols"], writes=["tc2"])
        op("vector", lambda e: [
            e.tensor_tensor(tcols[:, 0:16], tc2[:, 0:16], tc2[:, 16:32], ALU.add)],
            reads=["tc2"], writes=["tcols"])
        op("vector", lambda e: [
            e.tensor_tensor(carry[:], tcols[:, 0:8], tcols[:, 8:16], ALU.add)],
            reads=["tcols"], writes=["carry"])
        if debug_taps:
            for m in ("wk", "wv", "wr", "wo"):
                tap(m + "T", wT[m][:], [m + "T"])

        def front_A(st):
            pp = st % 2
            emit_transposes(pp)
            emit_gate(pp)

        def front_B(st):
            pp = st % 2
            emit_proj(pp, "wk", [pb[i] for i in range(4)], [f"p{i}" for i in range(4)])
            emit_scan_a(st, pp, with_scale=True, kb0=0)

        def front_C(st):
            pp = st % 2
            emit_proj(pp, "wv", [pb[i] for i in range(4)], [f"p{i}" for i in range(4)])
            emit_scan_b(st, pp)

        def front_D(st):
            pp = st % 2
            emit_proj(pp, "wr", [pb[i] for i in range(4)], [f"p{i}" for i in range(4)])

        emit_ln(0, 0, xmain_h, prefix=False)
        emit_scan_ue(0, 0)
        front_A(0)
        front_B(0)
        front_C(0)
        front_D(0)
        def mk_hooks(st):
            pp0 = (st - 1) % 2

            def h1():
                emit_y(rb0=0)
                if st > 0:
                    emit_gpre(st - 1, pp0)
                if st + 1 < NST:
                    emit_ln_load(st + 1, (st + 1) % 2, xmain_h, prefix=False)

            def h2():
                if st + 1 < NST:
                    emit_ln_math(st + 1, (st + 1) % 2, xmain_h, prefix=False)
                    emit_scan_ue(st + 1, (st + 1) % 2)

            def h3():
                if st + 1 < NST:
                    front_A(st + 1)

            def h4():
                if st > 0:
                    emit_tail2a(st - 1, pp0)

            def h5():
                if st + 1 < NST:
                    front_B(st + 1)
                if st > 0:
                    emit_tail2b(st - 1, pp0)
                emit_xt_load(st)
                op("scalar", lambda e: [e.activation(y2g[:], ybf[:], AF.Square)],
                   reads=["ybf"], writes=["y2g"])

            def h6():
                if st + 1 < NST:
                    front_C(st + 1)

            def h8():
                if st + 1 < NST:
                    front_D(st + 1)
            return {1: h1, 2: h2, 3: h3, 4: h4, 5: h5, 6: h6, 8: h8}

        for st in range(NST):
            pp = st % 2
            emit_experts(st, pp, mk_hooks(st))
            emit_rms_wo_out(st, pp)
        emit_gpre(NST - 1, (NST - 1) % 2)
        emit_tail2(NST - 1, (NST - 1) % 2)

        P.finalize()
        for k in P.dma_semkeys:
            sems[k] = stack.enter_context(nc.semaphore("sd_" + k))
        block = stack.enter_context(nc.Block())

        @block.sync
        def _(sync):
            P.emit("sync", sync, sems)
            sync.wait_ge(sems["outhbm"], P.totals["outhbm"])
            if debug_taps and "taphbm" in sems:
                sync.wait_ge(sems["taphbm"], P.totals["taphbm"])

        @block.gpsimd
        def _(gpsimd):
            P.emit("gpsimd", gpsimd, sems)

        @block.vector
        def _(vector):
            P.emit("vector", vector, sems)

        @block.scalar
        def _(scalar):
            P.emit("scalar", scalar, sems)

        @block.tensor
        def _(tensor):
            P.emit("tensor", tensor, sems)

    return nc


_NC_CACHE = {}


def _get_nc():
    if "nc" not in _NC_CACHE:
        _NC_CACHE["nc"] = build_nc()
    return _NC_CACHE["nc"]


def kernel(L0_attn_x, L0_attn_Wr, L0_attn_Wk, L0_attn_Wv, L0_attn_Wo,
           L0_attn_decay, L0_attn_ln_w, L0_ffn_gate_w, L0_ffn_expert_w):
    x = np.ascontiguousarray(np.asarray(L0_attn_x, dtype=np.float32))
    Wr = np.ascontiguousarray(np.asarray(L0_attn_Wr, dtype=np.float32))
    Wk = np.ascontiguousarray(np.asarray(L0_attn_Wk, dtype=np.float32))
    Wv = np.ascontiguousarray(np.asarray(L0_attn_Wv, dtype=np.float32))
    Wo = np.ascontiguousarray(np.asarray(L0_attn_Wo, dtype=np.float32))
    decay = np.asarray(L0_attn_decay, dtype=np.float32)
    ln_w = np.asarray(L0_attn_ln_w, dtype=np.float32)
    gate_w = np.asarray(L0_ffn_gate_w, dtype=np.float32)
    expert_w = np.asarray(L0_ffn_expert_w, dtype=np.float32)

    ident = np.eye(128, dtype=np.float32).astype(ml_dtypes.bfloat16)
    io8 = np.broadcast_to(np.arange(8, dtype=np.float32), (128, 8)).copy()
    io8r = (8.0 - io8).astype(np.float32)
    dec8 = np.ascontiguousarray(decay.reshape(8, 128).T)
    lnw8 = np.ascontiguousarray(ln_w.reshape(8, 128).T)
    gwt = np.ascontiguousarray(gate_w.T)
    ewt = np.ascontiguousarray(expert_w.transpose(0, 2, 1))
    ar = np.arange(HALF, dtype=np.float32)
    io_lo = np.broadcast_to(ar, (128, HALF)).copy()
    io_hi = np.broadcast_to(ar + HALF, (128, HALF)).copy()
    zeros_pre = np.zeros((HALF, D), dtype=np.float32)

    in_maps = []
    for c in range(8):
        b, half = c // 2, c % 2
        in_maps.append({
            "xmain": np.ascontiguousarray(x[b, half * HALF:(half + 1) * HALF]),
            "xpre": (np.ascontiguousarray(x[b, 0:HALF]) if half else zeros_pre),
            "wk": Wk, "wv": Wv, "wr": Wr, "wo": Wo,
            "ewt": ewt, "gwt": gwt,
            "dec8": dec8, "lnw8": lnw8,
            "iomain": io_hi if half else io_lo,
            "iopre": io_lo,
            "ident": ident, "io8": io8, "io8r": io8r,
        })

    nc = _get_nc()
    res = run_bass_kernel_spmd(nc, in_maps, core_ids=list(range(8)))
    out = np.empty((B, T, D), dtype=np.float32)
    for c in range(8):
        b, half = c // 2, c % 2
        out[b, half * HALF:(half + 1) * HALF] = res.results[c]["out"]
    return out


# version 40 (final)
# speedup vs baseline: 1.0640x; 1.0025x over previous
import numpy as np
import ml_dtypes
import concourse.bass as bass
import concourse.mybir as mybir
from concourse.bass_utils import run_bass_kernel_spmd
from contextlib import ExitStack

F32 = mybir.dt.float32
BF16 = mybir.dt.bfloat16
AF = mybir.ActivationFunctionType
ALU = mybir.AluOpType
AX = mybir.AxisListType

B, T, D = 4, 4096, 1024
HALF = T // 2
ST = 256
NST = HALF // ST
NSUB = ST // 128
NCH = D // 128
CLIP_INV = 1.0e10
LN10E = 23.025850929940457

ENGINES = ("sync", "gpsimd", "vector", "scalar", "tensor")


class Op:
    __slots__ = ("engine", "fn", "deps", "kind", "ndma", "awaited", "cnt", "idx",
                 "semkey")

    def __init__(self, engine, fn, deps, kind, ndma, idx):
        self.engine = engine
        self.fn = fn
        self.deps = deps
        self.kind = kind
        self.ndma = ndma
        self.awaited = False
        self.cnt = 0
        self.idx = idx


class Prog:
    def __init__(self):
        self.ops = []
        self.writer = {}
        self.readers = {}

    def op(self, engine, fn, reads=(), writes=(), kind="comp", ndma=1):
        if kind == "dma":
            assert writes, "dma ops must name their destination buffer"
        deps = []
        for b in reads:
            w = self.writer.get(b)
            if w is not None:
                deps.append(w)
        for b in writes:
            deps.extend(self.readers.get(b, ()))
            w = self.writer.get(b)
            if w is not None:
                deps.append(w)
        seen = set()
        ud = []
        for d in deps:
            if id(d) not in seen:
                seen.add(id(d))
                ud.append(d)
        h = Op(engine, fn, ud, kind, ndma, len(self.ops))
        h.semkey = writes[0] if kind == "dma" else engine
        self.ops.append(h)
        for b in reads:
            self.readers.setdefault(b, []).append(h)
        for b in writes:
            self.writer[b] = h
            self.readers[b] = []
        return h

    def finalize(self):
        for h in self.ops:
            for d in h.deps:
                d.awaited = True
        cnt = {e: 0 for e in ENGINES}
        for h in self.ops:
            if h.kind == "dma":
                cnt.setdefault(h.semkey, 0)
                cnt[h.semkey] += 16 * h.ndma
            elif h.awaited:
                cnt[h.engine] += 1
            h.cnt = cnt[h.semkey]
        self.totals = cnt
        self.dma_semkeys = sorted(k for k in cnt if k not in ENGINES)

    def emit(self, eng_name, eng, sems):
        last = {}
        for h in self.ops:
            if h.engine != eng_name:
                continue
            for d in h.deps:
                if (h.engine == "tensor" and d.engine == "tensor"
                        and d.kind != "dma"):
                    continue
                if not d.awaited or d.cnt <= last.get(d.semkey, 0):
                    continue
                eng.wait_ge(sems[d.semkey], d.cnt)
                last[d.semkey] = d.cnt
            insts = h.fn(eng)
            if h.kind == "dma":
                for i in insts:
                    i.then_inc(sems[h.semkey], 16)
            elif h.awaited:
                insts[-1].then_inc(sems[h.engine], 1)


def build_nc(debug_taps=False):
    nc = bass.Bass("TRN2", target_bir_lowering=False)

    xmain_h = nc.dram_tensor("xmain", [HALF, D], F32, kind="ExternalInput")
    xpre_h = nc.dram_tensor("xpre", [HALF, D], F32, kind="ExternalInput")
    w_h = {m: nc.dram_tensor(m, [D, D], F32, kind="ExternalInput")
           for m in ("wk", "wv", "wr", "wo")}
    ewt_h = nc.dram_tensor("ewt", [8, D, D], F32, kind="ExternalInput")
    gwt_h = nc.dram_tensor("gwt", [D, 8], F32, kind="ExternalInput")
    dec_h = nc.dram_tensor("dec8", [128, 8], F32, kind="ExternalInput")
    lnw_h = nc.dram_tensor("lnw8", [128, 8], F32, kind="ExternalInput")
    iom_h = nc.dram_tensor("iomain", [128, HALF], F32, kind="ExternalInput")
    iop_h = nc.dram_tensor("iopre", [128, HALF], F32, kind="ExternalInput")
    id_h = nc.dram_tensor("ident", [128, 128], BF16, kind="ExternalInput")
    io8_h = nc.dram_tensor("io8", [128, 8], F32, kind="ExternalInput")
    io8r_h = nc.dram_tensor("io8r", [128, 8], F32, kind="ExternalInput")
    out_h = nc.dram_tensor("out", [HALF, D], F32, kind="ExternalOutput")
    escr_h = nc.dram_tensor("escr", [8, 128, NCH * D], BF16, kind="Internal")

    P = Prog()
    tap_handles = {}

    with ExitStack() as stack:
        def sb(name, shape, dt=F32):
            return stack.enter_context(nc.sbuf_tensor(name, shape, dt))

        wT = {m: sb(m + "T", [128, NCH * D], BF16) for m in ("wk", "wv", "wr", "wo")}
        ebuf = [sb(f"ebuf{i}", [128, NCH * D], BF16) for i in range(2)]
        gws = sb("gws", [128, NCH * 8], BF16)
        gwsf = sb("gwsf", [128, NCH * 8], F32)
        xb = [sb(f"xb{i}", [128, NSUB * D]) for i in range(2)]
        iot = [sb(f"iot{i}", [128, ST]) for i in range(2)]
        xn_tm = [sb(f"xn_tm{i}", [128, NSUB * D], BF16) for i in range(2)]
        xn_fm = [sb(f"xn_fm{i}", [128, NCH * ST], BF16) for i in range(2)]
        ybf = sb("ybf", [128, NCH * ST], BF16)
        y2g = sb("y2g", [128, NCH * ST], BF16)
        moe = [sb(f"moe{i}", [128, NSUB * D], BF16) for i in range(2)]
        sA = sb("sA", [128, NCH * ST])
        sB = sb("sB", [128, NCH * ST])
        sC = sb("sC", [128, NCH * ST])
        sD = sb("sD", [128, NCH * ST])
        s_sb = sb("s_sb", [128, NSUB * D])
        stageQ = [sb(f"stageQ{i}", [128, D]) for i in range(2)]
        stageP = [sb(f"stageP{i}", [128, D]) for i in range(2)]
        z256 = sb("z256", [128, ST])
        ident = sb("identb", [128, 128], BF16)
        onesb = sb("onesb", [128, 1], BF16)
        io8 = sb("io8b", [128, 8])
        io8r = sb("io8rb", [128, 8])
        decs = sb("decs", [128, 8])
        sgd = sb("sgd", [128, 8])
        dmx = sb("dmx", [128, 8])
        Lc = sb("Lc", [128, 8])
        nLc = sb("nLc", [128, 8])
        lnw = sb("lnwb", [128, 8])
        carry = sb("carryb", [128, NCH])
        tcols = sb("tcols", [128, NST * NCH])
        tc2 = sb("tc2", [128, 4 * NCH])
        scb = sb("scb", [128, 2 * 8])
        ohb = [sb(f"ohb{i}", [128, 2 * 8]) for i in range(2)]
        t8a = sb("t8a", [128, 8])
        t8b = sb("t8b", [128, 8])
        t8c = sb("t8c", [128, 8])
        t8d = sb("t8d", [128, 8])
        gmx = sb("gmx", [128, 2])
        gmx2 = sb("gmx2", [128, 2])
        m3g = sb("m3g", [128, 16])
        mx3 = sb("mx3", [128, 2])
        idxc = sb("idxc", [128, 2])
        mcol = sb("mcol", [128, 2])
        sq2 = sb("sq2", [128, 2])
        t1c = sb("t1c", [128, 2])
        s2c = sb("s2c", [128, 2])
        nvar = sb("nvar", [128, 2])
        lnc = sb("lncol", [128, 2])
        acol = sb("acol", [128, 2])
        acol2 = sb("acol2", [128, 2])
        nbc = sb("nbc", [128, 2])
        rsv = sb("rsv", [128, 2])
        rsv2 = sb("rsv2", [128, 2])
        rsl = sb("rsl", [128, 2])
        cst = sb("cstb", [128, 4])
        qa = sb("qa", [128, 1])
        qs = sb("qs", [128, 1])
        qsh = sb("qsh", [128, 1])
        qnsh = sb("qnsh", [128, 1])
        qwb = sb("qwb", [128, D], BF16)

        pb = [stack.enter_context(nc.psum_tensor(f"pb{i}", [128, 512], F32))
              for i in range(8)]
        sems = {e: stack.enter_context(nc.semaphore("sem_" + e)) for e in ENGINES}
        block = None

        op = P.op

        def tap(name, ap, readkeys):
            if not debug_taps:
                return
            th = nc.dram_tensor("tap_" + name, list(ap.shape), ap.dtype,
                                kind="ExternalOutput")
            tap_handles[name] = th
            op("sync", lambda e, th=th, ap=ap: [e.dma_start(th[:], ap)],
               reads=list(readkeys), writes=["taphbm"], kind="dma")

        op("sync", lambda e: [e.dma_start(ident[:], id_h[:])], writes=["ident"], kind="dma")
        op("sync", lambda e: [e.dma_start(io8[:], io8_h[:])], writes=["io8"], kind="dma")
        op("sync", lambda e: [e.dma_start(io8r[:], io8r_h[:])], writes=["io8r"], kind="dma")
        op("sync", lambda e: [e.dma_start(decs[:], dec_h[:])], writes=["decs"], kind="dma")
        op("sync", lambda e: [e.dma_start(lnw[:], lnw_h[:])], writes=["lnw"], kind="dma")
        op("sync", lambda e: [e.dma_start(gwsf[:, kc * 8:(kc + 1) * 8],
                                          gwt_h[kc * 128:(kc + 1) * 128, :])
                              for kc in range(NCH)],
           writes=["gwsf"], kind="dma", ndma=NCH)
        op("vector", lambda e: [e.memset(z256[:], 0.0)], writes=["z256"])
        op("vector", lambda e: [e.memset(onesb[:], 1.0)], writes=["onesb"])
        op("vector", lambda e: [e.memset(carry[:], 0.0)],
           writes=["carry0", "carry4"])
        op("vector", lambda e: [e.memset(cst[:, 0:1], 1e-5)], writes=["cst"])
        op("vector", lambda e: [e.memset(cst[:, 1:2], 1e-6)], writes=["cst"])
        op("vector", lambda e: [e.memset(cst[:, 2:3], 1.0)], writes=["cst"])
        op("vector", lambda e: [e.memset(cst[:, 3:4], -8.0)], writes=["cst"])
        op("vector", lambda e: [e.tensor_copy(gws[:], gwsf[:])], reads=["gwsf"], writes=["gws"])
        op("scalar", lambda e: [e.activation(sgd[:], decs[:], AF.Sigmoid)],
           reads=["decs"], writes=["sgd"])
        op("vector", lambda e: [e.tensor_scalar(dmx[:], sgd[:], 1e-7, None, ALU.max)],
           reads=["sgd"], writes=["dmx"])
        op("scalar", lambda e: [e.activation(Lc[:], dmx[:], AF.Ln)],
           reads=["dmx"], writes=["Lc"])
        op("vector", lambda e: [e.tensor_scalar(nLc[:], Lc[:], -1.0, None, ALU.mult)],
           reads=["Lc"], writes=["nLc"])

        import numpy as _np
        _eps = float(_np.float32(1e-5))

        def emit_quant_round(m, c):
            wTm = wT[m]
            sg = stageQ[c % 2]
            skey = f"stageQ{c % 2}"
            op("sync", lambda e, m=m, c=c, sg=sg: [
                e.dma_start(sg[:], w_h[m][c * 128:(c + 1) * 128, :])],
                writes=[skey], kind="dma")
            qta = sA[:, 0:D]
            qtb = sB[:, 0:D]
            qwq = sC[:, 0:D]
            qwbf = qwb[:]
            op("scalar", lambda e, sg=sg, qta=qta: [
                e.activation(qta, sg[:], AF.Abs, accum_out=qa[:, 0:1])],
               reads=[skey], writes=["sA", "qa"])
            op("gpsimd", lambda e: [
                e.tensor_scalar(qs[:], qa[:], 1.0 / D, _eps, ALU.mult, ALU.max),
                e.tensor_scalar(qsh[:], qa[:], 0.5 / D, 0.5 * _eps, ALU.mult, ALU.max)],
               reads=["qa"], writes=["qs", "qsh"])
            op("scalar", lambda e, sg=sg, qtb=qtb: [
                e.activation(qtb, sg[:], AF.Sign)],
               reads=[skey], writes=["sB"])
            op("vector", lambda e, qwq=qwq, qta=qta: [
                e.tensor_scalar(qwq, qta, qsh[:, 0:1], qs[:, 0:1],
                                ALU.is_ge, ALU.mult)],
               reads=["sA", "qsh", "qs"], writes=["sC", "sCh0", "sCh4"])
            op("vector", lambda e, qwbf=qwbf, qwq=qwq, qtb=qtb: [
                e.tensor_tensor(qwbf, qwq, qtb, ALU.mult)],
               reads=["sC", "sB"], writes=["qwb"])
            for q in range(2):
                bank = pb[q]

                def trq(e, q=q, bank=bank, qwbf=qwbf):
                    return [e.matmul(bank[:, i2 * 128:(i2 + 1) * 128],
                                     qwbf[:, (4 * q + i2) * 128:(4 * q + i2 + 1) * 128],
                                     ident[:], start=True, stop=True)
                            for i2 in range(4)]
                op("tensor", trq, reads=["qwb", "ident"], writes=[f"p{q}"])

                def evq(e, q=q, bank=bank, m=m, c=c, wTm=wTm):
                    ins = []
                    for i2 in range(4):
                        kc = 4 * q + i2
                        dst = wTm[:, kc * D + c * 128: kc * D + (c + 1) * 128]
                        src = bank[:, i2 * 128:(i2 + 1) * 128]
                        if m == "wo":
                            ins.append(e.tensor_scalar(dst, src, lnw[:, kc:kc + 1], None, ALU.mult))
                        else:
                            ins.append(e.tensor_copy(dst, src))
                    return ins
                op("vector", evq, reads=[f"p{q}", "lnw"], writes=[m + "T"])

        def quantize_matrix(m):
            for c in range(8):
                emit_quant_round(m, c)

        quantize_matrix("wk")
        quantize_matrix("wv")

        def emit_prep_expert(ex):
            kb = ebuf[ex % 2]
            bkey = f"ebuf{ex % 2}"

            def ldcast(e, ex=ex, kb=kb):
                return [e.dma_start(kb[:, hh * D:(hh + 1) * D],
                                    ewt_h[ex, hh * 128:(hh + 1) * 128, :])
                        for hh in range(8)]
            op("gpsimd", ldcast, writes=[bkey], kind="dma", ndma=8)
            op("gpsimd", lambda e, ex=ex, kb=kb: [e.dma_start(escr_h[ex], kb[:])],
               reads=[bkey], writes=[f"escr{ex}"], kind="dma")

        MAGICBIG = None

        def emit_ln_load(st, pp, x_hbm, prefix):
            xs = xb[pp]
            io_hbm = iop_h if prefix else iom_h

            def ldx(e, st=st, xs=xs, x_hbm=x_hbm):
                return [e.dma_start(xs[:, j * D:(j + 1) * D],
                                    x_hbm[st * ST + j * 128: st * ST + (j + 1) * 128, :])
                        for j in range(NSUB)]
            op("sync", ldx, writes=[f"x{pp}"], kind="dma", ndma=NSUB)
            op("sync", lambda e, st=st: [e.dma_start(iot[pp][:], io_hbm[:, st * ST:(st + 1) * ST])],
               writes=[f"iot{pp}"], kind="dma")

        def emit_ln_math(st, pp, x_hbm, prefix):
            xs = xb[pp]
            xnt = xn_tm[pp]
            if prefix:
                op("vector", lambda e, xs=xs: [
                    e.reduce_sum(t1c[:, j:j + 1], xs[:, j * D:(j + 1) * D], axis=AX.X)
                    for j in range(NSUB)],
                    reads=[f"x{pp}"], writes=["t1c"])
            else:
                op("scalar", lambda e, xs=xs: [
                    e.activation(sA[:, j * D:(j + 1) * D], xs[:, j * D:(j + 1) * D],
                                 AF.Identity, accum_out=t1c[:, j:j + 1])
                    for j in range(NSUB)],
                    reads=[f"x{pp}"], writes=["sA", "t1c"])
            op("scalar", lambda e: [
                e.mul(mcol[:, j:j + 1], t1c[:, j:j + 1], 1.0 / D)
                for j in range(NSUB)],
                reads=["t1c"], writes=["mcol"])
            op("scalar", lambda e, xs=xs: [
                e.activation(sA[:, j * D:(j + 1) * D], xs[:, j * D:(j + 1) * D],
                             AF.Square, accum_out=sq2[:, j:j + 1])
                for j in range(NSUB)],
                reads=[f"x{pp}"], writes=["sA", "sq2"])
            op("scalar", lambda e: [
                e.mul(s2c[:, j:j + 1], sq2[:, j:j + 1], 1.0 / D)
                for j in range(NSUB)],
                reads=["sq2"], writes=["s2c"])
            op("vector", lambda e: [
                e.scalar_tensor_tensor(nvar[:, j:j + 1], mcol[:, j:j + 1],
                                       mcol[:, j:j + 1], s2c[:, j:j + 1],
                                       ALU.mult, ALU.subtract)
                for j in range(NSUB)],
                reads=["mcol", "s2c"], writes=["nvar"])
            op("scalar", lambda e: [
                e.activation(lnc[:, j:j + 1], nvar[:, j:j + 1], AF.Ln,
                             bias=cst[:, 0:1], scale=-1.0)
                for j in range(NSUB)],
                reads=["nvar", "cst"], writes=["lncol"])
            op("scalar", lambda e: [
                e.activation(acol[:, j:j + 1], lnc[:, j:j + 1], AF.Exp, scale=-0.5)
                for j in range(NSUB)],
                reads=["lncol"], writes=["acol"])
            if prefix:
                op("vector", lambda e, xs=xs, xnt=xnt: [
                    e.tensor_scalar(xnt[:, j * D:(j + 1) * D], xs[:, j * D:(j + 1) * D],
                                    mcol[:, j:j + 1], acol[:, j:j + 1],
                                    ALU.subtract, ALU.mult)
                    for j in range(NSUB)],
                    reads=[f"x{pp}", "mcol", "acol"], writes=[f"xn_tm{pp}"])
            else:
                op("vector", lambda e: [
                    e.tensor_scalar(acol2[:, 0:NSUB], acol[:, 0:NSUB], 1.0,
                                    None, ALU.mult)],
                   reads=["acol"], writes=["acol2"])
                op("vector", lambda e: [
                    e.scalar_tensor_tensor(nbc[:, 0:NSUB], mcol[:, 0:NSUB], -1.0,
                                           acol[:, 0:NSUB], ALU.mult, ALU.mult)],
                   reads=["mcol", "acol"], writes=["nbc"])
                op("scalar", lambda e, xs=xs, xnt=xnt: [
                    e.activation(xnt[:, j * D:(j + 1) * D], xs[:, j * D:(j + 1) * D],
                                 AF.Identity, bias=nbc[:, j:j + 1],
                                 scale=acol2[:, j:j + 1])
                    for j in range(NSUB)],
                    reads=[f"x{pp}", "nbc", "acol2"], writes=[f"xn_tm{pp}"])

        def emit_ln(st, pp, x_hbm, prefix):
            emit_ln_load(st, pp, x_hbm, prefix)
            emit_ln_math(st, pp, x_hbm, prefix)

        def emit_transposes(pp):
            xnt, xnf = xn_tm[pp], xn_fm[pp]
            for g in range(4):
                bank = pb[g % 2]

                def trg(e, g=g, bank=bank, xnt=xnt):
                    ins = []
                    for u2 in range(4):
                        c = 2 * g + u2 // 2
                        j = u2 % 2
                        ins.append(e.matmul(bank[:, u2 * 128:(u2 + 1) * 128],
                                            xnt[:, j * D + c * 128: j * D + (c + 1) * 128],
                                            ident[:], start=True, stop=True))
                    return ins
                op("tensor", trg, reads=[f"xn_tm{pp}", "ident"], writes=[f"p{g % 2}"])
                op("scalar", lambda e, g=g, bank=bank, xnf=xnf: [
                    e.copy(xnf[:, (2 * g) * ST:(2 * g + 2) * ST], bank[:, 0:512])],
                    reads=[f"p{g % 2}"], writes=[f"xn_fm{pp}"])

        def emit_gate(pp, dbg=False):
            xnf = xn_fm[pp]
            ohx = ohb[pp]
            for j in range(NSUB):
                gb = pb[2 + j]
                gbk = f"p{2 + j}"

                def gmm(e, j=j, xnf=xnf, gb=gb):
                    ins = []
                    for kc in range(NCH):
                        ins.append(e.matmul(gb[:, j * 8:(j + 1) * 8],
                                            xnf[:, kc * ST + j * 128: kc * ST + (j + 1) * 128],
                                            gws[:, kc * 8:(kc + 1) * 8],
                                            start=(kc == 0), stop=(kc == NCH - 1)))
                    return ins
                op("tensor", gmm, reads=[f"xn_fm{pp}", "gws"], writes=[gbk])
                op("scalar", lambda e, j=j, gb=gb: [
                    e.copy(scb[:, j * 8:(j + 1) * 8], gb[:, j * 8:(j + 1) * 8])],
                    reads=[gbk], writes=["scb"])
            for j in range(NSUB):
                op("vector", lambda e, j=j: [
                    e.reduce_max(gmx[:, j:j + 1], scb[:, j * 8:(j + 1) * 8], axis=AX.X)],
                    reads=["scb"], writes=["gmx"])
            op("scalar", lambda e: [
                e.mul(gmx2[:], gmx[:], -1.0)],
                reads=["gmx"], writes=["gmx2"])
            for j, (ta, tb) in enumerate(((t8a, t8b), (t8c, t8d))):
                ka, kb = f"t8{j}a", f"t8{j}b"
                op("scalar", lambda e, j=j, ta=ta: [
                    e.activation(ta[:], scb[:, j * 8:(j + 1) * 8], AF.Identity,
                                 bias=gmx2[:, j:j + 1])],
                    reads=["scb", "gmx2"], writes=[ka])
                op("scalar", lambda e, ta=ta, tb=tb: [
                    e.activation(tb[:], ta[:], AF.Sign)],
                    reads=[ka], writes=[kb])
                op("scalar", lambda e, ta=ta, tb=tb: [
                    e.activation(ta[:], tb[:], AF.Abs)],
                    reads=[kb], writes=[ka])
                op("scalar", lambda e, ta=ta, tb=tb: [
                    e.activation(tb[:], ta[:], AF.Identity, bias=cst[:, 2:3], scale=-1.0)],
                    reads=[ka, "cst"], writes=[kb])
                op("vector", lambda e, ta=ta, tb=tb: [
                    e.tensor_tensor(ta[:], tb[:], io8r[:], ALU.mult)],
                    reads=[kb, "io8r"], writes=[ka])
            op("scalar", lambda e: [
                e.copy(m3g[:, 0:8], t8a[:]),
                e.copy(m3g[:, 8:16], t8c[:])],
                reads=["t80a", "t81a"], writes=["m3g"])
            for j in range(NSUB):
                op("vector", lambda e, j=j: [
                    e.reduce_max(mx3[:, j:j + 1], m3g[:, j * 8:(j + 1) * 8], axis=AX.X)],
                    reads=["m3g"], writes=["mx3"])
            op("scalar", lambda e: [
                e.activation(idxc[:, j:j + 1], mx3[:, j:j + 1], AF.Identity,
                             bias=cst[:, 3:4])
                for j in range(NSUB)],
                reads=["mx3", "cst"], writes=["idxc"])
            for j, (ta, tb) in enumerate(((t8a, t8b), (t8c, t8d))):
                ka, kb = f"t8{j}a", f"t8{j}b"
                op("scalar", lambda e, j=j, ta=ta: [
                    e.activation(ta[:], io8[:], AF.Identity, bias=idxc[:, j:j + 1])],
                    reads=["io8", "idxc"], writes=[ka])
                op("scalar", lambda e, ta=ta, tb=tb: [
                    e.activation(tb[:], ta[:], AF.Sign)],
                    reads=[ka], writes=[kb])
                op("scalar", lambda e, ta=ta, tb=tb: [
                    e.activation(ta[:], tb[:], AF.Abs)],
                    reads=[kb], writes=[ka])
                op("scalar", lambda e, j=j, ta=ta, ohx=ohx: [
                    e.activation(ohx[:, j * 8:(j + 1) * 8], ta[:], AF.Identity,
                                 bias=cst[:, 2:3], scale=-1.0)],
                    reads=[ka, "cst"], writes=[f"ohb{pp}"])
            if dbg:
                tap("idxc_g", idxc[:], ["idxc"])

        def emit_proj(pp, m, banks, bankkeys):
            xnf = xn_fm[pp]
            for i in range(4):
                bank = banks[i]

                def pmm(e, i=i, bank=bank, m=m, xnf=xnf):
                    ins = []
                    for ce in (2 * i, 2 * i + 1):
                        for kc in range(NCH):
                            ins.append(e.matmul(
                                bank[:, (ce % 2) * ST:(ce % 2) * ST + ST],
                                wT[m][:, kc * D + ce * 128: kc * D + (ce + 1) * 128],
                                xnf[:, kc * ST:(kc + 1) * ST],
                                start=(kc == 0), stop=(kc == NCH - 1)))
                    return ins
                op("tensor", pmm, reads=[m + "T", f"xn_fm{pp}"], writes=[bankkeys[i]])

        def emit_scan_ue(st, pp):
            iost = iot[pp]
            op("vector", lambda e, iost=iost: [
                e.tensor_scalar(sA[:, c * ST:(c + 1) * ST], iost[:],
                                nLc[:, c:c + 1], 23.025850929940457,
                                ALU.mult, ALU.min)
                for c in range(NCH)],
                reads=[f"iot{pp}", "nLc"], writes=["sA"])
            op("scalar", lambda e: [e.activation(sB[:], sA[:], AF.Exp)],
               reads=["sA"], writes=["sB"])

        def emit_scan_a(st, pp, with_scale, kb0=4):
            iost = iot[pp]
            op("vector", lambda e, kb0=kb0: [
                e.scalar_tensor_tensor(sC[:, 2 * i * ST:(2 * i + 2) * ST],
                                       sB[:, 2 * i * ST:(2 * i + 2) * ST],
                                       CLIP_INV, pb[kb0 + i][:, 0:512],
                                       ALU.min, ALU.mult)
                for i in range(4)],
                reads=["sB"] + [f"p{kb0 + i}" for i in range(4)],
                writes=["sC", "sCh0", "sCh4"])
            if with_scale:
                op("scalar", lambda e, iost=iost: [
                    e.activation(sA[:, c * ST:(c + 1) * ST], iost[:], AF.Exp,
                                 scale=Lc[:, c:c + 1])
                    for c in range(NCH)],
                    reads=[f"iot{pp}", "Lc"], writes=["sA"])

        def emit_scan_b(st, pp, prefix_st=None):
            if prefix_st is not None:
                op("vector", lambda e, k=prefix_st: [
                    e.scalar_tensor_tensor(sD[:, c * ST:(c + 1) * ST],
                                           pb[c // 2][:, (c % 2) * ST:(c % 2) * ST + ST],
                                           1.0,
                                           sC[:, c * ST:(c + 1) * ST],
                                           ALU.mult, ALU.mult,
                                           accum_out=tcols[:, k * 8 + c:k * 8 + c + 1])
                    for c in range(NCH)],
                    reads=["p0", "p1", "p2", "p3", "sC"], writes=["sD", "tcols"])
                return
            for bk0, c0 in ((0, 0), (2, 4)):
                op("vector", lambda e, bk0=bk0: [
                    e.tensor_tensor(sD[:, 2 * i * ST:(2 * i + 2) * ST],
                                    pb[i][:, 0:512],
                                    sC[:, 2 * i * ST:(2 * i + 2) * ST], ALU.mult)
                    for i in range(bk0, bk0 + 2)],
                    reads=[f"p{bk0}", f"p{bk0 + 1}", "sC"], writes=["sD"])
                op("vector", lambda e, c0=c0: [
                    e.tensor_tensor_scan(sC[:, c * ST:(c + 1) * ST],
                                         sD[:, c * ST:(c + 1) * ST], z256[:],
                                         carry[:, c:c + 1], ALU.add, ALU.add)
                    for c in range(c0, c0 + 4)],
                    reads=["sD", "z256", f"carry{c0}"],
                    writes=["sC", f"sCh{c0}"])
                op("scalar", lambda e, c0=c0: [
                    e.copy(carry[:, c:c + 1], sC[:, c * ST + ST - 1: c * ST + ST])
                    for c in range(c0, c0 + 4)],
                    reads=[f"sCh{c0}"], writes=[f"carry{c0}"])

        def emit_y(rb0=4):
            op("scalar", lambda e, rb0=rb0: [
                e.activation(sD[:, 2 * i * ST:(2 * i + 2) * ST],
                             pb[rb0 + i][:, 0:512], AF.Sigmoid)
                for i in range(4)],
                reads=[f"p{rb0 + i}" for i in range(4)], writes=["sD"])
            op("vector", lambda e: [e.tensor_tensor(sB[:], sD[:], sA[:], ALU.mult)],
               reads=["sD", "sA"], writes=["sB"])
            op("vector", lambda e: [e.tensor_tensor(ybf[:], sC[:], sB[:], ALU.mult)],
               reads=["sC", "sB"], writes=["ybf"])

        esched = []
        for _p in range(NST):
            _order = range(7, -1, -1) if _p % 2 == 0 else range(8)
            for _i, _ex in enumerate(_order):
                _buf = (_i + 1) % 2 if _p % 2 == 0 else _i % 2
                esched.append((_ex, _buf, _i >= 2))

        def emit_expert_load(g):
            if g >= len(esched):
                return
            ex, buf, needs = esched[g]
            if not needs:
                return
            op("gpsimd", lambda e, ex=ex, kbuf=ebuf[buf]: [e.dma_start(kbuf[:], escr_h[ex])],
               reads=[f"escr{ex}"], writes=[f"ebuf{buf}"], kind="dma")

        def emit_experts(st, pp, hooks=None):
            xnf = xn_fm[pp]
            for i in range(8):
                if hooks and i in hooks:
                    hooks[i]()
                ex, buf, _ = esched[st * 8 + i]
                kbuf = ebuf[buf]
                bkey = f"ebuf{buf}"
                for j in range(NSUB):
                    for nh in range(2):
                        bank = pb[4 + j * 2 + nh]
                        bk = f"p{4 + j * 2 + nh}"

                        def emm(e, j=j, nh=nh, bank=bank, kbuf=kbuf, xnf=xnf):
                            ins = []
                            for kc in range(NCH):
                                ins.append(e.matmul(
                                    bank[:, 0:512],
                                    xnf[:, kc * ST + j * 128: kc * ST + (j + 1) * 128],
                                    kbuf[:, kc * D + nh * 512: kc * D + nh * 512 + 512],
                                    start=(kc == 0), stop=(kc == NCH - 1)))
                            return ins
                        op("tensor", emm, reads=[f"xn_fm{pp}", bkey], writes=[bk])
                        dst = moe[(i + 1) % 2][:, j * D + nh * 512: j * D + nh * 512 + 512]
                        ohcol = ohb[pp][:, j * 8 + ex: j * 8 + ex + 1]
                        if i == 0:
                            op("vector", lambda e, dst=dst, bank=bank, ohcol=ohcol: [
                                e.tensor_scalar(dst, bank[:, 0:512], ohcol, None, ALU.mult)],
                                reads=[bk, f"ohb{pp}"], writes=["moe1"])
                        else:
                            src2 = moe[i % 2][:, j * D + nh * 512: j * D + nh * 512 + 512]
                            op("vector", lambda e, dst=dst, bank=bank, ohcol=ohcol, src2=src2: [
                                e.scalar_tensor_tensor(dst, bank[:, 0:512], ohcol, src2,
                                                       ALU.mult, ALU.add)],
                                reads=[bk, f"ohb{pp}", f"moe{i % 2}"], writes=[f"moe{(i + 1) % 2}"])
                emit_expert_load(st * 8 + i + 2)
            if hooks and 8 in hooks:
                hooks[8]()

        def emit_rms_wo_out(st, pp):
            xs = xb[pp]
            for j in range(NSUB):
                def s2mm(e, j=j):
                    ins = []
                    for kc in range(NCH):
                        ins.append(e.matmul(pb[5][:, 16 + j:17 + j],
                                            y2g[:, kc * ST + j * 128: kc * ST + (j + 1) * 128],
                                            onesb[:],
                                            start=(kc == 0), stop=(kc == NCH - 1)))
                    return ins
                op("tensor", s2mm, reads=["y2g", "onesb"], writes=["p5"])
                op("scalar", lambda e, j=j: [
                    e.activation(rsl[:, j:j + 1], pb[5][:, 16 + j:17 + j], AF.Ln,
                                 bias=cst[:, 1:2], scale=1.0 / D)],
                    reads=["p5", "cst"], writes=["rsl"])
                op("scalar", lambda e, j=j: [
                    e.activation(rsv[:, j:j + 1], rsl[:, j:j + 1], AF.Exp, scale=-0.5)],
                    reads=["rsl"], writes=["rsv"])
            op("vector", lambda e: [
                e.tensor_scalar(rsv2[:, 0:NSUB], rsv[:, 0:NSUB], 1.0, None, ALU.mult)],
               reads=["rsv"], writes=["rsv2"])
            wob = {0: (4, 6), 1: (7, 5)}
            for j in range(NSUB):
                for nh in range(2):
                    bank = pb[wob[j][nh]]
                    bk = f"p{wob[j][nh]}"

                    def womm(e, j=j, nh=nh, bank=bank):
                        ins = []
                        for kc in range(NCH):
                            ins.append(e.matmul(
                                bank[:, 0:512],
                                ybf[:, kc * ST + j * 128: kc * ST + (j + 1) * 128],
                                wT["wo"][:, kc * D + nh * 512: kc * D + nh * 512 + 512],
                                start=(kc == 0), stop=(kc == NCH - 1)))
                        return ins
                    op("tensor", womm, reads=["ybf", "woT"], writes=[bk])
                    op("scalar", lambda e, j=j, nh=nh, bank=bank: [
                        e.activation(s_sb[:, j * D + nh * 512: j * D + nh * 512 + 512],
                                     bank[:, 0:512], AF.Identity,
                                     scale=rsv2[:, j:j + 1])],
                        reads=[bk, "rsv2"], writes=["s_sb"])
        def emit_gpre(st, pp):
            for j in range(NSUB):
                op("vector", lambda e, j=j, pp=pp: [
                    e.tensor_tensor(stageP[j][:], moe[0][:, j * D:(j + 1) * D],
                                    xn_tm[pp][:, j * D:(j + 1) * D], ALU.add)],
                    reads=["moe0", f"xn_tm{pp}"], writes=[f"stageP{j}"])

        def emit_xt_load(st):
            def ldxt(e, st=st):
                return [e.dma_start(stageQ[j][:],
                                    xmain_h[st * ST + j * 128: st * ST + (j + 1) * 128, :])
                        for j in range(NSUB)]
            op("sync", ldxt, writes=["stageQ0", "stageQ1"], kind="dma", ndma=NSUB)

        def emit_tail2a(st, pp):
            op("scalar", lambda e: [
                e.activation(sD[:, 0:D], stageP[0][:], AF.Sigmoid),
                e.activation(sD[:, D:2 * D], stageP[1][:], AF.Sigmoid)],
               reads=["stageP0", "stageP1"], writes=["sD"])
            for j in range(NSUB):
                op("vector", lambda e, j=j: [
                    e.tensor_tensor(sB[:, j * D:(j + 1) * D], stageP[j][:],
                                    sD[:, j * D:(j + 1) * D], ALU.mult)],
                    reads=[f"stageP{j}", "sD"], writes=["sB"])

        def emit_tail2b(st, pp):
            for j in range(NSUB):
                op("vector", lambda e, j=j: [
                    e.tensor_tensor(sD[:, j * D:(j + 1) * D], s_sb[:, j * D:(j + 1) * D],
                                    sB[:, j * D:(j + 1) * D], ALU.mult)],
                    reads=["s_sb", "sB"], writes=["sD"])
                op("vector", lambda e, j=j: [
                    e.tensor_tensor(s_sb[:, j * D:(j + 1) * D], stageQ[j][:],
                                    sD[:, j * D:(j + 1) * D], ALU.add)],
                    reads=[f"stageQ{j}", "sD"], writes=["s_sb"])
                op("sync", lambda e, st=st, j=j: [
                    e.dma_start(out_h[st * ST + j * 128: st * ST + (j + 1) * 128, :],
                                s_sb[:, j * D:(j + 1) * D])],
                    reads=["s_sb"], writes=["outhbm"], kind="dma")

        def emit_tail2(st, pp):
            emit_tail2a(st, pp)
            emit_tail2b(st, pp)

        quantize_matrix("wk")
        quantize_matrix("wv")
        emit_ln(0, 0, xpre_h, prefix=True)
        emit_scan_ue(0, 0)
        for st in range(NST):
            pp = st % 2
            emit_transposes(pp)
            emit_proj(pp, "wk", [pb[4 + i] for i in range(4)], [f"p{4 + i}" for i in range(4)])
            emit_proj(pp, "wv", [pb[i] for i in range(4)], [f"p{i}" for i in range(4)])
            if st + 1 < NST:
                emit_ln(st + 1, (st + 1) % 2, xpre_h, prefix=True)
            emit_scan_a(st, pp, with_scale=False, kb0=4)
            if st + 1 < NST:
                emit_scan_ue(st + 1, (st + 1) % 2)
            emit_scan_b(st, pp, prefix_st=st)
            emit_quant_round("wr" if st < 4 else "wo", 2 * (st % 4))
            emit_quant_round("wr" if st < 4 else "wo", 2 * (st % 4) + 1)
            emit_prep_expert(st)
        op("vector", lambda e: [
            e.tensor_tensor(tc2[:], tcols[:, 0:32], tcols[:, 32:64], ALU.add)],
            reads=["tcols"], writes=["tc2"])
        op("vector", lambda e: [
            e.tensor_tensor(tcols[:, 0:16], tc2[:, 0:16], tc2[:, 16:32], ALU.add)],
            reads=["tc2"], writes=["tcols"])
        op("vector", lambda e: [
            e.tensor_tensor(carry[:], tcols[:, 0:8], tcols[:, 8:16], ALU.add)],
            reads=["tcols"], writes=["carry0", "carry4"])
        if debug_taps:
            for m in ("wk", "wv", "wr", "wo"):
                tap(m + "T", wT[m][:], [m + "T"])

        def front_A(st):
            pp = st % 2
            emit_transposes(pp)
            emit_gate(pp)

        def front_B(st):
            pp = st % 2
            emit_proj(pp, "wk", [pb[i] for i in range(4)], [f"p{i}" for i in range(4)])
            emit_scan_a(st, pp, with_scale=True, kb0=0)

        def front_C(st):
            pp = st % 2
            emit_proj(pp, "wv", [pb[i] for i in range(4)], [f"p{i}" for i in range(4)])
            emit_scan_b(st, pp)

        def front_D(st):
            pp = st % 2
            emit_proj(pp, "wr", [pb[i] for i in range(4)], [f"p{i}" for i in range(4)])

        emit_ln(0, 0, xmain_h, prefix=False)
        emit_scan_ue(0, 0)
        front_A(0)
        front_B(0)
        front_C(0)
        front_D(0)
        def mk_hooks(st):
            pp0 = (st - 1) % 2

            def h1():
                emit_y(rb0=0)
                if st > 0:
                    emit_gpre(st - 1, pp0)
                if st + 1 < NST:
                    emit_ln_load(st + 1, (st + 1) % 2, xmain_h, prefix=False)

            def h2():
                if st + 1 < NST:
                    emit_ln_math(st + 1, (st + 1) % 2, xmain_h, prefix=False)
                    emit_scan_ue(st + 1, (st + 1) % 2)

            def h3():
                if st + 1 < NST:
                    front_A(st + 1)

            def h4():
                if st > 0:
                    emit_tail2a(st - 1, pp0)

            def h5():
                if st + 1 < NST:
                    front_B(st + 1)
                if st > 0:
                    emit_tail2b(st - 1, pp0)
                emit_xt_load(st)
                op("scalar", lambda e: [e.activation(y2g[:], ybf[:], AF.Square)],
                   reads=["ybf"], writes=["y2g"])

            def h6():
                if st + 1 < NST:
                    front_C(st + 1)

            def h8():
                if st + 1 < NST:
                    front_D(st + 1)
            return {1: h1, 2: h2, 3: h3, 4: h4, 5: h5, 6: h6, 8: h8}

        for st in range(NST):
            pp = st % 2
            emit_experts(st, pp, mk_hooks(st))
            emit_rms_wo_out(st, pp)
        emit_gpre(NST - 1, (NST - 1) % 2)
        emit_tail2(NST - 1, (NST - 1) % 2)

        P.finalize()
        for k in P.dma_semkeys:
            sems[k] = stack.enter_context(nc.semaphore("sd_" + k))
        block = stack.enter_context(nc.Block())

        @block.sync
        def _(sync):
            P.emit("sync", sync, sems)
            sync.wait_ge(sems["outhbm"], P.totals["outhbm"])
            if debug_taps and "taphbm" in sems:
                sync.wait_ge(sems["taphbm"], P.totals["taphbm"])

        @block.gpsimd
        def _(gpsimd):
            P.emit("gpsimd", gpsimd, sems)

        @block.vector
        def _(vector):
            P.emit("vector", vector, sems)

        @block.scalar
        def _(scalar):
            P.emit("scalar", scalar, sems)

        @block.tensor
        def _(tensor):
            P.emit("tensor", tensor, sems)

    return nc


_NC_CACHE = {}


def _get_nc():
    if "nc" not in _NC_CACHE:
        _NC_CACHE["nc"] = build_nc()
    return _NC_CACHE["nc"]


def kernel(L0_attn_x, L0_attn_Wr, L0_attn_Wk, L0_attn_Wv, L0_attn_Wo,
           L0_attn_decay, L0_attn_ln_w, L0_ffn_gate_w, L0_ffn_expert_w):
    x = np.ascontiguousarray(np.asarray(L0_attn_x, dtype=np.float32))
    Wr = np.ascontiguousarray(np.asarray(L0_attn_Wr, dtype=np.float32))
    Wk = np.ascontiguousarray(np.asarray(L0_attn_Wk, dtype=np.float32))
    Wv = np.ascontiguousarray(np.asarray(L0_attn_Wv, dtype=np.float32))
    Wo = np.ascontiguousarray(np.asarray(L0_attn_Wo, dtype=np.float32))
    decay = np.asarray(L0_attn_decay, dtype=np.float32)
    ln_w = np.asarray(L0_attn_ln_w, dtype=np.float32)
    gate_w = np.asarray(L0_ffn_gate_w, dtype=np.float32)
    expert_w = np.asarray(L0_ffn_expert_w, dtype=np.float32)

    ident = np.eye(128, dtype=np.float32).astype(ml_dtypes.bfloat16)
    io8 = np.broadcast_to(np.arange(8, dtype=np.float32), (128, 8)).copy()
    io8r = (8.0 - io8).astype(np.float32)
    dec8 = np.ascontiguousarray(decay.reshape(8, 128).T)
    lnw8 = np.ascontiguousarray(ln_w.reshape(8, 128).T)
    gwt = np.ascontiguousarray(gate_w.T)
    ewt = np.ascontiguousarray(expert_w.transpose(0, 2, 1))
    ar = np.arange(HALF, dtype=np.float32)
    io_lo = np.broadcast_to(ar, (128, HALF)).copy()
    io_hi = np.broadcast_to(ar + HALF, (128, HALF)).copy()
    zeros_pre = np.zeros((HALF, D), dtype=np.float32)

    in_maps = []
    for c in range(8):
        b, half = c // 2, c % 2
        in_maps.append({
            "xmain": np.ascontiguousarray(x[b, half * HALF:(half + 1) * HALF]),
            "xpre": (np.ascontiguousarray(x[b, 0:HALF]) if half else zeros_pre),
            "wk": Wk, "wv": Wv, "wr": Wr, "wo": Wo,
            "ewt": ewt, "gwt": gwt,
            "dec8": dec8, "lnw8": lnw8,
            "iomain": io_hi if half else io_lo,
            "iopre": io_lo,
            "ident": ident, "io8": io8, "io8r": io8r,
        })

    nc = _get_nc()
    res = run_bass_kernel_spmd(nc, in_maps, core_ids=list(range(8)))
    out = np.empty((B, T, D), dtype=np.float32)
    for c in range(8):
        b, half = c // 2, c % 2
        out[b, half * HALF:(half + 1) * HALF] = res.results[c]["out"]
    return out
